# version 11
# baseline (speedup 1.0000x reference)
import contextlib
import math
import concourse.mybir as mybir

F32 = mybir.dt.float32
BF16 = mybir.dt.bfloat16
AF = mybir.ActivationFunctionType
ALU = mybir.AluOpType
AX = mybir.AxisListType

S, D, H, HD = 2048, 1024, 16, 64
NSB, NDB, HP = 16, 8, 8
PI = math.pi


def x_view(self, bi, sb):
    return self.X[bi].rearrange("(sb p) d -> p sb d", p=128)[:, sb, :]


def emit_ln_stats(self, c, src_ap, pe_wait=None):
    st = self.stat
    VE, SE = self.VE, self.SE
    junk = self.lnjunk
    w = [pe_wait] if pe_wait else []
    self.sop(lambda e, j=junk: e.activation(j[:, :], src_ap, AF.Identity,
                                            accum_out=st[:, c + 6:c + 7]), waits=w)
    s2 = self.sop(lambda e, j=junk: e.activation(j[:, :], src_ap, AF.Square,
                                                 accum_out=st[:, c + 7:c + 8]))
    v1 = self.vop(lambda e: e.tensor_scalar(st[:, c:c + 1], st[:, c + 6:c + 7],
                                            1.0 / 1024, None, ALU.mult), waits=[(SE, s2)])
    v2 = self.vop(lambda e: e.tensor_tensor(st[:, c + 2:c + 3], st[:, c:c + 1],
                                            st[:, c:c + 1], ALU.mult), waits=[(VE, v1)])
    v3 = self.vop(lambda e: e.scalar_tensor_tensor(st[:, c + 1:c + 2], st[:, c + 7:c + 8],
                                                   1.0 / 1024, st[:, c + 2:c + 3],
                                                   ALU.mult, ALU.subtract), waits=[(VE, v2)])
    s3 = self.sop(lambda e: e.activation(st[:, c + 3:c + 4], st[:, c + 1:c + 2], AF.Ln,
                                         bias=self.cst[:, 0:1]), waits=[(VE, v3)])
    s4 = self.sop(lambda e: e.activation(st[:, c + 4:c + 5], st[:, c + 3:c + 4],
                                         AF.Exp, scale=-0.5), waits=[(SE, s3)])
    tv2 = self.vop(lambda e: e.scalar_tensor_tensor(
        st[:, c + 5:c + 6], st[:, c:c + 1], -1.0, st[:, c + 4:c + 5], ALU.mult, ALU.mult),
        waits=[(SE, s4)])
    return tv2


def emit_attn(self, bi):
    VE, SE, PE, LDG, STS = self.VE, self.SE, self.PE, self.LDG, self.STS
    with contextlib.ExitStack() as tstack:
        self.tT = self.sb("tT_a", [128, 8 * 2049], BF16, tstack)
        self.ps_tr = self.ps("tr", [128, 1024], BF16, tstack)
        self.ps_tr2 = self.ps("tr2", [128, 1024], BF16, tstack)
        self.ps_a = self.ps("pa", [128, 512], F32, tstack)
        self.ps_b = self.ps("pb", [128, 512], F32, tstack)
        tT_now = self.tT
        mz_t = self.vop(lambda e, t=tT_now: e.memset(t[:, :], 0.0))
        emit_attn_inner(self, bi, mz_t)
    self.tT = None
    if self.cfg.get("upto") in ("a0", "a1q", "a1"):
        return
    emit_attn2(self, bi)


def emit_attn_inner(self, bi, mz_t):
    VE, SE, PE, LDG, STS = self.VE, self.SE, self.PE, self.LDG, self.STS

    self.mark(f'A0_b{bi}')
    with contextlib.ExitStack() as ph:
        x0t = self.sb("a0_x0", [128, 4 * D], F32, ph)
        xnb = self.sb("a0_xn", [128, 4 * D], BF16, ph)
        self.lnjunk = self.sb("a0_junk", [128, 1024], F32, ph)
        se_app, ve_ev, pe_tr = {}, {}, {}
        for sb in range(NSB):
            par = sb % 2
            p4 = sb % 4
            x0 = x0t[:, p4 * D:(p4 + 1) * D]
            xn = xnb[:, p4 * D:(p4 + 1) * D]
            w = [(SE, se_app[sb - 4])] if sb - 4 in se_app else []
            tld = self.dma_g(x0, x_view(self, bi, sb), waits=w)
            c = p4 * 16
            emit_ln_stats(self, c, x0, pe_wait=tld)
            w = [(VE, self.VE.v)] + ([(PE, pe_tr[sb - 4])] if sb - 4 in pe_tr else [])
            se_app[sb] = self.sop(lambda e, xn=xn, x0=x0, c=c: e.activation(
                xn, x0, AF.Identity, scale=self.stat[:, c + 4:c + 5],
                bias=self.stat[:, c + 5:c + 6]), waits=w)
            pst = self.ps_tr if par == 0 else self.ps_tr2
            w = [(SE, se_app[sb])] + ([(VE, ve_ev[sb - 2])] if sb - 2 in ve_ev else [])
            for db in range(NDB):
                pe_tr[sb] = self.tr(pst[:, db * 128:(db + 1) * 128],
                                    xn[:, db * 128:(db + 1) * 128], inc=PE,
                                    waits=(w if db == 0 else ()))
            dst = self.tT[:, :].rearrange("p (db c) -> p db c", db=NDB)[:, :, 1 + sb * 128: 129 + sb * 128]
            src = pst[:, :].rearrange("p (k c) -> p k c", k=NDB)
            ve_ev[sb] = self.vop(lambda e, d=dst, s=src: e.tensor_copy(d, s),
                                 waits=[(PE, pe_tr[sb])] + ([(VE, mz_t)] if sb == 0 else []))
        self.barrier()
        self.tap("xnT", self.tT[:, :])
    if self.cfg.get("upto") == "a0":
        return

    self.mark(f'A1_b{bi}')
    with contextlib.ExitStack() as ph:
        wp = self.sb("a1_wp", [128, NDB * D], BF16, ph)
        wd = self.sb("a1_wd", [128, NDB * D], BF16, ph)
        cosP = self.sb("a1_cos", [128, NDB * S], BF16, ph)
        sinP = self.sb("a1_sin", [128, NDB * S], BF16, ph)
        sc1 = self.sb("a1_sc1", [128, 3 * 512], F32, ph)
        sc2 = self.sb("a1_sc2", [128, 2 * 512], F32, ph)
        asqb = self.sb("a1_asq", [128, 2 * 512], BF16, ph)
        rln = self.sb("a1_rln", [128, 512], F32, ph)
        rinvf = self.sb("a1_rinv", [128, 2 * 512], F32, ph)
        ampv = self.sb("a1_ampv", [128, 512], F32, ph)
        dstg = self.sb("a1_dst", [128, 2 * S], BF16, ph)
        cstg = self.sb("a1_cst", [128, 2 * S], BF16, ph)
        sstg = self.sb("a1_sst", [128, 2 * S], BF16, ph)

        pe_last_p = [0]
        pe_last_a = [0]
        for side in ("q", "k"):
            w_p = self.wqp_c if side == "q" else self.wkp_c
            w_d = self.wq_c if side == "q" else self.wk_c
            d_s = self.qd_s if side == "q" else self.kd_s
            c_s = self.qc_s if side == "q" else self.kc_s
            s_s = self.qs_s if side == "q" else self.ks_s
            t_wp = self.dma_g(wp[:, :], w_p[:, :], waits=[(PE, pe_last_p[0])])
            t_wd = self.dma_g(wd[:, :], w_d[:, :], waits=[(PE, pe_last_a[0])])


            se_p = {}
            for it in range(int(self.cfg.get('a1n', 32))):
                ob, ch = divmod(it, 4)
                par = it % 2
                pss = self.ps_a if par == 0 else self.ps_b
                w = [t_wp] + ([(SE, se_p[it - 2])] if it - 2 in se_p else [])
                for db in range(NDB):
                    pem = self.mm(pss[:, :],
                                  wp[:, db * D + ob * 128: db * D + ob * 128 + 128],
                                  self.tTv(db, ch * 512, 512),
                                  start=(db == 0), stop=(db == NDB - 1),
                                  inc=(PE if db == NDB - 1 else None),
                                  waits=(w if db == 0 else ()))
                s1v = sc1[:, par * 512:(par + 1) * 512]
                s2v = sc2[:, par * 512:(par + 1) * 512]
                cpv = cosP[:, ob * S + ch * 512: ob * S + ch * 512 + 512]
                spv = sinP[:, ob * S + ch * 512: ob * S + ch * 512 + 512]
                self.sop(lambda e, o=s1v, i=pss: e.activation(o, i[:, :], AF.Tanh),
                         waits=[(PE, pem)])
                self.sop(lambda e, o=s2v, i=s1v: e.activation(o, i, AF.Abs))
                self.sop(lambda e, o=cpv, i=s2v: e.activation(o, i, AF.Sin,
                                                              scale=-PI, bias=self.cst[:, 2:3]))
                se_p[it] = self.sop(lambda e, o=spv, i=s1v: e.activation(o, i, AF.Sin,
                                                                         scale=PI))
            pe_last_p[0] = PE.v
            self.barrier(producers=("scalar",), consumers=("tensor",))
            if self.cfg.get("upto") == "a1p":
                self.barrier()
                return

            se_ex, se_sq, se_ri, ve_d, ve_f, rms_pe, st_s = {}, {}, {}, {}, {}, {}, {}

            def projA(it):
                ob, ch = divmod(it, 4)
                par = it % 2
                sbp = ob % 2
                pss = self.ps_a if par == 0 else self.ps_b
                w = [t_wd]
                if it - 2 in se_ex:
                    w += [(SE, se_ex[it - 2]), (VE, ve_d[it - 2])]
                pem = None
                for db in range(NDB):
                    pem = self.mm(pss[:, :],
                                  wd[:, db * D + ob * 128: db * D + ob * 128 + 128],
                                  self.tTv(db, ch * 512, 512),
                                  start=(db == 0), stop=(db == NDB - 1),
                                  inc=(PE if db == NDB - 1 else None),
                                  waits=(w if db == 0 else ()))
                s1v = sc1[:, (it % 3) * 512:(it % 3 + 1) * 512]
                s2v = sc2[:, par * 512:(par + 1) * 512]
                aqv = asqb[:, par * 512:(par + 1) * 512]
                ddst = dstg[:, sbp * S + ch * 512: sbp * S + ch * 512 + 512]
                w = [(PE, pem)]
                if ch == 0 and (ob - 2) in st_s:
                    w.append(st_s[ob - 2][0])
                ve_d[it] = self.vop(lambda e, o=ddst, i=pss: e.tensor_copy(o, i[:, :]), waits=w)
                se_ex[it] = self.sop(lambda e, o=s2v, i=pss: e.activation(o, i[:, :], AF.Exp),
                                     waits=[(PE, pem)])
                w = [(VE, ve_f[it - 3])] if it - 3 in ve_f else []
                self.sop(lambda e, o=s1v, i=s2v: e.activation(o, i, AF.Ln,
                                                              bias=self.cst[:, 1:2]), waits=w)
                w = [(PE, rms_pe[it - 2])] if it - 2 in rms_pe else []
                se_sq[it] = self.sop(lambda e, o=aqv, i=s1v: e.activation(o, i, AF.Square),
                                     waits=w)

            def postA(it):
                ob, ch = divmod(it, 4)
                par = it % 2
                sbp = ob % 2
                s1v = sc1[:, (it % 3) * 512:(it % 3 + 1) * 512]
                aqv = asqb[:, par * 512:(par + 1) * 512]
                riv = rinvf[:, par * 512:(par + 1) * 512]
                cdst = cstg[:, sbp * S + ch * 512: sbp * S + ch * 512 + 512]
                sdst = sstg[:, sbp * S + ch * 512: sbp * S + ch * 512 + 512]
                cpv = cosP[:, ob * S + ch * 512: ob * S + ch * 512 + 512]
                spv = sinP[:, ob * S + ch * 512: ob * S + ch * 512 + 512]
                prs = self.ps_big[:, (2 + par) * 512:(3 + par) * 512]
                w = [(SE, se_sq[it])]
                if it - 2 in se_ri:
                    w.append((SE, se_ri[it - 2]))
                rms_pe[it] = self.mm(prs, self.bd[:, :], aqv, start=True, stop=True,
                                     inc=PE, waits=w)
                self.sop(lambda e, i=prs: e.activation(rln[:, :], i[:, :] if False else i, AF.Ln,
                                                       scale=1.0 / 64, bias=self.cst[:, 3:4]),
                         waits=[(PE, rms_pe[it])])
                wri = [(VE, ve_f[it - 2])] if it - 2 in ve_f else []
                se_ri[it] = self.sop(lambda e, o=riv: e.activation(o, rln[:, :], AF.Exp,
                                                                   scale=-0.5), waits=wri)
                if side == "q":
                    self.vop(lambda e, o=ampv[:, :], a=s1v, r=riv, ob=ob: e.scalar_tensor_tensor(
                        o, a, self.lamcol[:, ob:ob + 1], r, ALU.mult, ALU.mult),
                        waits=[(SE, se_ri[it])])
                else:
                    self.vop(lambda e, o=ampv[:, :], a=s1v, r=riv: e.tensor_tensor(
                        o, a, r, ALU.mult), waits=[(SE, se_ri[it])])
                wc = [st_s[ob - 2][1]] if (ch == 0 and (ob - 2) in st_s) else []
                self.vop(lambda e, o=cdst, c=cpv, am=ampv[:, :]: e.tensor_tensor(o, am, c, ALU.mult),
                         waits=wc)
                ws = [st_s[ob - 2][2]] if (ch == 0 and (ob - 2) in st_s) else []
                ve_f[it] = self.vop(lambda e, o=sdst, s=spv, am=ampv[:, :]: e.tensor_tensor(
                    o, am, s, ALU.mult), waits=ws)
                if ch == 3:
                    w = [(VE, ve_f[it]), (VE, ve_d[it])]
                    t_d = self.dma_s(d_s[:, ob * S:(ob + 1) * S],
                                     dstg[:, sbp * S:(sbp + 1) * S], waits=w)
                    t_c = self.dma_s(c_s[:, ob * S:(ob + 1) * S], cstg[:, sbp * S:(sbp + 1) * S])
                    t_s2 = self.dma_s(s_s[:, ob * S:(ob + 1) * S], sstg[:, sbp * S:(sbp + 1) * S])
                    st_s[ob] = (t_d, t_c, t_s2)

            nA = int(self.cfg.get("a1an", 32))
            for it in range(nA + 1):
                if it < nA:
                    projA(it)
                if it >= 1:
                    postA(it - 1)
            pe_last_a[0] = PE.v
            self.barrier()
            if self.cfg.get("upto") == "a1q":
                return

        self.mark(f'A1V_b{bi}')
        t_wv = self.dma_g(wp[:, :], self.wv_c[:, :])
        vst = self.sb("a1_vst", [128, 2 * 1040], BF16, ph)
        mv = self.vop(lambda e: e.memset(vst[:, :], 1.0))
        ve_v, st_v = {}, {}
        for sb in range(NSB):
            par = sb % 2
            psy = self.ps_big[:, par * 1024:(par + 1) * 1024]
            w = [t_wv, (VE, mv)] + ([(VE, ve_v[sb - 2])] if sb - 2 in ve_v else [])
            for db in range(NDB):
                for nn in range(2):
                    pem = self.mm(psy[:, nn * 512:(nn + 1) * 512],
                                  self.tTv(db, sb * 128, 128),
                                  wp[:, db * D + nn * 512: db * D + (nn + 1) * 512],
                                  start=(db == 0), stop=(db == NDB - 1),
                                  inc=(PE if db == NDB - 1 and nn == 1 else None),
                                  waits=(w if db == 0 and nn == 0 else ()))
            dst = vst[:, par * 1040:(par + 1) * 1040].rearrange("p (h c) -> p h c", h=H)[:, :, 0:64]
            src = psy.rearrange("p (h c) -> p h c", h=H)
            w = [(PE, pem)] + ([st_v[sb - 2]] if sb - 2 in st_v else [])
            ve_v[sb] = self.vop(lambda e, d=dst, s=src: e.tensor_copy(d, s), waits=w)
            st_v[sb] = self.dma_s(self.vx_s[:, sb * 1040:(sb + 1) * 1040],
                                  vst[:, par * 1040:(par + 1) * 1040],
                                  waits=[(VE, ve_v[sb])])
        self.barrier()


def emit_attn2(self, bi):
    self.mark(f'A2_b{bi}')
    VE, SE, PE, LDG, STS, GE = self.VE, self.SE, self.PE, self.LDG, self.STS, self.GE
    with contextlib.ExitStack() as ph:
        ps_s = [self.ps_big[:, 0:1024], self.ps_big[:, 1024:2048]]
        ps_pv = self.ps("a2_pv", [128, 1024], F32, ph)
        ps_bc = self.ps("a2_bc", [128, 1024], F32, ph)
        kdcb = self.sb("a2_kdc", [128, 4 * S], BF16, ph)
        qdcb = self.sb("a2_qdc", [128, 4 * S], BF16, ph)
        ksb = self.sb("a2_ks", [128, 2 * S], BF16, ph)
        qszb = self.sb("a2_qsz", [128, 4 * S], BF16, ph)
        vxb = self.sb("a2_vx", [128, 2 * NSB * 196], BF16, ph)
        ptb = self.sb("a2_pt", [128, 2 * 1024], BF16, ph)
        otb = self.sb("a2_ot", [128, HP * S], BF16, ph)
        opv = self.sb("a2_op", [64, 2 * 512], BF16, ph)
        onrm = self.sb("a2_on", [64, 2 * 512], BF16, ph)
        rowA = self.sb("a2_rA", [1, 2 * 512], F32, ph)
        rowB = self.sb("a2_rB", [1, 2 * 512], BF16, ph)
        csu = self.sb("a2_csu", [128, 128], BF16, ph)
        cs_v = self.vop(lambda e: e.tensor_scalar(csu[:, :], self.m_lt[:, :],
                                                  30000.0, -30000.0, ALU.mult, ALU.add))
        mz0 = self.vop(lambda e: e.memset(ps_s[0][:, :], 0.0))
        mz1 = self.vop(lambda e: e.memset(ps_s[1][:, :], 0.0))
        mzq = self.vop(lambda e: e.memset(qszb[:, :], 0.0))
        mzv = self.vop(lambda e: e.memset(vxb[:, :], 1.0))
        pair_pe_done = {}
        pv_guard = {}
        bc_guard = {}
        row_guard = {}
        on_guard = {}
        ld_hp = {}

        def load_pair(hp):
            par = hp % 2
            w = [(PE, pair_pe_done[hp - 2])] if hp - 2 in pair_pe_done else []
            toks = []
            for hl in range(2):
                for half, st in ((0, self.kd_s), (1, self.kc_s)):
                    toks.append(self.dma_g(
                        kdcb[64 * half:64 * half + 64,
                             par * 2 * S + hl * S:par * 2 * S + (hl + 1) * S],
                        st[hl * 64:hl * 64 + 64, hp * S:(hp + 1) * S], waits=w))
                    w = []
                for half, st in ((0, self.qd_s), (1, self.qc_s)):
                    toks.append(self.dma_g(
                        qdcb[64 * half:64 * half + 64,
                             par * 2 * S + hl * S:par * 2 * S + (hl + 1) * S],
                        st[hl * 64:hl * 64 + 64, hp * S:(hp + 1) * S]))
            toks.append(self.dma_g(ksb[:, par * S:(par + 1) * S],
                                   self.ks_s[:, hp * S:(hp + 1) * S]))
            wq = [(VE, mzq)] if hp < 2 else []
            for hl in range(2):
                toks.append(self.dma_g(
                    qszb[hl * 64:hl * 64 + 64,
                         par * 2 * S + hl * S:par * 2 * S + hl * S + S],
                    self.qs_s[hl * 64:hl * 64 + 64, hp * S:(hp + 1) * S], waits=wq))
                wq = []
            vsrc = self.vx_s.rearrange("p (sb c) -> p sb c", sb=NSB)[:, :, 130 * hp:130 * (hp + 1)]
            vdst = vxb[:, par * NSB * 196:(par + 1) * NSB * 196] \
                .rearrange("p (sb c) -> p sb c", sb=NSB)[:, :, 0:130]
            toks.append(self.dma_g(vdst, vsrc, waits=([(VE, mzv)] if hp < 2 else [])))
            return toks

        ld_hp[0] = load_pair(0)
        for hp in range(HP):
            par = hp % 2
            if hp + 1 < HP:
                ld_hp[hp + 1] = load_pair(hp + 1)
            for parity in range(2):
                h = 2 * hp + parity
                base = 64 * parity
                tp = (base, 0)
                se_ex = {}
                first_w = list(ld_hp[hp]) + [(VE, cs_v), (VE, mz0), (VE, mz1)]
                tiles = []
                for c in range(4):
                    for kb in range(4 * c + 4):
                        tiles.append((kb, c))
                pairs = [tiles[i:i + 2] for i in range(0, len(tiles), 2)]

                def emit_S_pair(pi, pr):
                    sl = pi % 2
                    pss = ps_s[sl]
                    nonlocal first_w
                    w = list(first_w)
                    first_w = []
                    if pi - 2 in se_ex:
                        w.append((SE, se_ex[pi - 2]))
                    pem = None
                    for half, (kb, c) in enumerate(pr):
                        col0 = max(kb * 128, 512 * c)
                        n = 512 * (c + 1) - col0
                        off = half * 512
                        b2 = par * 2 * S + parity * S
                        self.mm(pss[:, off:off + n],
                                kdcb[:, b2 + kb * 128: b2 + kb * 128 + 128],
                                qdcb[:, b2 + col0: b2 + col0 + n],
                                start=True, stop=False, waits=w)
                        w = []
                        diag = (col0 == kb * 128)
                        qz = par * 2 * S + parity * S
                        pem = self.mm(pss[:, off:off + n],
                                      ksb[:, par * S + kb * 128: par * S + kb * 128 + 128],
                                      qszb[:, qz + col0: qz + col0 + n],
                                      start=False, stop=not diag,
                                      inc=(None if diag else PE))
                        if diag:
                            pem = self.mm(pss[:, off:off + 128], csu[:, :], self.idn[:, :],
                                          start=False, stop=True, inc=PE)
                    kb1, c1 = pr[-1]
                    ntot = 512 + 512 * (c1 + 1) - max(kb1 * 128, 512 * c1) if len(pr) == 2 \
                        else 512
                    ex = self.sop(lambda e, o=ptb[:, sl * 1024: sl * 1024 + ntot],
                                  i=pss, ntot=ntot:
                                  e.activation(o, i[:, 0:ntot], AF.Exp, scale=0.125),
                                  waits=[(PE, pem)])
                    se_ex[pi] = ex
                    return (sl, pr, ex)

                def emit_PV_pair(p):
                    sl, pr, ex = p
                    for half, (kb, c) in enumerate(pr):
                        col0 = max(kb * 128, 512 * c)
                        n = 512 * (c + 1) - col0
                        s = c % 2
                        lv = vxb[:, par * NSB * 196:(par + 1) * NSB * 196] \
                            .rearrange("p (sb c) -> p sb c", sb=NSB)[:, kb, parity * 65:parity * 65 + 128]
                        w = [(SE, ex)]
                        if kb == 0:
                            w += pv_guard.pop(s, [])
                        pv = self.mm(ps_pv[:, s * 512 + col0 - 512 * c:
                                           s * 512 + col0 - 512 * c + n], lv,
                                     ptb[:, sl * 1024 + half * 512: sl * 1024 + half * 512 + n],
                                     start=(kb == 0), stop=(kb == 4 * c + 3),
                                     inc=PE, waits=w)
                        if kb == 4 * c + 3:
                            emit_chunk_norm(c, s, pv)

                def emit_chunk_norm(c, s, pv):
                    ov = opv[:, s * 512:(s + 1) * 512]
                    cp = self.vop(lambda e, ov=ov, s=s: e.tensor_copy(
                        ov, ps_pv[0:64, s * 512:(s + 1) * 512]), waits=[(PE, pv)])
                    wl = [(PE, pv)] + ([row_guard[s]] if s in row_guard else [])
                    s1 = self.sop(lambda e, s=s: e.activation(
                        rowA[0:1, s * 512:(s + 1) * 512],
                        ps_pv[64:65, s * 512:(s + 1) * 512], AF.Ln), waits=wl)
                    s2 = self.sop(lambda e, s=s: e.activation(
                        rowB[0:1, s * 512:(s + 1) * 512],
                        rowA[0:1, s * 512:(s + 1) * 512], AF.Exp, scale=-1.0))
                    pv_guard[s] = [(VE, cp), (SE, s2)]
                    wb = [(SE, s2)] + ([bc_guard[s]] if s in bc_guard else [])
                    bc = self.mm(ps_bc[0:64, s * 512:(s + 1) * 512], self.ones1x64[:, :],
                                 rowB[0:1, s * 512:(s + 1) * 512], start=True, stop=True,
                                 inc=PE, waits=wb)
                    row_guard[s] = (PE, bc)
                    blk = otb[:, hp * S + c * 512: hp * S + (c + 1) * 512]
                    if parity == 0:
                        tt = self.vop(lambda e, blk=blk, ov=ov, s=s: e.tensor_tensor(
                            blk[0:64, :], ov, ps_bc[0:64, s * 512:(s + 1) * 512], ALU.mult),
                            waits=[(PE, bc)])
                        bc_guard[s] = (VE, tt)
                    else:
                        osl = c % 2
                        onv = onrm[:, osl * 512:(osl + 1) * 512]
                        w2 = [(PE, bc)] + ([on_guard[osl]] if osl in on_guard else [])
                        tt = self.vop(lambda e, onv=onv, ov=ov, s=s: e.tensor_tensor(
                            onv, ov, ps_bc[0:64, s * 512:(s + 1) * 512], ALU.mult), waits=w2)
                        bc_guard[s] = (VE, tt)
                        sv = self.sop(lambda e, blk=blk, onv=onv: e.activation(
                            blk[64:128, :], onv, AF.Copy), waits=[(VE, tt)])
                        on_guard[osl] = (SE, sv)

                pend = None
                for pi, pr in enumerate(pairs):
                    cur = emit_S_pair(pi, pr)
                    if pend is not None:
                        emit_PV_pair(pend)
                    pend = cur
                emit_PV_pair(pend)
                pair_pe_done[hp] = PE.v
        self.barrier()
        self.tap("otb", otb[:, :])

        self.mark(f'A2Wo_b{bi}')
        wo = self.sb("a2_wo", [128, NDB * D], BF16, ph)
        self.lnjunk = self.sb("a2_junk", [128, 1024], F32, ph)
        x0t = self.sb("a2_x0", [128, 4 * D], F32, ph)
        z1t = self.sb("a2_z1", [128, 4 * D], F32, ph)
        zsc = self.sb("a2_zs", [128, 4 * D], F32, ph)
        t_wo = self.dma_g(wo[:, :], self.wo_c[:, :])
        ve_ap, ve_z, st_z = {}, {}, {}
        for sb in range(NSB):
            par = sb % 2
            p4 = sb % 4
            w = [(VE, ve_z[sb - 4])] if sb - 4 in ve_z else []
            ldx = self.dma_g(x0t[:, p4 * D:(p4 + 1) * D], x_view(self, bi, sb), waits=w)
            psy = self.ps_big[:, par * 1024:(par + 1) * 1024]
            w = [t_wo] + ([(VE, ve_ap[sb - 2])] if sb - 2 in ve_ap else [])
            for db in range(NDB):
                for nn in range(2):
                    pem = self.mm(psy[:, nn * 512:(nn + 1) * 512],
                                  otb[:, db * S + sb * 128: db * S + sb * 128 + 128],
                                  wo[:, db * D + nn * 512: db * D + (nn + 1) * 512],
                                  start=(db == 0), stop=(db == NDB - 1),
                                  inc=(PE if db == NDB - 1 and nn == 1 else None),
                                  waits=(w if db == 0 and nn == 0 else ()))
            c = p4 * 16
            emit_ln_stats(self, c, psy, pe_wait=(PE, pem))
            zv = z1t[:, p4 * D:(p4 + 1) * D]
            zs = zsc[:, p4 * D:(p4 + 1) * D]
            w = [(VE, self.VE.v), ldx] + ([st_z[sb - 4]] if sb - 4 in st_z else [])
            ve_ap[sb] = self.vop(lambda e, zs=zs, c=c, psy=psy: e.tensor_scalar(
                zs, psy, self.stat[:, c + 4:c + 5],
                self.stat[:, c + 5:c + 6], ALU.mult, ALU.add), waits=w)
            ve_z[sb] = self.vop(lambda e, zv=zv, zs=zs, x0=x0t[:, p4 * D:(p4 + 1) * D]:
                                e.tensor_tensor(zv, zs, x0, ALU.add))
            st_z[sb] = self.dma_s(self.z1_s[:, sb * D:(sb + 1) * D], zv,
                                  waits=[(VE, ve_z[sb])])
        self.barrier()


"""FFN (2-layer local-attention transformer) phases + final combine. Mixin for MK."""
import contextlib
import concourse.mybir as mybir

F32 = mybir.dt.float32
BF16 = mybir.dt.bfloat16
AF = mybir.ActivationFunctionType
ALU = mybir.AluOpType
AX = mybir.AxisListType

S, D = 2048, 1024
NSB, NDB = 16, 8
FH, FHD, FIN, WIN = 8, 64, 512, 512
HID, HID2, NJ = 2730, 5460, 22


def emit_ffn(self, bi, l):
    with contextlib.ExitStack() as tstack:
        self.tT = self.sb("tT_f", [128, 8 * 2049], BF16, tstack)
        self.ps_a = self.ps("pa", [128, 512], F32, tstack)
        self.ps_b = self.ps("pb", [128, 512], F32, tstack)
        tT_now = self.tT
        mz_t = self.vop(lambda e, t=tT_now: e.memset(t[:, :], 0.0))
        emit_f0(self, bi, l, mz_t)
        with contextlib.ExitStack() as s1:
            vxf = self.sb("f1_vx", [128, NSB * 584], BF16, s1)
            qrTz = self.sb("f2_qTz", [128, 8 * S], BF16, s1)
            krT = self.sb("f2_kT", [128, 4 * S], BF16, s1)
            with contextlib.ExitStack() as s2:
                qrot = self.sb("f1_qr", [128, NSB * 512], BF16, s2)
                krot = self.sb("f1_kr", [128, NSB * 512], BF16, s2)
                emit_f1(self, bi, l, qrot, krot, vxf)
                emit_f2(self, qrot, krot, qrTz, krT)
            with contextlib.ExitStack() as s3:
                oT = self.sb("f3_oT", [128, 4 * S], BF16, s3)
                emit_f3(self, bi, l, qrTz, krT, vxf, oT)
                emit_f4(self, bi, l, oT)
        emit_f56(self, bi, l)
    self.tT = None


def emit_f0(self, bi, l, mz_t):
    self.mark(f'F0_b{bi}l{l}')
    """x_f stream tile-wise: l==0: LN(z1) -> f0_s spill + bf16 -> transpose;
    l==1: f0_s -> bf16 -> transpose."""
    VE, SE, PE, LDG, STS = self.VE, self.SE, self.PE, self.LDG, self.STS
    with contextlib.ExitStack() as ph:
        self.ps_tr = self.ps("tr", [128, 1024], BF16, ph)
        self.ps_tr2 = self.ps("tr2", [128, 1024], BF16, ph)
        xnb = self.sb("f0_xn", [128, 4 * D], BF16, ph)
        self.lnjunk = self.sb("f0_junk", [128, 1024], F32, ph)
        z1t = self.sb("f0_z1", [128, 4 * D], F32, ph)
        xft = self.sb("f0_xf", [128, 4 * D], F32, ph)
        se_app, ve_ev, pe_tr, st_f = {}, {}, {}, {}
        for sb in range(NSB):
            par = sb % 2
            p4 = sb % 4
            xn = xnb[:, p4 * D:(p4 + 1) * D]
            if l == 0:
                z1 = z1t[:, p4 * D:(p4 + 1) * D]
                xfv = xft[:, p4 * D:(p4 + 1) * D]
                w = [(SE, se_app[sb - 4])] if sb - 4 in se_app else []
                tld = self.dma_g(z1, self.z1_s[:, sb * D:(sb + 1) * D], waits=w)
                c = p4 * 16
                emit_ln_stats(self, c, z1, pe_wait=tld)
                w = [(VE, self.VE.v)] + ([st_f[sb - 4]] if sb - 4 in st_f else [])
                sf = self.sop(lambda e, o=xfv, i=z1, c=c: e.activation(
                    o, i, AF.Identity, scale=self.stat[:, c + 4:c + 5],
                    bias=self.stat[:, c + 5:c + 6]), waits=w)
                se_app[sb] = sf
                st_f[sb] = self.dma_s(self.f0_s[:, sb * D:(sb + 1) * D], xfv,
                                      waits=[(SE, sf)])
                w = [(SE, sf)] + ([(PE, pe_tr[sb - 4])] if sb - 4 in pe_tr else [])
                ve_c = self.vop(lambda e, o=xn, i=xfv: e.tensor_copy(o, i), waits=w)
            else:
                xfv = xft[:, p4 * D:(p4 + 1) * D]
                w = [(SE, se_app[sb - 4])] if sb - 4 in se_app else []
                tld = self.dma_g(xfv, self.f0_s[:, sb * D:(sb + 1) * D], waits=w)
                w = [tld] + ([(PE, pe_tr[sb - 4])] if sb - 4 in pe_tr else [])
                se_app[sb] = self.sop(lambda e, o=xn, i=xfv: e.activation(o, i, AF.Copy),
                                      waits=w)
                ve_c = self.VE.v
            pst = self.ps_tr if par == 0 else self.ps_tr2
            if l == 0:
                w = [(VE, ve_c)] + ([(VE, ve_ev[sb - 2])] if sb - 2 in ve_ev else [])
            else:
                w = [(SE, se_app[sb])] + ([(VE, ve_ev[sb - 2])] if sb - 2 in ve_ev else [])
            for db in range(NDB):
                pe_tr[sb] = self.tr(pst[:, db * 128:(db + 1) * 128],
                                    xn[:, db * 128:(db + 1) * 128], inc=PE,
                                    waits=(w if db == 0 else ()))
            dst = self.tT[:, :].rearrange("p (db c) -> p db c", db=NDB)[:, :, 1 + sb * 128: 129 + sb * 128]
            src = pst[:, :].rearrange("p (k c) -> p k c", k=NDB)
            ve_ev[sb] = self.vop(lambda e, d=dst, s=src: e.tensor_copy(d, s),
                                 waits=[(PE, pe_tr[sb])] + ([(VE, mz_t)] if sb == 0 else []))
        self.barrier()


def emit_f1(self, bi, l, qrot, krot, vxf):
    self.mark(f'F1_b{bi}l{l}')
    VE, SE, PE, LDG, STS, GE = self.VE, self.SE, self.PE, self.LDG, self.STS, self.GE
    with contextlib.ExitStack() as ph:
        wqkv = self.sb("f1_w", [128, NDB * 1536], BF16, ph)
        qsb_t = self.sb("f1_qs", [128, 512], BF16, ph)
        ksb_t = self.sb("f1_ks", [128, 512], BF16, ph)
        rot_t = self.sb("f1_rot", [128, 2 * 2048], BF16, ph)
        sq_t = self.sb("f1_sqt", [128, 2 * 512], F32, ph)
        qn_t = self.sb("f1_qn", [128, 1024], F32, ph)
        qn2_t = self.sb("f1_qn2", [128, 1024], F32, ph)
        r1_t = self.sb("f1_r1", [128, 1024], F32, ph)
        r2_t = self.sb("f1_r2", [128, 1024], F32, ph)
        t_w = self.dma_g(wqkv[:, :], self.fwqkv_c[l][:, :])
        t_qs = t_ks = None
        for hh in range(FH):
            t_qs = self.dma_g(qsb_t[:, hh * FHD:(hh + 1) * FHD],
                              self.FQS[l][None, :].partition_broadcast(128),
                              waits=([t_qs] if t_qs else []))
            t_ks = self.dma_g(ksb_t[:, hh * FHD:(hh + 1) * FHD],
                              self.FKS[l][None, :].partition_broadcast(128),
                              waits=([t_ks] if t_ks else []))
        mv = self.vop(lambda e: e.memset(vxf[:, :], 1.0))
        ve_done, ve_rot = {}, {}
        for sb in range(NSB):
            par = sb % 2
            t_rots = []
            for ti, tab in enumerate((self.cRQC, self.cRQS, self.cRKC, self.cRKS)):
                wt = [(VE, ve_rot[sb - 2])] if sb - 2 in ve_rot else []
                t_rots.append(self.dma_g(
                    rot_t[:, par * 2048 + ti * 512: par * 2048 + (ti + 1) * 512],
                    tab[sb * 128:(sb + 1) * 128, :], waits=wt))
            if par == 0:
                psums = [self.ps_big[:, 0:512], self.ps_big[:, 512:1024],
                         self.ps_big[:, 1024:1536]]
            else:
                psums = [self.ps_big[:, 1536:2048], self.ps_a[:, :], self.ps_b[:, :]]
            w = [t_w, t_qs, t_ks, (VE, mv)]
            if sb - 2 in ve_done:
                w.append((VE, ve_done[sb - 2]))
            pem = None
            for db in range(NDB):
                lhs = self.tTv(db, sb * 128, 128) if db < 4 else self.tTs(db, sb * 128, 128)
                for nn in range(3):
                    pem = self.mm(psums[nn], lhs,
                                  wqkv[:, db * 1536 + nn * 512: db * 1536 + (nn + 1) * 512],
                                  start=(db == 0), stop=(db == NDB - 1),
                                  inc=(PE if db == NDB - 1 and nn == 2 else None),
                                  waits=(w if db == 0 and nn == 0 else ()))
            last_rot = None
            for side in range(2):
                psv = psums[side]
                sqv = sq_t[:, par * 512:(par + 1) * 512]
                wsq = [(PE, pem)] + ([(VE, ve_done[sb - 2])] if sb - 2 in ve_done else [])
                sv = self.sop(lambda e, o=sqv, i=psv: e.activation(o, i, AF.Square),
                              waits=wsq)
                c = 32 + par * 32 + side * 16
                rv = self.vop(lambda e, c=c, sqv=sqv: e.tensor_reduce(
                    self.stat[:, c:c + 8], sqv.rearrange("p (h d) -> p h d", h=FH),
                    AX.X, ALU.add), waits=[(SE, sv)])
                rv = self.vop(lambda e, c=c: e.tensor_scalar(
                    self.stat[:, c + 8:c + 16], self.stat[:, c:c + 8], 1e-24, None, ALU.max),
                    waits=[(VE, rv)])
                s_l = self.sop(lambda e, c=c: e.activation(
                    self.stat[:, c:c + 8], self.stat[:, c + 8:c + 16], AF.Ln),
                    waits=[(VE, rv)])
                sv = self.sop(lambda e, c=c: e.activation(
                    self.stat[:, c + 8:c + 16], self.stat[:, c:c + 8], AF.Exp, scale=-0.5),
                    waits=[(SE, s_l)])
                qv = qn_t[:, side * 512:(side + 1) * 512]
                q2v = qn2_t[:, side * 512:(side + 1) * 512]
                sct = qsb_t if side == 0 else ksb_t
                self.vop(lambda e, qv=qv, i=psv, s=sct: e.tensor_tensor(qv, i, s[:, :], ALU.mult))
                for hh in range(FH):
                    self.vop(lambda e, hh=hh, c=c, qv=qv, q2v=q2v: e.tensor_scalar(
                        q2v[:, hh * 64:(hh + 1) * 64], qv[:, hh * 64:(hh + 1) * 64],
                        self.stat[:, c + 8 + hh:c + 9 + hh], None, ALU.mult),
                        waits=([(SE, sv)] if hh == 0 else ()))
                ct = rot_t[:, par * 2048 + (2 * side) * 512: par * 2048 + (2 * side) * 512 + 512]
                st_ = rot_t[:, par * 2048 + (2 * side + 1) * 512: par * 2048 + (2 * side + 1) * 512 + 512]
                t1 = r1_t[:, side * 512:(side + 1) * 512]
                t2 = r2_t[:, side * 512:(side + 1) * 512]
                self.vop(lambda e, t1=t1, q2v=q2v, ct=ct: e.tensor_tensor(t1, q2v, ct, ALU.mult),
                         waits=t_rots)
                q23 = q2v.rearrange("p (h two d) -> p h two d", h=FH, two=2)
                st3 = st_.rearrange("p (h two d) -> p h two d", h=FH, two=2)
                t23 = t2.rearrange("p (h two d) -> p h two d", h=FH, two=2)
                self.vop(lambda e, t23=t23, q23=q23, st3=st3: e.tensor_tensor(
                    t23[:, :, 0, :], q23[:, :, 1, :], st3[:, :, 0, :], ALU.mult))
                self.vop(lambda e, t23=t23, q23=q23, st3=st3: e.tensor_tensor(
                    t23[:, :, 1, :], q23[:, :, 0, :], st3[:, :, 1, :], ALU.mult))
                dstq = (qrot if side == 0 else krot)[:, sb * 512:(sb + 1) * 512]
                last_rot = self.vop(lambda e, d=dstq, t1=t1, t2=t2: e.tensor_tensor(
                    d, t1, t2, ALU.add))
            ve_rot[sb] = last_rot
            dst = vxf[:, sb * 584: sb * 584 + 520].rearrange("p (h c) -> p h c", h=FH)[:, :, 0:64]
            src = psums[2].rearrange("p (h c) -> p h c", h=FH)
            ve_done[sb] = self.vop(lambda e, d=dst, s=src: e.tensor_copy(d, s))
        self.barrier()


def emit_f2(self, qrot, krot, qrTz, krT):
    self.mark('F2')
    VE, SE, PE = self.VE, self.SE, self.PE
    with contextlib.ExitStack() as ph:
        self.ps_tr = self.ps("tr", [128, 1024], BF16, ph)
        self.ps_tr2 = self.ps("tr2", [128, 1024], BF16, ph)
        mzq = self.vop(lambda e: e.memset(qrTz[:, :], 0.0))
        ve_ev2, pe_tr2 = {}, {}
        qz = qrTz[:, :].rearrange("p (pb two c) -> p pb two c", pb=4, two=2)
        for sb in range(NSB):
            par = sb % 2
            pst = self.ps_tr if par == 0 else self.ps_tr2
            w = ([(VE, ve_ev2[sb - 2])] if sb - 2 in ve_ev2 else [])
            for pb in range(4):
                self.tr(pst[:, pb * 128:(pb + 1) * 128],
                        qrot[:, sb * 512 + pb * 128: sb * 512 + (pb + 1) * 128], inc=PE,
                        waits=(w if pb == 0 else ()))
            for pb in range(4):
                pe_tr2[sb] = self.tr(pst[:, 512 + pb * 128: 512 + (pb + 1) * 128],
                                     krot[:, sb * 512 + pb * 128: sb * 512 + (pb + 1) * 128],
                                     inc=PE)
            dqe = qz[0:64, :, 0, sb * 128:(sb + 1) * 128]
            dqo = qz[64:128, :, 1, sb * 128:(sb + 1) * 128]
            dk = krT[:, :].rearrange("p (pb c) -> p pb c", pb=4)[:, :, sb * 128:(sb + 1) * 128]
            sq_ = pst[:, 0:512].rearrange("p (k c) -> p k c", k=4)
            sk_ = pst[:, 512:1024].rearrange("p (k c) -> p k c", k=4)
            self.vop(lambda e, d=dqe, s=sq_: e.tensor_copy(d, s[0:64]),
                     waits=[(PE, pe_tr2[sb])] + ([(VE, mzq)] if sb == 0 else []))
            self.vop(lambda e, d=dqo, s=sq_: e.tensor_copy(d, s[64:128]))
            ve_ev2[sb] = self.vop(lambda e, d=dk, s=sk_: e.tensor_copy(d, s))
        self.barrier()


def emit_f3(self, bi, l, qrTz, krT, vxf, oT):
    self.mark(f'F3_b{bi}l{l}')
    VE, SE, PE, LDG, STS, GE = self.VE, self.SE, self.PE, self.LDG, self.STS, self.GE
    with contextlib.ExitStack() as ph:
        ps_bc = self.ps("f3_bc", [128, 1024], F32, ph)
        ptf = self.sb("f3_pt", [128, 8 * 512], BF16, ph)
        opv = self.sb("f3_op", [64, 2 * 512], BF16, ph)
        onrm = self.sb("f3_on", [64, 2 * 512], BF16, ph)
        csl = self.sb("f3_csl", [128, 128], BF16, ph)
        csu = self.sb("f3_csu", [128, 128], BF16, ph)
        rowA = self.sb("f3_rA", [1, 2 * 512], F32, ph)
        rowB = self.sb("f3_rB", [1, 2 * 512], BF16, ph)
        mz = self.vop(lambda e: e.memset(ptf[:, :], 0.0))
        self.vop(lambda e: e.tensor_scalar(csu[:, :], self.m_lt[:, :], 30000.0, -30000.0,
                                           ALU.mult, ALU.add))
        cs_v = self.vop(lambda e: e.tensor_scalar(csl[:, :], self.m_ut[:, :],
                                                  30000.0, -30000.0, ALU.mult, ALU.add))
        norm_fin, row_guard, bc_guard, on_guard, se_ex = {}, {}, {}, {}, {}
        sl_n = [0]
        units = []
        for w_ in range(4):
            for hp in range(4):
                for parity in range(2):
                    units.append((w_, hp, parity))

        def emit_S(w_, hp, parity, j):
            h = 2 * hp + parity
            a0 = max(0, j - 4)
            a1 = min(3, j)
            col0, n = a0 * 128, (a1 - a0 + 1) * 128
            kb_abs = 4 * (w_ - 1) + j
            sn = sl_n[0]
            sl_n[0] += 1
            sl = sn % 2
            pss = self.ps_a if sl == 0 else self.ps_b
            wt = [(VE, cs_v), (VE, mz)] if sn == 0 else []
            if sn - 2 in se_ex:
                wt.append((SE, se_ex[sn - 2]))
            self.mm(pss[:, 0:n],
                    krT[:, hp * S + kb_abs * 128: hp * S + kb_abs * 128 + 128],
                    qrTz[:, h * S + w_ * 512 + col0: h * S + w_ * 512 + col0 + n],
                    start=True, stop=False, waits=wt)
            if j >= 4:
                pem = self.mm(pss[:, 0:128], csu[:, :], self.idn[:, :],
                              start=False, stop=True, inc=PE)
            else:
                off = (a1 - a0) * 128
                pem = self.mm(pss[:, off:off + 128], csl[:, :], self.idn[:, :],
                              start=False, stop=True, inc=PE)
            ex = self.sop(lambda e, o=ptf[:, j * 512 + col0: j * 512 + col0 + n],
                          i=pss, n=n: e.activation(o, i[:, 0:n], AF.Exp, scale=8.0),
                          waits=[(PE, pem)])
            se_ex[sn] = ex
            return ex

        def emit_PV(it, w_, hp, parity, j, first, last, ex):
            h = 2 * hp + parity
            bank = it % 4
            kb_abs = 4 * (w_ - 1) + j
            wv = [(SE, ex)]
            if first and it - 4 in norm_fin:
                wv += norm_fin[it - 4]
            return self.mm(self.ps_big[:, bank * 512:(bank + 1) * 512],
                           vxf[:, kb_abs * 584 + h * 65: kb_abs * 584 + h * 65 + 128],
                           ptf[:, j * 512:(j + 1) * 512],
                           start=first, stop=last, inc=PE, waits=wv)

        def emit_norm(it, w_, hp, parity, pv):
            bank = it % 4
            rslot = it % 2
            ov = opv[:, rslot * 512:(rslot + 1) * 512]
            cp = self.vop(lambda e, ov=ov, bank=bank: e.tensor_copy(
                ov, self.ps_big[0:64, bank * 512:(bank + 1) * 512]), waits=[(PE, pv)])
            w = [(PE, pv)] + ([row_guard[rslot]] if rslot in row_guard else [])
            s1 = self.sop(lambda e, bank=bank, rslot=rslot: e.activation(
                rowA[0:1, rslot * 512:(rslot + 1) * 512],
                self.ps_big[64:65, bank * 512:(bank + 1) * 512], AF.Ln), waits=w)
            s2 = self.sop(lambda e, rslot=rslot: e.activation(
                rowB[0:1, rslot * 512:(rslot + 1) * 512],
                rowA[0:1, rslot * 512:(rslot + 1) * 512], AF.Exp, scale=-1.0))
            norm_fin[it] = [(VE, cp), (SE, s2)]
            wb = [(SE, s2)] + ([bc_guard[rslot]] if rslot in bc_guard else [])
            bc = self.mm(ps_bc[0:64, rslot * 512:(rslot + 1) * 512], self.ones1x64[:, :],
                         rowB[0:1, rslot * 512:(rslot + 1) * 512], start=True, stop=True,
                         inc=PE, waits=wb)
            row_guard[rslot] = (PE, bc)
            if parity == 0:
                tt = self.vop(lambda e, hp=hp, w_=w_, ov=ov, rslot=rslot: e.tensor_tensor(
                    oT[0:64, hp * S + w_ * 512: hp * S + (w_ + 1) * 512],
                    ov, ps_bc[0:64, rslot * 512:(rslot + 1) * 512], ALU.mult),
                    waits=[(PE, bc)])
                bc_guard[rslot] = (VE, tt)
            else:
                osl = (it // 2) % 2
                onv = onrm[:, osl * 512:(osl + 1) * 512]
                w2 = [(PE, bc)] + ([on_guard[osl]] if osl in on_guard else [])
                tt = self.vop(lambda e, onv=onv, ov=ov, rslot=rslot: e.tensor_tensor(
                    onv, ov, ps_bc[0:64, rslot * 512:(rslot + 1) * 512], ALU.mult), waits=w2)
                bc_guard[rslot] = (VE, tt)
                sv = self.sop(lambda e, hp=hp, w_=w_, onv=onv: e.activation(
                    oT[64:128, hp * S + w_ * 512: hp * S + (w_ + 1) * 512],
                    onv, AF.Copy), waits=[(VE, tt)])
                on_guard[osl] = (SE, sv)

        flat = []
        for ui, (w_, hp, parity) in enumerate(units):
            jlist = list(range(4, 8)) if w_ == 0 else list(range(8))
            for jn, j in enumerate(jlist):
                flat.append((ui, w_, hp, parity, j, jn == 0, jn == len(jlist) - 1))
        pend = None
        for rec in flat:
            ex = emit_S(rec[1], rec[2], rec[3], rec[4])
            if pend is not None:
                pv = emit_PV(pend[0], pend[1], pend[2], pend[3], pend[4], pend[5], pend[6],
                             pend[7])
                if pend[6]:
                    emit_norm(pend[0], pend[1], pend[2], pend[3], pv)
            pend = rec + (ex,)
        pv = emit_PV(pend[0], pend[1], pend[2], pend[3], pend[4], pend[5], pend[6], pend[7])
        emit_norm(pend[0], pend[1], pend[2], pend[3], pv)
        self.barrier()


def emit_f4(self, bi, l, oT):
    self.mark(f'F4_b{bi}l{l}')
    VE, SE, PE, LDG, STS, GE = self.VE, self.SE, self.PE, self.LDG, self.STS, self.GE
    with contextlib.ExitStack() as ph:
        self.ps_tr = self.ps("tr", [128, 1024], BF16, ph)
        self.ps_tr2 = self.ps("tr2", [128, 1024], BF16, ph)
        wo = self.sb("f4_wo", [128, 4 * D], BF16, ph)
        self.lnjunk = self.sb("f4_junk", [128, 1024], F32, ph)
        ut = self.sb("f4_u", [128, 4 * D], F32, ph)
        xft = self.sb("f4_xf", [128, 4 * D], F32, ph)
        shv4 = self.sb("f4_sh", [128, 4 * 512], F32, ph)
        h2nb = self.sb("f4_hn", [128, 2 * D], BF16, ph)
        lngb = self.sb("f4_lg", [128, D], BF16, ph)
        lnbb = self.sb("f4_lb", [128, D], BF16, ph)
        row2 = self.sb("f4_r2", [128, 4 * 1024], F32, ph)
        row3 = self.sb("f4_r3", [128, 2 * 1024], F32, ph)
        st4 = self.sb("f4_st", [128, 64], F32, ph)
        t_wo = self.dma_g(wo[:, :], self.fwo_c[l][:, :])
        t_lg = self.dma_g(lngb[:, :], self.FLG[l][None, :].partition_broadcast(128))
        t_lb = self.dma_g(lnbb[:, :], self.FLB[l][None, :].partition_broadcast(128))
        ve_u, st_u, se_ap, ve_hn, pe_tr, ve_ev, sh_pem, cp_sh = {}, {}, {}, {}, {}, {}, {}, {}
        for sb in range(NSB):
            par = sb % 2
            p4 = sb % 4
            wx = [(VE, ve_u[sb - 4])] if sb - 4 in ve_u else []
            ldx = self.dma_g(xft[:, p4 * D:(p4 + 1) * D],
                             self.f0_s[:, sb * D:(sb + 1) * D], waits=wx)
            psy = self.ps_big[:, par * 1024:(par + 1) * 1024]
            w = [t_wo] + ([(VE, ve_u[sb - 2])] if sb - 2 in ve_u else [])
            pem = None
            for pb in range(4):
                for nn in range(2):
                    pem = self.mm(psy[:, nn * 512:(nn + 1) * 512],
                                  oT[:, pb * S + sb * 128: pb * S + sb * 128 + 128],
                                  wo[:, pb * D + nn * 512: pb * D + (nn + 1) * 512],
                                  start=(pb == 0), stop=(pb == 3),
                                  inc=(PE if pb == 3 and nn == 1 else None),
                                  waits=(w if pb == 0 and nn == 0 else ()))
            uv = ut[:, p4 * D:(p4 + 1) * D]
            w = [(PE, pem), ldx]
            if sb - 4 in st_u:
                w += [st_u[sb - 4], (SE, se_ap[sb - 4])]
            ve_u[sb] = self.vop(lambda e, uv=uv, xf=xft[:, p4 * D:(p4 + 1) * D], psy=psy:
                                e.tensor_tensor(uv, psy, xf, ALU.add), waits=w)
            st_u[sb] = self.dma_s(self.u_s[:, sb * D:(sb + 1) * D], uv, waits=[(VE, ve_u[sb])])
            psv = self.ps_a if par == 0 else self.ps_b
            w = [(VE, ve_u[sb])] + ([(VE, cp_sh[sb - 2])] if sb - 2 in cp_sh else [])
            m1 = self.mm(psv[:, :], self.shiftT[:, :], uv[:, 512:1024],
                         start=True, stop=(sb == 0), inc=(PE if sb == 0 else None),
                         waits=w)
            if sb > 0:
                pu = ut[:, ((sb - 1) % 4) * D:((sb - 1) % 4) * D + D]
                m1 = self.mm(psv[:, :], self.e127T[:, :], pu[:, 512:1024],
                             start=False, stop=True, inc=PE)
            sh_pem[sb] = m1
            shp = shv4[:, p4 * 512:(p4 + 1) * 512]
            cp_sh[sb] = self.vop(lambda e, shp=shp, psv=psv: e.tensor_copy(shp, psv[:, :]),
                                 waits=[(PE, m1)])
            c = p4 * 16
            st = st4
            junk = self.lnjunk
            w = [(VE, cp_sh[sb])]
            self.sop(lambda e, uv=uv, c=c, j=junk: e.activation(
                j[:, 0:512], uv[:, 0:512], AF.Identity,
                accum_out=st[:, c + 6:c + 7]), waits=w)
            self.sop(lambda e, shp=shp, c=c, j=junk: e.activation(
                j[:, 0:512], shp, AF.Identity, accum_out=st[:, c + 8:c + 9]))
            self.sop(lambda e, uv=uv, c=c, j=junk: e.activation(
                j[:, 0:512], uv[:, 0:512], AF.Square,
                accum_out=st[:, c + 7:c + 8]))
            s2 = self.sop(lambda e, shp=shp, c=c, j=junk: e.activation(
                j[:, 0:512], shp, AF.Square, accum_out=st[:, c + 9:c + 10]))
            v1 = self.vop(lambda e, c=c: e.tensor_tensor(
                st[:, c + 10:c + 11], st[:, c + 6:c + 7], st[:, c + 8:c + 9], ALU.add),
                waits=[(SE, s2)])
            v1 = self.vop(lambda e, c=c: e.tensor_tensor(
                st[:, c + 11:c + 12], st[:, c + 7:c + 8], st[:, c + 9:c + 10], ALU.add),
                waits=[(VE, v1)])
            v1 = self.vop(lambda e, c=c: e.tensor_scalar(st[:, c:c + 1], st[:, c + 10:c + 11],
                                                         1.0 / 1024, None, ALU.mult),
                          waits=[(VE, v1)])
            v1 = self.vop(lambda e, c=c: e.tensor_tensor(st[:, c + 2:c + 3], st[:, c:c + 1],
                                                         st[:, c:c + 1], ALU.mult),
                          waits=[(VE, v1)])
            v1 = self.vop(lambda e, c=c: e.scalar_tensor_tensor(
                st[:, c + 1:c + 2], st[:, c + 11:c + 12], 1.0 / 1024, st[:, c + 2:c + 3],
                ALU.mult, ALU.subtract), waits=[(VE, v1)])
            s3 = self.sop(lambda e, c=c: e.activation(st[:, c + 3:c + 4], st[:, c + 1:c + 2],
                                                      AF.Ln, bias=self.cst[:, 0:1]),
                          waits=[(VE, v1)])
            s2 = self.sop(lambda e, c=c: e.activation(st[:, c + 4:c + 5], st[:, c + 3:c + 4],
                                                      AF.Exp, scale=-0.5), waits=[(SE, s3)])
            tv2 = self.vop(lambda e, c=c: e.scalar_tensor_tensor(
                st[:, c + 5:c + 6], st[:, c:c + 1], -1.0, st[:, c + 4:c + 5],
                ALU.mult, ALU.mult), waits=[(SE, s2)])
            hv = h2nb[:, par * D:(par + 1) * D]
            r2v = row2[:, p4 * D:(p4 + 1) * D]
            r3v = row3[:, par * D:(par + 1) * D]
            w = [(VE, tv2), t_lg, t_lb]
            if sb - 4 in ve_hn:
                w.append((VE, ve_hn[sb - 4]))
            self.sop(lambda e, uv=uv, c=c, r2v=r2v: e.activation(
                r2v[:, 0:512], uv[:, 0:512], AF.Identity,
                scale=st[:, c + 4:c + 5], bias=st[:, c + 5:c + 6]), waits=w)
            se_ap[sb] = self.sop(lambda e, shp=shp, c=c, r2v=r2v: e.activation(
                r2v[:, 512:1024], shp, AF.Identity,
                scale=st[:, c + 4:c + 5], bias=st[:, c + 5:c + 6]))
            self.vop(lambda e, r2v=r2v, r3v=r3v: e.tensor_tensor(
                r3v, r2v, lngb[:, :], ALU.mult), waits=[(SE, se_ap[sb])])
            ve_hn[sb] = self.vop(lambda e, hv=hv, r3v=r3v: e.tensor_tensor(
                hv, r3v, lnbb[:, :], ALU.add))
            pst = self.ps_tr if par == 0 else self.ps_tr2
            w = [(VE, ve_hn[sb])] + ([(VE, ve_ev[sb - 2])] if sb - 2 in ve_ev else [])
            for db in range(NDB):
                pe_tr[sb] = self.tr(pst[:, db * 128:(db + 1) * 128],
                                    hv[:, db * 128:(db + 1) * 128], inc=PE,
                                    waits=(w if db == 0 else ()))
            dst = self.tT[:, :].rearrange("p (db c) -> p db c", db=NDB)[:, :, 1 + sb * 128: 129 + sb * 128]
            src = pst[:, :].rearrange("p (k c) -> p k c", k=NDB)
            ve_ev[sb] = self.vop(lambda e, d=dst, s=src: e.tensor_copy(d, s),
                                 waits=[(PE, pe_tr[sb])])
        self.barrier()


def emit_f56(self, bi, l):
    self.mark(f'F5_b{bi}l{l}')
    VE, SE, PE, LDG, STS, GE = self.VE, self.SE, self.PE, self.LDG, self.STS, self.GE
    with contextlib.ExitStack() as phA:
        hT = self.sb("f5_hT", [128, NJ * S], BF16, phA)
        w2b = self.sb("f5_w2", [128, NJ * D], BF16, phA)
        b2b = self.sb("f5_b2", [128, D], F32, phA)
        ph = contextlib.ExitStack()
        ph.__enter__()
        w1a = self.sb("f5_wa", [128, 2 * NDB * 128], BF16, ph)
        w1g = self.sb("f5_wg", [128, 2 * NDB * 128], BF16, ph)
        b1a = self.sb("f5_ba", [128, NJ], F32, ph)
        b1g = self.sb("f5_bg", [128, NJ], F32, ph)
        sga = self.sb("f5_sg", [128, 2 * 512], F32, ph)
        saa = self.sb("f5_sa", [128, 2 * 512], F32, ph)

        t_ba = self.dma_g(b1a[:, 0:21], self.FB1[l][0:2688].rearrange("(j p) -> p j", p=128),
                          slow=True)
        t_ba = self.dma_g(b1a[0:42, 21:22], self.FB1[l][2688:2730][:, None], slow=True,
                          waits=[t_ba])
        t_bg = self.dma_g(b1g[:, 0:21], self.FB1[l][2730:5418].rearrange("(j p) -> p j", p=128),
                          slow=True)
        t_bg = self.dma_g(b1g[0:42, 21:22], self.FB1[l][5418:5460][:, None], slow=True,
                          waits=[t_bg])
        t_b2 = self.dma_g(b2b[:, :], self.FB2[l][None, :].partition_broadcast(128))
        t_w2 = self.dma_g(w2b[:, 0:21 * D], self.fw2_c[l][:, 0:21 * D])
        t_w2 = self.dma_g(w2b[0:42, 21 * D:22 * D], self.fw2_c[l][0:42, 21 * D:22 * D],
                          waits=[t_w2])
        w1v = self.fw1_c[l].rearrange("p (db o) -> p db o", db=NDB)
        pe_j, ve_h, t_w1 = {}, {}, {}

        def load_w1(j):
            par = j % 2
            mj = 128 if j < 21 else 42
            wt = [(PE, pe_j[j - 2])] if j - 2 in pe_j else []
            dsta = w1a[:, par * 1024:par * 1024 + 1024].rearrange("p (db o) -> p db o", db=NDB)
            dstg = w1g[:, par * 1024:par * 1024 + 1024].rearrange("p (db o) -> p db o", db=NDB)
            ta = self.dma_g(dsta[:, :, 0:mj], w1v[:, :, 128 * j:128 * j + mj], waits=wt)
            return self.dma_g(dstg[:, :, 0:mj], w1v[:, :, 2730 + 128 * j:2730 + 128 * j + mj],
                              waits=[ta])

        t_w1[0] = load_w1(0)
        for j in range(NJ):
            par = j % 2
            mj = 128 if j < 21 else 42
            if j + 1 < NJ:
                t_w1[j + 1] = load_w1(j + 1)
            for ch in range(4):
                ba = ch % 2
                psa = self.ps_big[0:128, ba * 512:(ba + 1) * 512]
                psg = self.ps_big[0:128, (2 + ba) * 512:(3 + ba) * 512]
                w = [t_w1[j], t_ba, t_bg]
                key = j * 4 + ch - 2
                if key in ve_h:
                    w.append((VE, ve_h[key]))
                for db in range(NDB):
                    self.mm(psa[0:mj, :], w1a[:, par * 1024 + db * 128: par * 1024 + db * 128 + mj],
                            self.tTv(db, ch * 512, 512), start=(db == 0), stop=(db == NDB - 1),
                            waits=(w if db == 0 else ()))
                pem = None
                for db in range(NDB):
                    pem = self.mm(psg[0:mj, :],
                                  w1g[:, par * 1024 + db * 128: par * 1024 + db * 128 + mj],
                                  self.tTv(db, ch * 512, 512), start=(db == 0),
                                  stop=(db == NDB - 1), inc=(PE if db == NDB - 1 else None))
                pe_j[j] = pem
                sgv = sga[0:mj, ba * 512:ba * 512 + 512]
                sav = saa[0:mj, ba * 512:ba * 512 + 512]
                sg_ = self.sop(lambda e, o=sgv, i=psg[0:mj, :], j=j, mj=mj: e.activation(
                    o, i, AF.Gelu, bias=b1g[0:mj, j:j + 1]), waits=[(PE, pem)])
                self.vop(lambda e, o=sav, i=psa[0:mj, :], j=j, mj=mj: e.tensor_scalar(
                    o, i, b1a[0:mj, j:j + 1], None, ALU.add), waits=[(PE, pem)])
                ve_h[j * 4 + ch] = self.vop(
                    lambda e, o=hT[0:mj, j * S + ch * 512: j * S + ch * 512 + 512],
                    a=sav, g=sgv: e.tensor_tensor(o, a, g, ALU.mult), waits=[(SE, sg_)])
        self.barrier(producers=("vector",), consumers=("tensor",))
        self.barrier()
        ph.__exit__(None, None, None)

        self.mark(f'F6_b{bi}l{l}')
        ut2 = self.sb("f6_u", [128, 2 * D], F32, phA)
        xsc = self.sb("f6_x", [128, 2 * D], F32, phA)
        xout = self.sb("f6_o", [128, 2 * D], F32, phA)
        ve_x, st_x = {}, {}
        for sb in range(NSB):
            par = sb % 2
            w = [(VE, ve_x[sb - 2])] if sb - 2 in ve_x else []
            tld = self.dma_g(ut2[:, par * D:(par + 1) * D], self.u_s[:, sb * D:(sb + 1) * D],
                             waits=w)
            psy = self.ps_big[:, par * 1024:(par + 1) * 1024]
            w = [t_w2] + ([(VE, ve_x[sb - 2])] if sb - 2 in ve_x else [])
            pem = None
            for j in range(NJ):
                mj = 128 if j < 21 else 42
                for nn in range(2):
                    pem = self.mm(psy[:, nn * 512:(nn + 1) * 512],
                                  hT[0:mj, j * S + sb * 128: j * S + sb * 128 + 128],
                                  w2b[0:mj, j * D + nn * 512: j * D + (nn + 1) * 512],
                                  start=(j == 0), stop=(j == NJ - 1),
                                  inc=(PE if j == NJ - 1 and nn == 1 else None),
                                  waits=(w if j == 0 and nn == 0 else ()))
            w = [(PE, pem), t_b2, tld]
            if sb - 2 in st_x:
                w.append(st_x[sb - 2])
            self.vop(lambda e, par=par, psy=psy: e.tensor_tensor(
                xsc[:, par * D:(par + 1) * D], psy, b2b[:, :], ALU.add),
                waits=w)
            ve_x[sb] = self.vop(lambda e, par=par: e.tensor_tensor(
                xout[:, par * D:(par + 1) * D], xsc[:, par * D:(par + 1) * D],
                ut2[:, par * D:(par + 1) * D], ALU.add))
            st_x[sb] = self.dma_s(self.f0_s[:, sb * D:(sb + 1) * D],
                                  xout[:, par * D:(par + 1) * D], waits=[(VE, ve_x[sb])])
        self.barrier()


def emit_final(self, bi):
    self.mark(f'FIN_b{bi}')
    VE, SE, PE, LDG, STS = self.VE, self.SE, self.PE, self.LDG, self.STS
    with contextlib.ExitStack() as ph:
        z1t = self.sb("fin_z1", [128, 4 * D], F32, ph)
        self.lnjunk = self.sb("fin_junk", [128, 1024], F32, ph)
        xft = self.sb("fin_xf", [128, 4 * D], F32, ph)
        ot = self.sb("fin_o", [128, 4 * D], F32, ph)
        osc = self.sb("fin_s", [128, 4 * D], F32, ph)
        st_o, ve_o, se_a = {}, {}, {}
        for sb in range(NSB):
            p4 = sb % 4
            w = [(VE, ve_o[sb - 4])] if sb - 4 in ve_o else []
            tld = self.dma_g(z1t[:, p4 * D:(p4 + 1) * D],
                             self.z1_s[:, sb * D:(sb + 1) * D], waits=w)
            w = [(SE, se_a[sb - 4])] if sb - 4 in se_a else []
            tlx = self.dma_g(xft[:, p4 * D:(p4 + 1) * D],
                             self.f0_s[:, sb * D:(sb + 1) * D], waits=w)
            c = p4 * 16
            xfv = xft[:, p4 * D:(p4 + 1) * D]
            emit_ln_stats(self, c, xfv, pe_wait=tlx)
            w = [(VE, self.VE.v)] + ([st_o[sb - 4]] if sb - 4 in st_o else [])
            se_a[sb] = self.sop(lambda e, o=osc[:, p4 * D:(p4 + 1) * D], i=xfv, c=c:
                                e.activation(o, i, AF.Identity, scale=self.stat[:, c + 4:c + 5],
                                             bias=self.stat[:, c + 5:c + 6]), waits=w)
            ve_o[sb] = self.vop(lambda e, p4=p4: e.tensor_tensor(
                ot[:, p4 * D:(p4 + 1) * D], osc[:, p4 * D:(p4 + 1) * D],
                z1t[:, p4 * D:(p4 + 1) * D], ALU.add),
                waits=[(SE, se_a[sb]), tld])
            dst = self.Y[bi].rearrange("(sb p) d -> p sb d", p=128)[:, sb, :]
            st_o[sb] = self.dma_s(dst, ot[:, p4 * D:(p4 + 1) * D],
                                  waits=[(VE, ve_o[sb])])
        self.barrier()


"""Megakernel builder for mk_sandwich_d1 (16x2048x1024 sandwich block).

Layouts:
 - normal: [128, NSB*1024], col = sb*1024 + d (token = sb*128 + p)
 - ^T:     [128, NDB*2049], col = db*2049 + 1 + s (d = db*128 + p); col db*2049
           is a zero column so a col-offset of -1 implements token-shift.
Outer attention: S^T (k on partitions), no max subtraction, exp PSUM->bf16 P^T,
softmax denominators via ones column (idx 64) in V_ext, normalization after PV
via reciprocal + split-bf16 K=1 PE broadcast. All matmuls bf16 / fp32 PSUM.
"""
import contextlib
import numpy as np
import ml_dtypes
import concourse.bass as bass
import concourse.mybir as mybir
from concourse.bass_utils import run_bass_kernel_spmd

F32 = mybir.dt.float32
BF16 = mybir.dt.bfloat16
AF = mybir.ActivationFunctionType
ALU = mybir.AluOpType
AX = mybir.AxisListType

S, D, H, HD = 2048, 1024, 16, 64
NSB, NDB, HP = 16, 8, 8
TCOL = NDB * 2049
FH, FHD, FIN, WIN = 8, 64, 512, 512
HID, HID2 = 2730, 5460
NJ = 22
SCALE = 1.0 / 8.0
QK_SCALE = 8.0
ENGINES = ("tensor", "vector", "scalar", "gpsimd", "sync")


def host_constants():
    bf = ml_dtypes.bfloat16
    c = {}
    c["IDN"] = np.eye(128, dtype=np.float32).astype(bf)
    c["ONES64"] = np.ones((128, 1), np.float32).astype(bf)
    c["ONES1x64"] = np.ones((1, 64), np.float32).astype(bf)
    sel0 = np.zeros((1, 128), np.float32); sel0[0, :64] = 1.0
    sel1 = np.zeros((1, 128), np.float32); sel1[0, 64:] = 1.0
    c["SEL0"], c["SEL1"] = sel0.astype(bf), sel1.astype(bf)
    r = np.arange(128)
    c["M_UT"] = (r[:, None] <= r[None, :]).astype(np.float32).astype(bf)
    c["M_LT"] = (r[:, None] >= r[None, :]).astype(np.float32).astype(bf)
    c["BD"] = ((r[:, None] < 64) == (r[None, :] < 64)).astype(np.float32).astype(bf)
    c["SHIFTT"] = (r[None, :] == r[:, None] + 1).astype(np.float32)
    e127 = np.zeros((128, 128), np.float32); e127[127, 0] = 1.0
    c["E127T"] = e127
    n = 2048
    half = np.arange(0, FHD, 2, dtype=np.float64)
    inv_freq = 1.0 / 10000.0 ** (half / FHD)
    t = np.arange(n, dtype=np.float64)
    freqs = np.concatenate([t[:, None] * inv_freq[None, :]] * 2, axis=-1)
    base = (half + 0.4 * FHD) / (1.4 * FHD)
    power = (t - n // 2) / float(WIN // 2)
    scale = np.concatenate([base[None, :] ** power[:, None]] * 2, axis=-1)
    cosv, sinv = np.cos(freqs), np.sin(freqs)
    sin_signed = sinv.copy()
    sin_signed[:, :32] *= -1.0
    c["RQC"] = np.tile((cosv * scale).astype(np.float32), (1, FH)).astype(bf)
    c["RQS"] = np.tile((sin_signed * scale).astype(np.float32), (1, FH)).astype(bf)
    c["RKC"] = np.tile((cosv / scale).astype(np.float32), (1, FH)).astype(bf)
    c["RKS"] = np.tile((sin_signed / scale).astype(np.float32), (1, FH)).astype(bf)
    return c


class Sem:
    def __init__(self, h):
        self.h, self.v = h, 0

    def inc(self, n=1):
        self.v += n
        return self.v


class MK:
    def __init__(self, cfg=None):
        self.cfg = cfg or {}
        self.nc = bass.Bass("TRN2", target_bir_lowering=False)
        self.ops = {e: [] for e in ENGINES}
        self.tap_t = {}

    def emit(self, eng, fn, waits=(), inc=None, inc_n=1):
        tgt = inc.inc(inc_n) if inc is not None else None
        wl = [(s.h, v) for s, v in waits if v > 0]

        def run(e, fn=fn, wl=wl, inc=inc, inc_n=inc_n):
            for h, v in wl:
                e.wait_ge(h, v)
            inst = fn(e)
            if inc is not None:
                inst.then_inc(inc.h, inc_n)
        self.ops[eng].append(run)
        return tgt

    def wait(self, eng, *sems_vals):
        wl = [(s.h, v) for s, v in sems_vals if v > 0]
        if wl:
            self.ops[eng].append(lambda e, wl=wl: [e.wait_ge(h, v) for h, v in wl])

    def _dma(self, eng, ring, idx_attr, dst, src, waits, slow=False):
        i = getattr(self, idx_attr)
        setattr(self, idx_attr, i + 1)
        s = ring[i % len(ring)]
        pre = ([(s, s.v)] if s.v > 0 else []) + [t for t in waits if t[1] > 0]
        tgt = self.emit(eng, lambda e, d=dst, sr=src, sl=slow: e.dma_start(
            out=d, in_=sr, allow_slow_non_contiguous=sl), waits=pre, inc=s, inc_n=16)
        return (s, tgt)

    def dma_g(self, dst, src, waits=(), slow=False):
        return self._dma("gpsimd", self.LDGr, "_ldg_i", dst, src, waits, slow)

    def dma_s(self, dst, src, waits=()):
        return self._dma("sync", self.STSr, "_sts_i", dst, src, waits)

    def mark(self, name):
        if not hasattr(self, "_marks"):
            self._marks = []
            self._marks_e = []
        self._marks.append((name, getattr(self, "_mmn", 0)))
        self._marks_e.append((name, {e: len(self.ops[e]) for e in ENGINES}))

    def barrier(self, producers=ENGINES, consumers=ENGINES):
        tg = []
        for e in producers:
            if e == "sync":
                tg += [(s, s.v) for s in self.STSr]
            elif e == "gpsimd":
                tg.append((self.GE, self.GE.v))
                tg += [(s, s.v) for s in self.LDGr]
            else:
                tg.append((self.SEMS[e], self.SEMS[e].v))
        for e in consumers:
            self.wait(e, *tg)

    def _count_mm(self):
        self._mmn = getattr(self, "_mmn", 0) + 1

    def tr(self, out, in_, inc=None, waits=()):
        self._count_mm()
        return self.emit(
            "tensor",
            lambda e, o=out, i=in_: e.transpose(o, i, self.idn[:, :]),
            waits=waits, inc=inc)

    def mm(self, out, lhsT, rhs, start, stop, inc=None, tile_position=None, waits=()):
        self._count_mm()
        return self.emit(
            "tensor",
            lambda e, o=out, l=lhsT, r=rhs, st=start, sp=stop, tp=tile_position:
                e.matmul(o, l, r, start=st, stop=sp, tile_position=tp, skip_group_check=True),
            waits=waits, inc=inc)

    def vop(self, fn, waits=(), inc=None):
        sem = inc if inc is not None else self.VE
        w = ([(self.VE, self.VE.v)] if self.VE.v > 0 else []) + list(waits)
        return self.emit("vector", fn, waits=w, inc=sem)

    def sop(self, fn, waits=(), inc=None):
        sem = inc if inc is not None else self.SE
        w = ([(self.SE, self.SE.v)] if self.SE.v > 0 else []) + list(waits)
        return self.emit("scalar", fn, waits=w, inc=sem)

    def gop(self, fn, waits=(), inc=None):
        sem = inc if inc is not None else self.GE
        w = ([(self.GE, self.GE.v)] if self.GE.v > 0 else []) + list(waits)
        return self.emit("gpsimd", fn, waits=w, inc=sem)

    def tap(self, name, sbuf_ap, waits=()):
        if name in self.tap_t:
            self.dma_s(self.tap_t[name], sbuf_ap, waits=waits)

    def build(self):
        nc = self.nc
        cfg = self.cfg
        self.batches = cfg.get("batches", [0, 1])

        def din(name, shape, dt=F32):
            return nc.dram_tensor(name, list(shape), dt, kind="ExternalInput").ap()

        self.X = din("x", [2, S, D])
        self.Wq, self.Wk, self.Wv, self.Wo = (din("wq", [D, D]), din("wk", [D, D]),
                                              din("wv", [D, D]), din("wo", [D, D]))
        self.Wqp, self.Wkp = din("wqp", [D, D]), din("wkp", [D, D])
        self.LRAW = din("lraw", [1, H])
        self.FWQKV = din("fwqkv", [2, D, 3 * FIN])
        self.FQS, self.FKS = din("fqs", [2, FHD]), din("fks", [2, FHD])
        self.FWO = din("fwo", [2, FIN, D])
        self.FLG, self.FLB = din("flg", [2, D]), din("flb", [2, D])
        self.FW1, self.FB1 = din("fw1", [2, D, HID2]), din("fb1", [2, HID2])
        self.FW2, self.FB2 = din("fw2", [2, HID, D]), din("fb2", [2, D])
        for cname, cshape in (("IDN", [128, 128]), ("ONES64", [128, 1]), ("ONES1x64", [1, 64]),
                              ("SEL0", [1, 128]), ("SEL1", [1, 128]),
                              ("M_UT", [128, 128]), ("M_LT", [128, 128]),
                              ("BD", [128, 128]),
                              ("RQC", [S, 512]), ("RQS", [S, 512]),
                              ("RKC", [S, 512]), ("RKS", [S, 512])):
            setattr(self, "c" + cname, din(cname, cshape, BF16))
        self.cSHIFTT = din("SHIFTT", [128, 128], F32)
        self.cE127T = din("E127T", [128, 128], F32)
        self.Y = nc.dram_tensor("y", [2, S, D], F32, kind="ExternalOutput").ap()
        for tname, tshape, tdt in cfg.get("taps", []):
            self.tap_t[tname] = nc.dram_tensor("tap_" + tname, list(tshape), tdt,
                                               kind="ExternalOutput").ap()

        def scr(name, shape, dt):
            kind = "ExternalOutput" if name in cfg.get("dbg_out", ()) else "Internal"
            return nc.dram_tensor(name, list(shape), dt, kind=kind).ap()

        self.wq_c = scr("wq_c", [128, NDB * D], BF16)
        self.wk_c = scr("wk_c", [128, NDB * D], BF16)
        self.wqp_c = scr("wqp_c", [128, NDB * D], BF16)
        self.wkp_c = scr("wkp_c", [128, NDB * D], BF16)
        self.wv_c = scr("wv_c", [128, NDB * D], BF16)
        self.wo_c = scr("wo_c", [128, NDB * D], BF16)
        self.fwqkv_c = scr("fwqkv_c", [2, 128, NDB * 1536], BF16)
        self.fwo_c = scr("fwo_c", [2, 128, 4 * D], BF16)
        self.fw1_c = scr("fw1_c", [2, 128, NDB * HID2], BF16)
        self.fw2_c = scr("fw2_c", [2, 128, NJ * D], BF16)
        self.lam_d = scr("lam_d", [1, H], F32)
        self.qd_s = scr("qd_s", [128, HP * S], BF16)
        self.qc_s = scr("qc_s", [128, HP * S], BF16)
        self.qs_s = scr("qs_s", [128, HP * S], BF16)
        self.kd_s = scr("kd_s", [128, HP * S], BF16)
        self.kc_s = scr("kc_s", [128, HP * S], BF16)
        self.ks_s = scr("ks_s", [128, HP * S], BF16)
        self.vx_s = scr("vx_s", [128, NSB * H * 65], BF16)
        self.z1_s = scr("z1_s", [128, NSB * D], F32)
        self.f0_s = scr("f0_s", [128, NSB * D], F32)
        self.rbc_d = scr("rbc_d", [1, 4096], F32)
        self.u_s = scr("u_s", [128, NSB * D], F32)

        es = contextlib.ExitStack()
        self.es = es
        with es:
            self._nm = [0]
            def _sb(name, shape, dt=F32, stack=es):
                self._nm[0] += 1
                return stack.enter_context(
                    nc.sbuf_tensor(f"{name}_{self._nm[0]}", list(shape), dt))
            self.sb = _sb
            def _ps(name, shape, dt=F32, stack=es):
                self._nm[0] += 1
                return stack.enter_context(
                    nc.psum_tensor(f"{name}_{self._nm[0]}", list(shape), dt))
            self.ps = _ps
            self.ps_big = es.enter_context(nc.psum_tensor("ps_big", [128, 2048], F32))
            self.LDGr = [Sem(es.enter_context(nc.semaphore(f"ldg{i}"))) for i in range(8)]
            self.STSr = [Sem(es.enter_context(nc.semaphore(f"sts{i}"))) for i in range(8)]
            self._ldg_i = 0
            self._sts_i = 0
            self.LDG = self.LDGr[0]
            self.STS = self.STSr[0]
            self.PE = Sem(es.enter_context(nc.semaphore("pe")))
            self.VE = Sem(es.enter_context(nc.semaphore("ve")))
            self.SE = Sem(es.enter_context(nc.semaphore("se")))
            self.GE = Sem(es.enter_context(nc.semaphore("ge")))
            self.SEMS = dict(tensor=self.PE, vector=self.VE, scalar=self.SE,
                             gpsimd=self.GE)

            self.alloc_persistent()
            self.prologue()
            for bi in self.batches:
                self.emit_attn(bi)
                if not cfg.get("attn_only"):
                    for l in (0, 1):
                        self.emit_ffn(bi, l)
                    self.emit_final(bi)
            self.barrier(producers=ENGINES, consumers=("sync",))
            self.wait("sync", *[(s, s.v) for s in self.STSr])

            block = es.enter_context(nc.Block())
            for eng in ENGINES:
                def _replay(e, eng=eng):
                    ops = self.ops[eng]
                    bounds = [(nm, d[eng]) for nm, d in getattr(self, "_marks_e", [])]
                    idx = 0
                    for k, (nm, start) in enumerate(bounds):
                        while idx < start:
                            ops[idx](e)
                            idx += 1
                        end = bounds[k + 1][1] if k + 1 < len(bounds) else len(ops)
                        with nc.named_scope(nm):
                            while idx < end:
                                ops[idx](e)
                                idx += 1
                    while idx < len(ops):
                        ops[idx](e)
                        idx += 1
                getattr(block, eng)(_replay)
        return nc

    def alloc_persistent(self):
        sb = self.sb
        self.idn = sb("idn", [128, 128], BF16)
        self.ones64 = sb("ones64", [128, 1], BF16)
        self.ones1x64 = sb("ones1x64", [1, 64], BF16)
        self.sel0 = sb("sel0", [1, 128], BF16)
        self.sel1 = sb("sel1", [1, 128], BF16)
        self.m_ut = sb("m_ut", [128, 128], BF16)
        self.m_lt = sb("m_lt", [128, 128], BF16)
        self.bd = sb("bd", [128, 128], BF16)
        self.shiftT = sb("shiftT", [128, 128], F32)
        self.e127T = sb("e127T", [128, 128], F32)
        self.lamcol = sb("lamcol", [128, 8])
        self.lamq = sb("lamq", [1, H])
        self.cst = sb("cst", [128, 4])
        self.lnjunk = None
        self.stat = sb("stat", [128, 96])
        self.row = None
        self.rowb = None
        self.tT = None

    def tTv(self, db, s0, n):
        return self.tT[:, db * 2049 + 1 + s0: db * 2049 + 1 + s0 + n]

    def tTs(self, db, s0, n):
        return self.tT[:, db * 2049 + s0: db * 2049 + s0 + n]

    def prologue(self):
        sb, nc = self.sb, self.nc
        import math as _m
        self.vop(lambda e: e.memset(self.cst[:, 0:1], 1e-5))
        self.vop(lambda e: e.memset(self.cst[:, 1:2], 1.0))
        self.vop(lambda e: e.memset(self.cst[:, 2:3], _m.pi / 2))
        self.vop(lambda e: e.memset(self.cst[:, 3:4], 1e-6))
        for dst, src in ((self.idn, self.cIDN), (self.ones64, self.cONES64),
                         (self.ones1x64, self.cONES1x64), (self.sel0, self.cSEL0),
                         (self.sel1, self.cSEL1), (self.m_ut, self.cM_UT),
                         (self.m_lt, self.cM_LT), (self.bd, self.cBD),
                         (self.shiftT, self.cSHIFTT), (self.e127T, self.cE127T)):
            self.dma_g(dst[:, :], src)

        with contextlib.ExitStack() as ph:
            lr = self.sb("lr", [1, H], F32, ph)
            lr2 = self.sb("lr2", [1, H], F32, ph)
            t_ld = self.dma_g(lr[:, :], self.LRAW)
            t_tan = self.sop(lambda e: e.activation(lr2[:, :], lr[:, :], AF.Tanh, scale=1.0),
                             waits=[t_ld])
            t_lv = self.vop(lambda e: e.tensor_scalar(self.lamq[:, :], lr2[:, :], 0.5,
                                                      None, ALU.mult),
                            waits=[(self.SE, t_tan)])
            t_ls = self.dma_s(self.lam_d, self.lamq[:, :], waits=[(self.VE, t_lv)])
            lam2d = self.lam_d.rearrange("one (ob two) -> one ob two", two=2)
            t_l1 = self.dma_g(self.lamcol[0:64, :], lam2d[0:1, :, 0].partition_broadcast(64),
                              waits=[t_ls], slow=True)
            self.dma_g(self.lamcol[64:128, :], lam2d[0:1, :, 1].partition_broadcast(64),
                       waits=[t_l1], slow=True)

            wbufs = [self.sb("wbufA", [128, 22528], BF16, ph),
                     self.sb("wbufB", [128, 22528], BF16, ph)]
            cast_i = [0]
            cast_hist = {}

            def cast_w3(w, c, nk, dout):
                i = cast_i[0]
                cast_i[0] += 1
                wbuf = wbufs[i % 2]
                src = w.rearrange("(db p) o -> p db o", p=128)
                wt = [cast_hist[i - 2]] if i - 2 in cast_hist else []
                t1 = self.dma_g(wbuf[:, 0:nk * dout].rearrange("p (db o) -> p db o", db=nk),
                                src, waits=wt)
                t2 = self.dma_s(c, wbuf[:, 0:nk * dout], waits=[t1])
                cast_hist[i] = t2
                return t2

            for w, c in ((self.Wq, self.wq_c), (self.Wqp, self.wqp_c),
                         (self.Wk, self.wk_c), (self.Wkp, self.wkp_c),
                         (self.Wv, self.wv_c), (self.Wo, self.wo_c)):
                cast_w3(w, c[:, :], NDB, D)
            for l in (0, 1):
                cast_w3(self.FWQKV[l], self.fwqkv_c[l][:, :], NDB, 1536)
                cast_w3(self.FWO[l], self.fwo_c[l][:, :], 4, D)
                for hh in range(2):
                    cast_w3(self.FW1[l][512 * hh:512 * hh + 512],
                            self.fw1_c[l][:, 4 * hh * HID2:(4 * hh + 4) * HID2], 4, HID2)
                cast_w3(self.FW2[l][0:2688], self.fw2_c[l][:, 0:21 * D], 21, D)
                i = cast_i[0]
                cast_i[0] += 1
                wbuf = wbufs[i % 2]
                wt = [cast_hist[i - 2]] if i - 2 in cast_hist else []
                t1 = self.dma_g(wbuf[0:42, 0:D], self.FW2[l][2688:2730, :], waits=wt)
                cast_hist[i] = self.dma_s(self.fw2_c[l][0:42, 21 * D:22 * D],
                                          wbuf[0:42, 0:D], waits=[t1])
            self.barrier(producers=("sync", "gpsimd", "vector", "scalar"), consumers=ENGINES)



def shard_inputs(inputs):
    c = host_constants()
    xs = np.ascontiguousarray(inputs["L0_attn_x"])
    maps = []
    for core in range(8):
        m = dict(
            x=xs[2 * core:2 * core + 2],
            wq=inputs["L0_attn_Wq"], wk=inputs["L0_attn_Wk"], wv=inputs["L0_attn_Wv"],
            wo=inputs["L0_attn_Wo"], wqp=inputs["L0_attn_Wqp"], wkp=inputs["L0_attn_Wkp"],
            lraw=inputs["L0_attn_lambda_raw"].reshape(1, H),
            fwqkv=inputs["L0_ffn_wqkv"], fqs=inputs["L0_ffn_q_scale"],
            fks=inputs["L0_ffn_k_scale"], fwo=inputs["L0_ffn_wo"],
            flg=inputs["L0_ffn_ln_g"], flb=inputs["L0_ffn_ln_b"],
            fw1=inputs["L0_ffn_w1"], fb1=inputs["L0_ffn_b1"],
            fw2=inputs["L0_ffn_w2"], fb2=inputs["L0_ffn_b2"],
        )
        m.update(c)
        maps.append(m)
    return maps


def kernel(**inputs):
    mk = MK()
    nc = mk.build()
    maps = shard_inputs(inputs)
    res = run_bass_kernel_spmd(nc, maps, core_ids=list(range(8)))
    return np.concatenate([r["y"] for r in res.results], axis=0)


MK.emit_attn = emit_attn
MK.emit_ffn = emit_ffn
MK.emit_final = emit_final


# version 12
# speedup vs baseline: 1.0470x; 1.0470x over previous
import contextlib
import math
import concourse.mybir as mybir

F32 = mybir.dt.float32
BF16 = mybir.dt.bfloat16
AF = mybir.ActivationFunctionType
ALU = mybir.AluOpType
AX = mybir.AxisListType

S, D, H, HD = 2048, 1024, 16, 64
NSB, NDB, HP = 16, 8, 8
PI = math.pi


def x_view(self, bi, sb):
    return self.X[bi].rearrange("(sb p) d -> p sb d", p=128)[:, sb, :]


def emit_ln_stats(self, c, src_ap, pe_wait=None):
    st = self.stat
    VE, SE = self.VE, self.SE
    junk = self.lnjunk
    w = [pe_wait] if pe_wait else []
    self.sop(lambda e, j=junk: e.activation(j[:, :], src_ap, AF.Identity,
                                            accum_out=st[:, c + 6:c + 7]), waits=w)
    s2 = self.sop(lambda e, j=junk: e.activation(j[:, :], src_ap, AF.Square,
                                                 accum_out=st[:, c + 7:c + 8]))
    v1 = self.vop(lambda e: e.tensor_scalar(st[:, c:c + 1], st[:, c + 6:c + 7],
                                            1.0 / 1024, None, ALU.mult), waits=[(SE, s2)])
    v2 = self.vop(lambda e: e.tensor_tensor(st[:, c + 2:c + 3], st[:, c:c + 1],
                                            st[:, c:c + 1], ALU.mult), waits=[(VE, v1)])
    v3 = self.vop(lambda e: e.scalar_tensor_tensor(st[:, c + 1:c + 2], st[:, c + 7:c + 8],
                                                   1.0 / 1024, st[:, c + 2:c + 3],
                                                   ALU.mult, ALU.subtract), waits=[(VE, v2)])
    s3 = self.sop(lambda e: e.activation(st[:, c + 3:c + 4], st[:, c + 1:c + 2], AF.Ln,
                                         bias=self.cst[:, 0:1]), waits=[(VE, v3)])
    s4 = self.sop(lambda e: e.activation(st[:, c + 4:c + 5], st[:, c + 3:c + 4],
                                         AF.Exp, scale=-0.5), waits=[(SE, s3)])
    tv2 = self.vop(lambda e: e.scalar_tensor_tensor(
        st[:, c + 5:c + 6], st[:, c:c + 1], -1.0, st[:, c + 4:c + 5], ALU.mult, ALU.mult),
        waits=[(SE, s4)])
    return tv2


def emit_attn(self, bi):
    VE, SE, PE, LDG, STS = self.VE, self.SE, self.PE, self.LDG, self.STS
    with contextlib.ExitStack() as tstack:
        self.tT = self.sb("tT_a", [128, 8 * 2049], BF16, tstack)
        self.ps_tr = self.ps("tr", [128, 1024], BF16, tstack)
        self.ps_tr2 = self.ps("tr2", [128, 1024], BF16, tstack)
        self.ps_a = self.ps("pa", [128, 512], F32, tstack)
        self.ps_b = self.ps("pb", [128, 512], F32, tstack)
        tT_now = self.tT
        mz_t = self.vop(lambda e, t=tT_now: e.memset(t[:, :], 0.0))
        emit_attn_inner(self, bi, mz_t)
    self.tT = None
    if self.cfg.get("upto") in ("a0", "a1q", "a1"):
        return
    emit_attn2(self, bi)


def emit_attn_inner(self, bi, mz_t):
    VE, SE, PE, LDG, STS = self.VE, self.SE, self.PE, self.LDG, self.STS

    self.mark(f'A0_b{bi}')
    with contextlib.ExitStack() as ph:
        x0t = self.sb("a0_x0", [128, 4 * D], F32, ph)
        xnb = self.sb("a0_xn", [128, 4 * D], BF16, ph)
        self.lnjunk = self.sb("a0_junk", [128, 1024], F32, ph)
        se_app, ve_ev, pe_tr = {}, {}, {}
        for sb in range(NSB):
            par = sb % 2
            p4 = sb % 4
            x0 = x0t[:, p4 * D:(p4 + 1) * D]
            xn = xnb[:, p4 * D:(p4 + 1) * D]
            w = [(SE, se_app[sb - 4])] if sb - 4 in se_app else []
            tld = self.dma_g(x0, x_view(self, bi, sb), waits=w)
            c = p4 * 16
            emit_ln_stats(self, c, x0, pe_wait=tld)
            w = [(VE, self.VE.v)] + ([(PE, pe_tr[sb - 4])] if sb - 4 in pe_tr else [])
            se_app[sb] = self.sop(lambda e, xn=xn, x0=x0, c=c: e.activation(
                xn, x0, AF.Identity, scale=self.stat[:, c + 4:c + 5],
                bias=self.stat[:, c + 5:c + 6]), waits=w)
            pst = self.ps_tr if par == 0 else self.ps_tr2
            w = [(SE, se_app[sb])] + ([(VE, ve_ev[sb - 2])] if sb - 2 in ve_ev else [])
            for db in range(NDB):
                pe_tr[sb] = self.tr(pst[:, db * 128:(db + 1) * 128],
                                    xn[:, db * 128:(db + 1) * 128], inc=PE,
                                    waits=(w if db == 0 else ()))
            dst = self.tT[:, :].rearrange("p (db c) -> p db c", db=NDB)[:, :, 1 + sb * 128: 129 + sb * 128]
            src = pst[:, :].rearrange("p (k c) -> p k c", k=NDB)
            ve_ev[sb] = self.vop(lambda e, d=dst, s=src: e.tensor_copy(d, s),
                                 waits=[(PE, pe_tr[sb])] + ([(VE, mz_t)] if sb == 0 else []))
        self.barrier()
        self.tap("xnT", self.tT[:, :])
    if self.cfg.get("upto") == "a0":
        return

    self.mark(f'A1_b{bi}')
    with contextlib.ExitStack() as ph:
        wp = self.sb("a1_wp", [128, NDB * D], BF16, ph)
        wd = self.sb("a1_wd", [128, NDB * D], BF16, ph)
        cosP = self.sb("a1_cos", [128, NDB * S], BF16, ph)
        sinP = self.sb("a1_sin", [128, NDB * S], BF16, ph)
        sc1 = self.sb("a1_sc1", [128, 3 * 512], F32, ph)
        sc2 = self.sb("a1_sc2", [128, 2 * 512], F32, ph)
        asqb = self.sb("a1_asq", [128, 2 * 512], BF16, ph)
        rln = self.sb("a1_rln", [128, 512], F32, ph)
        rinvf = self.sb("a1_rinv", [128, 2 * 512], F32, ph)
        ampv = self.sb("a1_ampv", [128, 512], F32, ph)
        dstg = self.sb("a1_dst", [128, 2 * S], BF16, ph)
        cstg = self.sb("a1_cst", [128, 2 * S], BF16, ph)
        sstg = self.sb("a1_sst", [128, 2 * S], BF16, ph)

        pe_last_p = [0]
        pe_last_a = [0]
        for side in ("q", "k"):
            w_p = self.Wqp if side == "q" else self.Wkp
            w_d = self.Wq if side == "q" else self.Wk
            d_s = self.qd_s if side == "q" else self.kd_s
            c_s = self.qc_s if side == "q" else self.kc_s
            s_s = self.qs_s if side == "q" else self.ks_s
            t_wp = self.dma_w3(wp[:, :], w_p, NDB, waits=[(PE, pe_last_p[0])])
            t_wd = self.dma_w3(wd[:, :], w_d, NDB, waits=[(PE, pe_last_a[0])])


            se_p = {}
            for it in range(int(self.cfg.get('a1n', 32))):
                ob, ch = divmod(it, 4)
                par = it % 2
                pss = self.ps_a if par == 0 else self.ps_b
                w = [t_wp] + ([(SE, se_p[it - 2])] if it - 2 in se_p else [])
                for db in range(NDB):
                    pem = self.mm(pss[:, :],
                                  wp[:, db * D + ob * 128: db * D + ob * 128 + 128],
                                  self.tTv(db, ch * 512, 512),
                                  start=(db == 0), stop=(db == NDB - 1),
                                  inc=(PE if db == NDB - 1 else None),
                                  waits=(w if db == 0 else ()))
                s1v = sc1[:, par * 512:(par + 1) * 512]
                s2v = sc2[:, par * 512:(par + 1) * 512]
                cpv = cosP[:, ob * S + ch * 512: ob * S + ch * 512 + 512]
                spv = sinP[:, ob * S + ch * 512: ob * S + ch * 512 + 512]
                self.sop(lambda e, o=s1v, i=pss: e.activation(o, i[:, :], AF.Tanh),
                         waits=[(PE, pem)])
                self.sop(lambda e, o=s2v, i=s1v: e.activation(o, i, AF.Abs))
                self.sop(lambda e, o=cpv, i=s2v: e.activation(o, i, AF.Sin,
                                                              scale=-PI, bias=self.cst[:, 2:3]))
                se_p[it] = self.sop(lambda e, o=spv, i=s1v: e.activation(o, i, AF.Sin,
                                                                         scale=PI))
            pe_last_p[0] = PE.v
            self.barrier(producers=("scalar",), consumers=("tensor",))
            if self.cfg.get("upto") == "a1p":
                self.barrier()
                return

            se_ex, se_sq, se_ri, ve_d, ve_f, rms_pe, st_s = {}, {}, {}, {}, {}, {}, {}

            def projA(it):
                ob, ch = divmod(it, 4)
                par = it % 2
                sbp = ob % 2
                pss = self.ps_a if par == 0 else self.ps_b
                w = [t_wd]
                if it - 2 in se_ex:
                    w += [(SE, se_ex[it - 2]), (VE, ve_d[it - 2])]
                pem = None
                for db in range(NDB):
                    pem = self.mm(pss[:, :],
                                  wd[:, db * D + ob * 128: db * D + ob * 128 + 128],
                                  self.tTv(db, ch * 512, 512),
                                  start=(db == 0), stop=(db == NDB - 1),
                                  inc=(PE if db == NDB - 1 else None),
                                  waits=(w if db == 0 else ()))
                s1v = sc1[:, (it % 3) * 512:(it % 3 + 1) * 512]
                s2v = sc2[:, par * 512:(par + 1) * 512]
                aqv = asqb[:, par * 512:(par + 1) * 512]
                ddst = dstg[:, sbp * S + ch * 512: sbp * S + ch * 512 + 512]
                w = [(PE, pem)]
                if ch == 0 and (ob - 2) in st_s:
                    w.append(st_s[ob - 2][0])
                ve_d[it] = self.vop(lambda e, o=ddst, i=pss: e.tensor_copy(o, i[:, :]), waits=w)
                se_ex[it] = self.sop(lambda e, o=s2v, i=pss: e.activation(o, i[:, :], AF.Exp),
                                     waits=[(PE, pem)])
                w = [(VE, ve_f[it - 3])] if it - 3 in ve_f else []
                self.sop(lambda e, o=s1v, i=s2v: e.activation(o, i, AF.Ln,
                                                              bias=self.cst[:, 1:2]), waits=w)
                w = [(PE, rms_pe[it - 2])] if it - 2 in rms_pe else []
                se_sq[it] = self.sop(lambda e, o=aqv, i=s1v: e.activation(o, i, AF.Square),
                                     waits=w)

            def postA(it):
                ob, ch = divmod(it, 4)
                par = it % 2
                sbp = ob % 2
                s1v = sc1[:, (it % 3) * 512:(it % 3 + 1) * 512]
                aqv = asqb[:, par * 512:(par + 1) * 512]
                riv = rinvf[:, par * 512:(par + 1) * 512]
                cdst = cstg[:, sbp * S + ch * 512: sbp * S + ch * 512 + 512]
                sdst = sstg[:, sbp * S + ch * 512: sbp * S + ch * 512 + 512]
                cpv = cosP[:, ob * S + ch * 512: ob * S + ch * 512 + 512]
                spv = sinP[:, ob * S + ch * 512: ob * S + ch * 512 + 512]
                prs = self.ps_big[:, (2 + par) * 512:(3 + par) * 512]
                w = [(SE, se_sq[it])]
                if it - 2 in se_ri:
                    w.append((SE, se_ri[it - 2]))
                rms_pe[it] = self.mm(prs, self.bd[:, :], aqv, start=True, stop=True,
                                     inc=PE, waits=w)
                self.sop(lambda e, i=prs: e.activation(rln[:, :], i[:, :] if False else i, AF.Ln,
                                                       scale=1.0 / 64, bias=self.cst[:, 3:4]),
                         waits=[(PE, rms_pe[it])])
                wri = [(VE, ve_f[it - 2])] if it - 2 in ve_f else []
                se_ri[it] = self.sop(lambda e, o=riv: e.activation(o, rln[:, :], AF.Exp,
                                                                   scale=-0.5), waits=wri)
                if side == "q":
                    self.vop(lambda e, o=ampv[:, :], a=s1v, r=riv, ob=ob: e.scalar_tensor_tensor(
                        o, a, self.lamcol[:, ob:ob + 1], r, ALU.mult, ALU.mult),
                        waits=[(SE, se_ri[it])])
                else:
                    self.vop(lambda e, o=ampv[:, :], a=s1v, r=riv: e.tensor_tensor(
                        o, a, r, ALU.mult), waits=[(SE, se_ri[it])])
                wc = [st_s[ob - 2][1]] if (ch == 0 and (ob - 2) in st_s) else []
                self.vop(lambda e, o=cdst, c=cpv, am=ampv[:, :]: e.tensor_tensor(o, am, c, ALU.mult),
                         waits=wc)
                ws = [st_s[ob - 2][2]] if (ch == 0 and (ob - 2) in st_s) else []
                ve_f[it] = self.vop(lambda e, o=sdst, s=spv, am=ampv[:, :]: e.tensor_tensor(
                    o, am, s, ALU.mult), waits=ws)
                if ch == 3:
                    w = [(VE, ve_f[it]), (VE, ve_d[it])]
                    t_d = self.dma_s(d_s[:, ob * S:(ob + 1) * S],
                                     dstg[:, sbp * S:(sbp + 1) * S], waits=w)
                    t_c = self.dma_s(c_s[:, ob * S:(ob + 1) * S], cstg[:, sbp * S:(sbp + 1) * S])
                    t_s2 = self.dma_s(s_s[:, ob * S:(ob + 1) * S], sstg[:, sbp * S:(sbp + 1) * S])
                    st_s[ob] = (t_d, t_c, t_s2)

            nA = int(self.cfg.get("a1an", 32))
            for it in range(nA + 1):
                if it < nA:
                    projA(it)
                if it >= 1:
                    postA(it - 1)
            pe_last_a[0] = PE.v
            self.barrier()
            if self.cfg.get("upto") == "a1q":
                return

        self.mark(f'A1V_b{bi}')
        t_wv = self.dma_w3(wp[:, :], self.Wv, NDB)
        vst = self.sb("a1_vst", [128, 2 * 1040], BF16, ph)
        mv = self.vop(lambda e: e.memset(vst[:, :], 1.0))
        ve_v, st_v = {}, {}
        for sb in range(NSB):
            par = sb % 2
            psy = self.ps_big[:, par * 1024:(par + 1) * 1024]
            w = [t_wv, (VE, mv)] + ([(VE, ve_v[sb - 2])] if sb - 2 in ve_v else [])
            for db in range(NDB):
                for nn in range(2):
                    pem = self.mm(psy[:, nn * 512:(nn + 1) * 512],
                                  self.tTv(db, sb * 128, 128),
                                  wp[:, db * D + nn * 512: db * D + (nn + 1) * 512],
                                  start=(db == 0), stop=(db == NDB - 1),
                                  inc=(PE if db == NDB - 1 and nn == 1 else None),
                                  waits=(w if db == 0 and nn == 0 else ()))
            dst = vst[:, par * 1040:(par + 1) * 1040].rearrange("p (h c) -> p h c", h=H)[:, :, 0:64]
            src = psy.rearrange("p (h c) -> p h c", h=H)
            w = [(PE, pem)] + ([st_v[sb - 2]] if sb - 2 in st_v else [])
            ve_v[sb] = self.vop(lambda e, d=dst, s=src: e.tensor_copy(d, s), waits=w)
            st_v[sb] = self.dma_s(self.vx_s[:, sb * 1040:(sb + 1) * 1040],
                                  vst[:, par * 1040:(par + 1) * 1040],
                                  waits=[(VE, ve_v[sb])])
        self.barrier()


def emit_attn2(self, bi):
    self.mark(f'A2_b{bi}')
    VE, SE, PE, LDG, STS, GE = self.VE, self.SE, self.PE, self.LDG, self.STS, self.GE
    with contextlib.ExitStack() as ph:
        ps_s = [self.ps_big[:, 0:1024], self.ps_big[:, 1024:2048]]
        ps_pv = self.ps("a2_pv", [128, 1024], F32, ph)
        ps_bc = self.ps("a2_bc", [128, 1024], F32, ph)
        kdcb = self.sb("a2_kdc", [128, 4 * S], BF16, ph)
        qdcb = self.sb("a2_qdc", [128, 4 * S], BF16, ph)
        ksb = self.sb("a2_ks", [128, 2 * S], BF16, ph)
        qszb = self.sb("a2_qsz", [128, 4 * S], BF16, ph)
        vxb = self.sb("a2_vx", [128, 2 * NSB * 196], BF16, ph)
        ptb = self.sb("a2_pt", [128, 2 * 1024], BF16, ph)
        otb = self.sb("a2_ot", [128, HP * S], BF16, ph)
        opv = self.sb("a2_op", [64, 2 * 512], BF16, ph)
        onrm = self.sb("a2_on", [64, 2 * 512], BF16, ph)
        rowA = self.sb("a2_rA", [1, 2 * 512], F32, ph)
        rowB = self.sb("a2_rB", [1, 2 * 512], BF16, ph)
        csu = self.sb("a2_csu", [128, 128], BF16, ph)
        cs_v = self.vop(lambda e: e.tensor_scalar(csu[:, :], self.m_lt[:, :],
                                                  30000.0, -30000.0, ALU.mult, ALU.add))
        mz0 = self.vop(lambda e: e.memset(ps_s[0][:, :], 0.0))
        mz1 = self.vop(lambda e: e.memset(ps_s[1][:, :], 0.0))
        mzq = self.vop(lambda e: e.memset(qszb[:, :], 0.0))
        mzv = self.vop(lambda e: e.memset(vxb[:, :], 1.0))
        pair_pe_done = {}
        pv_guard = {}
        bc_guard = {}
        row_guard = {}
        on_guard = {}
        ld_hp = {}

        def load_pair(hp):
            par = hp % 2
            w = [(PE, pair_pe_done[hp - 2])] if hp - 2 in pair_pe_done else []
            toks = []
            for hl in range(2):
                for half, st in ((0, self.kd_s), (1, self.kc_s)):
                    toks.append(self.dma_g(
                        kdcb[64 * half:64 * half + 64,
                             par * 2 * S + hl * S:par * 2 * S + (hl + 1) * S],
                        st[hl * 64:hl * 64 + 64, hp * S:(hp + 1) * S], waits=w))
                    w = []
                for half, st in ((0, self.qd_s), (1, self.qc_s)):
                    toks.append(self.dma_g(
                        qdcb[64 * half:64 * half + 64,
                             par * 2 * S + hl * S:par * 2 * S + (hl + 1) * S],
                        st[hl * 64:hl * 64 + 64, hp * S:(hp + 1) * S]))
            toks.append(self.dma_g(ksb[:, par * S:(par + 1) * S],
                                   self.ks_s[:, hp * S:(hp + 1) * S]))
            wq = [(VE, mzq)] if hp < 2 else []
            for hl in range(2):
                toks.append(self.dma_g(
                    qszb[hl * 64:hl * 64 + 64,
                         par * 2 * S + hl * S:par * 2 * S + hl * S + S],
                    self.qs_s[hl * 64:hl * 64 + 64, hp * S:(hp + 1) * S], waits=wq))
                wq = []
            vsrc = self.vx_s.rearrange("p (sb c) -> p sb c", sb=NSB)[:, :, 130 * hp:130 * (hp + 1)]
            vdst = vxb[:, par * NSB * 196:(par + 1) * NSB * 196] \
                .rearrange("p (sb c) -> p sb c", sb=NSB)[:, :, 0:130]
            toks.append(self.dma_g(vdst, vsrc, waits=([(VE, mzv)] if hp < 2 else [])))
            return toks

        ld_hp[0] = load_pair(0)
        for hp in range(HP):
            par = hp % 2
            if hp + 1 < HP:
                ld_hp[hp + 1] = load_pair(hp + 1)
            for parity in range(2):
                h = 2 * hp + parity
                base = 64 * parity
                tp = (base, 0)
                se_ex = {}
                first_w = list(ld_hp[hp]) + [(VE, cs_v), (VE, mz0), (VE, mz1)]
                tiles = []
                for c in range(4):
                    for kb in range(4 * c + 4):
                        tiles.append((kb, c))
                pairs = [tiles[i:i + 2] for i in range(0, len(tiles), 2)]

                def emit_S_pair(pi, pr):
                    sl = pi % 2
                    pss = ps_s[sl]
                    nonlocal first_w
                    w = list(first_w)
                    first_w = []
                    if pi - 2 in se_ex:
                        w.append((SE, se_ex[pi - 2]))
                    pem = None
                    for half, (kb, c) in enumerate(pr):
                        col0 = max(kb * 128, 512 * c)
                        n = 512 * (c + 1) - col0
                        off = half * 512
                        b2 = par * 2 * S + parity * S
                        self.mm(pss[:, off:off + n],
                                kdcb[:, b2 + kb * 128: b2 + kb * 128 + 128],
                                qdcb[:, b2 + col0: b2 + col0 + n],
                                start=True, stop=False, waits=w)
                        w = []
                        diag = (col0 == kb * 128)
                        qz = par * 2 * S + parity * S
                        pem = self.mm(pss[:, off:off + n],
                                      ksb[:, par * S + kb * 128: par * S + kb * 128 + 128],
                                      qszb[:, qz + col0: qz + col0 + n],
                                      start=False, stop=not diag,
                                      inc=(None if diag else PE))
                        if diag:
                            pem = self.mm(pss[:, off:off + 128], csu[:, :], self.idn[:, :],
                                          start=False, stop=True, inc=PE)
                    kb1, c1 = pr[-1]
                    ntot = 512 + 512 * (c1 + 1) - max(kb1 * 128, 512 * c1) if len(pr) == 2 \
                        else 512
                    ex = self.sop(lambda e, o=ptb[:, sl * 1024: sl * 1024 + ntot],
                                  i=pss, ntot=ntot:
                                  e.activation(o, i[:, 0:ntot], AF.Exp, scale=0.125),
                                  waits=[(PE, pem)])
                    se_ex[pi] = ex
                    return (sl, pr, ex)

                def emit_PV_pair(p):
                    sl, pr, ex = p
                    for half, (kb, c) in enumerate(pr):
                        col0 = max(kb * 128, 512 * c)
                        n = 512 * (c + 1) - col0
                        s = c % 2
                        lv = vxb[:, par * NSB * 196:(par + 1) * NSB * 196] \
                            .rearrange("p (sb c) -> p sb c", sb=NSB)[:, kb, parity * 65:parity * 65 + 128]
                        w = [(SE, ex)]
                        if kb == 0:
                            w += pv_guard.pop(s, [])
                        pv = self.mm(ps_pv[:, s * 512 + col0 - 512 * c:
                                           s * 512 + col0 - 512 * c + n], lv,
                                     ptb[:, sl * 1024 + half * 512: sl * 1024 + half * 512 + n],
                                     start=(kb == 0), stop=(kb == 4 * c + 3),
                                     inc=PE, waits=w)
                        if kb == 4 * c + 3:
                            emit_chunk_norm(c, s, pv)

                def emit_chunk_norm(c, s, pv):
                    ov = opv[:, s * 512:(s + 1) * 512]
                    cp = self.vop(lambda e, ov=ov, s=s: e.tensor_copy(
                        ov, ps_pv[0:64, s * 512:(s + 1) * 512]), waits=[(PE, pv)])
                    wl = [(PE, pv)] + ([row_guard[s]] if s in row_guard else [])
                    s1 = self.sop(lambda e, s=s: e.activation(
                        rowA[0:1, s * 512:(s + 1) * 512],
                        ps_pv[64:65, s * 512:(s + 1) * 512], AF.Ln), waits=wl)
                    s2 = self.sop(lambda e, s=s: e.activation(
                        rowB[0:1, s * 512:(s + 1) * 512],
                        rowA[0:1, s * 512:(s + 1) * 512], AF.Exp, scale=-1.0))
                    pv_guard[s] = [(VE, cp), (SE, s2)]
                    wb = [(SE, s2)] + ([bc_guard[s]] if s in bc_guard else [])
                    bc = self.mm(ps_bc[0:64, s * 512:(s + 1) * 512], self.ones1x64[:, :],
                                 rowB[0:1, s * 512:(s + 1) * 512], start=True, stop=True,
                                 inc=PE, waits=wb)
                    row_guard[s] = (PE, bc)
                    blk = otb[:, hp * S + c * 512: hp * S + (c + 1) * 512]
                    if parity == 0:
                        tt = self.vop(lambda e, blk=blk, ov=ov, s=s: e.tensor_tensor(
                            blk[0:64, :], ov, ps_bc[0:64, s * 512:(s + 1) * 512], ALU.mult),
                            waits=[(PE, bc)])
                        bc_guard[s] = (VE, tt)
                    else:
                        osl = c % 2
                        onv = onrm[:, osl * 512:(osl + 1) * 512]
                        w2 = [(PE, bc)] + ([on_guard[osl]] if osl in on_guard else [])
                        tt = self.vop(lambda e, onv=onv, ov=ov, s=s: e.tensor_tensor(
                            onv, ov, ps_bc[0:64, s * 512:(s + 1) * 512], ALU.mult), waits=w2)
                        bc_guard[s] = (VE, tt)
                        sv = self.sop(lambda e, blk=blk, onv=onv: e.activation(
                            blk[64:128, :], onv, AF.Copy), waits=[(VE, tt)])
                        on_guard[osl] = (SE, sv)

                pend = None
                for pi, pr in enumerate(pairs):
                    cur = emit_S_pair(pi, pr)
                    if pend is not None:
                        emit_PV_pair(pend)
                    pend = cur
                emit_PV_pair(pend)
                pair_pe_done[hp] = PE.v
        self.barrier()
        self.tap("otb", otb[:, :])

        self.mark(f'A2Wo_b{bi}')
        wo = self.sb("a2_wo", [128, NDB * D], BF16, ph)
        self.lnjunk = self.sb("a2_junk", [128, 1024], F32, ph)
        x0t = self.sb("a2_x0", [128, 4 * D], F32, ph)
        z1t = self.sb("a2_z1", [128, 4 * D], F32, ph)
        zsc = self.sb("a2_zs", [128, 4 * D], F32, ph)
        t_wo = self.dma_w3(wo[:, :], self.Wo, NDB)
        ve_ap, ve_z, st_z = {}, {}, {}
        for sb in range(NSB):
            par = sb % 2
            p4 = sb % 4
            w = [(VE, ve_z[sb - 4])] if sb - 4 in ve_z else []
            ldx = self.dma_g(x0t[:, p4 * D:(p4 + 1) * D], x_view(self, bi, sb), waits=w)
            psy = self.ps_big[:, par * 1024:(par + 1) * 1024]
            w = [t_wo] + ([(VE, ve_ap[sb - 2])] if sb - 2 in ve_ap else [])
            for db in range(NDB):
                for nn in range(2):
                    pem = self.mm(psy[:, nn * 512:(nn + 1) * 512],
                                  otb[:, db * S + sb * 128: db * S + sb * 128 + 128],
                                  wo[:, db * D + nn * 512: db * D + (nn + 1) * 512],
                                  start=(db == 0), stop=(db == NDB - 1),
                                  inc=(PE if db == NDB - 1 and nn == 1 else None),
                                  waits=(w if db == 0 and nn == 0 else ()))
            c = p4 * 16
            emit_ln_stats(self, c, psy, pe_wait=(PE, pem))
            zv = z1t[:, p4 * D:(p4 + 1) * D]
            zs = zsc[:, p4 * D:(p4 + 1) * D]
            w = [(VE, self.VE.v), ldx] + ([st_z[sb - 4]] if sb - 4 in st_z else [])
            ve_ap[sb] = self.vop(lambda e, zs=zs, c=c, psy=psy: e.tensor_scalar(
                zs, psy, self.stat[:, c + 4:c + 5],
                self.stat[:, c + 5:c + 6], ALU.mult, ALU.add), waits=w)
            ve_z[sb] = self.vop(lambda e, zv=zv, zs=zs, x0=x0t[:, p4 * D:(p4 + 1) * D]:
                                e.tensor_tensor(zv, zs, x0, ALU.add))
            st_z[sb] = self.dma_s(self.z1_s[:, sb * D:(sb + 1) * D], zv,
                                  waits=[(VE, ve_z[sb])])
        self.barrier()


"""FFN (2-layer local-attention transformer) phases + final combine. Mixin for MK."""
import contextlib
import concourse.mybir as mybir

F32 = mybir.dt.float32
BF16 = mybir.dt.bfloat16
AF = mybir.ActivationFunctionType
ALU = mybir.AluOpType
AX = mybir.AxisListType

S, D = 2048, 1024
NSB, NDB = 16, 8
FH, FHD, FIN, WIN = 8, 64, 512, 512
HID, HID2, NJ = 2730, 5460, 22


def emit_ffn(self, bi, l):
    with contextlib.ExitStack() as tstack:
        self.tT = self.sb("tT_f", [128, 8 * 2049], BF16, tstack)
        self.ps_a = self.ps("pa", [128, 512], F32, tstack)
        self.ps_b = self.ps("pb", [128, 512], F32, tstack)
        tT_now = self.tT
        mz_t = self.vop(lambda e, t=tT_now: e.memset(t[:, :], 0.0))
        emit_f0(self, bi, l, mz_t)
        with contextlib.ExitStack() as s1:
            vxf = self.sb("f1_vx", [128, NSB * 584], BF16, s1)
            qrTz = self.sb("f2_qTz", [128, 8 * S], BF16, s1)
            krT = self.sb("f2_kT", [128, 4 * S], BF16, s1)
            with contextlib.ExitStack() as s2:
                qrot = self.sb("f1_qr", [128, NSB * 512], BF16, s2)
                krot = self.sb("f1_kr", [128, NSB * 512], BF16, s2)
                emit_f1(self, bi, l, qrot, krot, vxf)
                emit_f2(self, qrot, krot, qrTz, krT)
            with contextlib.ExitStack() as s3:
                oT = self.sb("f3_oT", [128, 4 * S], BF16, s3)
                emit_f3(self, bi, l, qrTz, krT, vxf, oT)
                emit_f4(self, bi, l, oT)
        emit_f56(self, bi, l)
    self.tT = None


def emit_f0(self, bi, l, mz_t):
    self.mark(f'F0_b{bi}l{l}')
    """x_f stream tile-wise: l==0: LN(z1) -> f0_s spill + bf16 -> transpose;
    l==1: f0_s -> bf16 -> transpose."""
    VE, SE, PE, LDG, STS = self.VE, self.SE, self.PE, self.LDG, self.STS
    with contextlib.ExitStack() as ph:
        self.ps_tr = self.ps("tr", [128, 1024], BF16, ph)
        self.ps_tr2 = self.ps("tr2", [128, 1024], BF16, ph)
        xnb = self.sb("f0_xn", [128, 4 * D], BF16, ph)
        self.lnjunk = self.sb("f0_junk", [128, 1024], F32, ph)
        z1t = self.sb("f0_z1", [128, 4 * D], F32, ph)
        xft = self.sb("f0_xf", [128, 4 * D], F32, ph)
        se_app, ve_ev, pe_tr, st_f = {}, {}, {}, {}
        for sb in range(NSB):
            par = sb % 2
            p4 = sb % 4
            xn = xnb[:, p4 * D:(p4 + 1) * D]
            if l == 0:
                z1 = z1t[:, p4 * D:(p4 + 1) * D]
                xfv = xft[:, p4 * D:(p4 + 1) * D]
                w = [(SE, se_app[sb - 4])] if sb - 4 in se_app else []
                tld = self.dma_g(z1, self.z1_s[:, sb * D:(sb + 1) * D], waits=w)
                c = p4 * 16
                emit_ln_stats(self, c, z1, pe_wait=tld)
                w = [(VE, self.VE.v)] + ([st_f[sb - 4]] if sb - 4 in st_f else [])
                sf = self.sop(lambda e, o=xfv, i=z1, c=c: e.activation(
                    o, i, AF.Identity, scale=self.stat[:, c + 4:c + 5],
                    bias=self.stat[:, c + 5:c + 6]), waits=w)
                se_app[sb] = sf
                st_f[sb] = self.dma_s(self.f0_s[:, sb * D:(sb + 1) * D], xfv,
                                      waits=[(SE, sf)])
                w = [(SE, sf)] + ([(PE, pe_tr[sb - 4])] if sb - 4 in pe_tr else [])
                ve_c = self.vop(lambda e, o=xn, i=xfv: e.tensor_copy(o, i), waits=w)
            else:
                xfv = xft[:, p4 * D:(p4 + 1) * D]
                w = [(SE, se_app[sb - 4])] if sb - 4 in se_app else []
                tld = self.dma_g(xfv, self.f0_s[:, sb * D:(sb + 1) * D], waits=w)
                w = [tld] + ([(PE, pe_tr[sb - 4])] if sb - 4 in pe_tr else [])
                se_app[sb] = self.sop(lambda e, o=xn, i=xfv: e.activation(o, i, AF.Copy),
                                      waits=w)
                ve_c = self.VE.v
            pst = self.ps_tr if par == 0 else self.ps_tr2
            if l == 0:
                w = [(VE, ve_c)] + ([(VE, ve_ev[sb - 2])] if sb - 2 in ve_ev else [])
            else:
                w = [(SE, se_app[sb])] + ([(VE, ve_ev[sb - 2])] if sb - 2 in ve_ev else [])
            for db in range(NDB):
                pe_tr[sb] = self.tr(pst[:, db * 128:(db + 1) * 128],
                                    xn[:, db * 128:(db + 1) * 128], inc=PE,
                                    waits=(w if db == 0 else ()))
            dst = self.tT[:, :].rearrange("p (db c) -> p db c", db=NDB)[:, :, 1 + sb * 128: 129 + sb * 128]
            src = pst[:, :].rearrange("p (k c) -> p k c", k=NDB)
            ve_ev[sb] = self.vop(lambda e, d=dst, s=src: e.tensor_copy(d, s),
                                 waits=[(PE, pe_tr[sb])] + ([(VE, mz_t)] if sb == 0 else []))
        self.barrier()


def emit_f1(self, bi, l, qrot, krot, vxf):
    self.mark(f'F1_b{bi}l{l}')
    VE, SE, PE, LDG, STS, GE = self.VE, self.SE, self.PE, self.LDG, self.STS, self.GE
    with contextlib.ExitStack() as ph:
        wqkv = self.sb("f1_w", [128, NDB * 1536], BF16, ph)
        qsb_t = self.sb("f1_qs", [128, 512], BF16, ph)
        ksb_t = self.sb("f1_ks", [128, 512], BF16, ph)
        rot_t = self.sb("f1_rot", [128, 2 * 2048], BF16, ph)
        sq_t = self.sb("f1_sqt", [128, 2 * 512], F32, ph)
        qn_t = self.sb("f1_qn", [128, 1024], F32, ph)
        qn2_t = self.sb("f1_qn2", [128, 1024], F32, ph)
        r1_t = self.sb("f1_r1", [128, 1024], F32, ph)
        r2_t = self.sb("f1_r2", [128, 1024], F32, ph)
        t_w = self.dma_w3(wqkv[:, :], self.FWQKV[l], NDB)
        t_qs = t_ks = None
        for hh in range(FH):
            t_qs = self.dma_g(qsb_t[:, hh * FHD:(hh + 1) * FHD],
                              self.FQS[l][None, :].partition_broadcast(128),
                              waits=([t_qs] if t_qs else []))
            t_ks = self.dma_g(ksb_t[:, hh * FHD:(hh + 1) * FHD],
                              self.FKS[l][None, :].partition_broadcast(128),
                              waits=([t_ks] if t_ks else []))
        mv = self.vop(lambda e: e.memset(vxf[:, :], 1.0))
        ve_done, ve_rot = {}, {}
        for sb in range(NSB):
            par = sb % 2
            t_rots = []
            for ti, tab in enumerate((self.cRQC, self.cRQS, self.cRKC, self.cRKS)):
                wt = [(VE, ve_rot[sb - 2])] if sb - 2 in ve_rot else []
                t_rots.append(self.dma_g(
                    rot_t[:, par * 2048 + ti * 512: par * 2048 + (ti + 1) * 512],
                    tab[sb * 128:(sb + 1) * 128, :], waits=wt))
            if par == 0:
                psums = [self.ps_big[:, 0:512], self.ps_big[:, 512:1024],
                         self.ps_big[:, 1024:1536]]
            else:
                psums = [self.ps_big[:, 1536:2048], self.ps_a[:, :], self.ps_b[:, :]]
            w = [t_w, t_qs, t_ks, (VE, mv)]
            if sb - 2 in ve_done:
                w.append((VE, ve_done[sb - 2]))
            pem = None
            for db in range(NDB):
                lhs = self.tTv(db, sb * 128, 128) if db < 4 else self.tTs(db, sb * 128, 128)
                for nn in range(3):
                    pem = self.mm(psums[nn], lhs,
                                  wqkv[:, db * 1536 + nn * 512: db * 1536 + (nn + 1) * 512],
                                  start=(db == 0), stop=(db == NDB - 1),
                                  inc=(PE if db == NDB - 1 and nn == 2 else None),
                                  waits=(w if db == 0 and nn == 0 else ()))
            last_rot = None
            for side in range(2):
                psv = psums[side]
                sqv = sq_t[:, par * 512:(par + 1) * 512]
                wsq = [(PE, pem)] + ([(VE, ve_done[sb - 2])] if sb - 2 in ve_done else [])
                sv = self.sop(lambda e, o=sqv, i=psv: e.activation(o, i, AF.Square),
                              waits=wsq)
                c = 32 + par * 32 + side * 16
                rv = self.vop(lambda e, c=c, sqv=sqv: e.tensor_reduce(
                    self.stat[:, c:c + 8], sqv.rearrange("p (h d) -> p h d", h=FH),
                    AX.X, ALU.add), waits=[(SE, sv)])
                rv = self.vop(lambda e, c=c: e.tensor_scalar(
                    self.stat[:, c + 8:c + 16], self.stat[:, c:c + 8], 1e-24, None, ALU.max),
                    waits=[(VE, rv)])
                s_l = self.sop(lambda e, c=c: e.activation(
                    self.stat[:, c:c + 8], self.stat[:, c + 8:c + 16], AF.Ln),
                    waits=[(VE, rv)])
                sv = self.sop(lambda e, c=c: e.activation(
                    self.stat[:, c + 8:c + 16], self.stat[:, c:c + 8], AF.Exp, scale=-0.5),
                    waits=[(SE, s_l)])
                qv = qn_t[:, side * 512:(side + 1) * 512]
                q2v = qn2_t[:, side * 512:(side + 1) * 512]
                sct = qsb_t if side == 0 else ksb_t
                self.vop(lambda e, qv=qv, i=psv, s=sct: e.tensor_tensor(qv, i, s[:, :], ALU.mult))
                for hh in range(FH):
                    self.vop(lambda e, hh=hh, c=c, qv=qv, q2v=q2v: e.tensor_scalar(
                        q2v[:, hh * 64:(hh + 1) * 64], qv[:, hh * 64:(hh + 1) * 64],
                        self.stat[:, c + 8 + hh:c + 9 + hh], None, ALU.mult),
                        waits=([(SE, sv)] if hh == 0 else ()))
                ct = rot_t[:, par * 2048 + (2 * side) * 512: par * 2048 + (2 * side) * 512 + 512]
                st_ = rot_t[:, par * 2048 + (2 * side + 1) * 512: par * 2048 + (2 * side + 1) * 512 + 512]
                t1 = r1_t[:, side * 512:(side + 1) * 512]
                t2 = r2_t[:, side * 512:(side + 1) * 512]
                self.vop(lambda e, t1=t1, q2v=q2v, ct=ct: e.tensor_tensor(t1, q2v, ct, ALU.mult),
                         waits=t_rots)
                q23 = q2v.rearrange("p (h two d) -> p h two d", h=FH, two=2)
                st3 = st_.rearrange("p (h two d) -> p h two d", h=FH, two=2)
                t23 = t2.rearrange("p (h two d) -> p h two d", h=FH, two=2)
                self.vop(lambda e, t23=t23, q23=q23, st3=st3: e.tensor_tensor(
                    t23[:, :, 0, :], q23[:, :, 1, :], st3[:, :, 0, :], ALU.mult))
                self.vop(lambda e, t23=t23, q23=q23, st3=st3: e.tensor_tensor(
                    t23[:, :, 1, :], q23[:, :, 0, :], st3[:, :, 1, :], ALU.mult))
                dstq = (qrot if side == 0 else krot)[:, sb * 512:(sb + 1) * 512]
                last_rot = self.vop(lambda e, d=dstq, t1=t1, t2=t2: e.tensor_tensor(
                    d, t1, t2, ALU.add))
            ve_rot[sb] = last_rot
            dst = vxf[:, sb * 584: sb * 584 + 520].rearrange("p (h c) -> p h c", h=FH)[:, :, 0:64]
            src = psums[2].rearrange("p (h c) -> p h c", h=FH)
            ve_done[sb] = self.vop(lambda e, d=dst, s=src: e.tensor_copy(d, s))
        self.barrier()


def emit_f2(self, qrot, krot, qrTz, krT):
    self.mark('F2')
    VE, SE, PE = self.VE, self.SE, self.PE
    with contextlib.ExitStack() as ph:
        self.ps_tr = self.ps("tr", [128, 1024], BF16, ph)
        self.ps_tr2 = self.ps("tr2", [128, 1024], BF16, ph)
        mzq = self.vop(lambda e: e.memset(qrTz[:, :], 0.0))
        ve_ev2, pe_tr2 = {}, {}
        qz = qrTz[:, :].rearrange("p (pb two c) -> p pb two c", pb=4, two=2)
        for sb in range(NSB):
            par = sb % 2
            pst = self.ps_tr if par == 0 else self.ps_tr2
            w = ([(VE, ve_ev2[sb - 2])] if sb - 2 in ve_ev2 else [])
            for pb in range(4):
                self.tr(pst[:, pb * 128:(pb + 1) * 128],
                        qrot[:, sb * 512 + pb * 128: sb * 512 + (pb + 1) * 128], inc=PE,
                        waits=(w if pb == 0 else ()))
            for pb in range(4):
                pe_tr2[sb] = self.tr(pst[:, 512 + pb * 128: 512 + (pb + 1) * 128],
                                     krot[:, sb * 512 + pb * 128: sb * 512 + (pb + 1) * 128],
                                     inc=PE)
            dqe = qz[0:64, :, 0, sb * 128:(sb + 1) * 128]
            dqo = qz[64:128, :, 1, sb * 128:(sb + 1) * 128]
            dk = krT[:, :].rearrange("p (pb c) -> p pb c", pb=4)[:, :, sb * 128:(sb + 1) * 128]
            sq_ = pst[:, 0:512].rearrange("p (k c) -> p k c", k=4)
            sk_ = pst[:, 512:1024].rearrange("p (k c) -> p k c", k=4)
            self.vop(lambda e, d=dqe, s=sq_: e.tensor_copy(d, s[0:64]),
                     waits=[(PE, pe_tr2[sb])] + ([(VE, mzq)] if sb == 0 else []))
            self.vop(lambda e, d=dqo, s=sq_: e.tensor_copy(d, s[64:128]))
            ve_ev2[sb] = self.vop(lambda e, d=dk, s=sk_: e.tensor_copy(d, s))
        self.barrier()


def emit_f3(self, bi, l, qrTz, krT, vxf, oT):
    self.mark(f'F3_b{bi}l{l}')
    VE, SE, PE, LDG, STS, GE = self.VE, self.SE, self.PE, self.LDG, self.STS, self.GE
    with contextlib.ExitStack() as ph:
        ps_bc = self.ps("f3_bc", [128, 1024], F32, ph)
        ptf = self.sb("f3_pt", [128, 8 * 512], BF16, ph)
        opv = self.sb("f3_op", [64, 2 * 512], BF16, ph)
        onrm = self.sb("f3_on", [64, 2 * 512], BF16, ph)
        csl = self.sb("f3_csl", [128, 128], BF16, ph)
        csu = self.sb("f3_csu", [128, 128], BF16, ph)
        rowA = self.sb("f3_rA", [1, 2 * 512], F32, ph)
        rowB = self.sb("f3_rB", [1, 2 * 512], BF16, ph)
        mz = self.vop(lambda e: e.memset(ptf[:, :], 0.0))
        self.vop(lambda e: e.tensor_scalar(csu[:, :], self.m_lt[:, :], 30000.0, -30000.0,
                                           ALU.mult, ALU.add))
        cs_v = self.vop(lambda e: e.tensor_scalar(csl[:, :], self.m_ut[:, :],
                                                  30000.0, -30000.0, ALU.mult, ALU.add))
        norm_fin, row_guard, bc_guard, on_guard, se_ex = {}, {}, {}, {}, {}
        sl_n = [0]
        units = []
        for w_ in range(4):
            for hp in range(4):
                for parity in range(2):
                    units.append((w_, hp, parity))

        def emit_S(w_, hp, parity, j):
            h = 2 * hp + parity
            a0 = max(0, j - 4)
            a1 = min(3, j)
            col0, n = a0 * 128, (a1 - a0 + 1) * 128
            kb_abs = 4 * (w_ - 1) + j
            sn = sl_n[0]
            sl_n[0] += 1
            sl = sn % 2
            pss = self.ps_a if sl == 0 else self.ps_b
            wt = [(VE, cs_v), (VE, mz)] if sn == 0 else []
            if sn - 2 in se_ex:
                wt.append((SE, se_ex[sn - 2]))
            self.mm(pss[:, 0:n],
                    krT[:, hp * S + kb_abs * 128: hp * S + kb_abs * 128 + 128],
                    qrTz[:, h * S + w_ * 512 + col0: h * S + w_ * 512 + col0 + n],
                    start=True, stop=False, waits=wt)
            if j >= 4:
                pem = self.mm(pss[:, 0:128], csu[:, :], self.idn[:, :],
                              start=False, stop=True, inc=PE)
            else:
                off = (a1 - a0) * 128
                pem = self.mm(pss[:, off:off + 128], csl[:, :], self.idn[:, :],
                              start=False, stop=True, inc=PE)
            ex = self.sop(lambda e, o=ptf[:, j * 512 + col0: j * 512 + col0 + n],
                          i=pss, n=n: e.activation(o, i[:, 0:n], AF.Exp, scale=8.0),
                          waits=[(PE, pem)])
            se_ex[sn] = ex
            return ex

        def emit_PV(it, w_, hp, parity, j, first, last, ex):
            h = 2 * hp + parity
            bank = it % 4
            kb_abs = 4 * (w_ - 1) + j
            wv = [(SE, ex)]
            if first and it - 4 in norm_fin:
                wv += norm_fin[it - 4]
            return self.mm(self.ps_big[:, bank * 512:(bank + 1) * 512],
                           vxf[:, kb_abs * 584 + h * 65: kb_abs * 584 + h * 65 + 128],
                           ptf[:, j * 512:(j + 1) * 512],
                           start=first, stop=last, inc=PE, waits=wv)

        def emit_norm(it, w_, hp, parity, pv):
            bank = it % 4
            rslot = it % 2
            ov = opv[:, rslot * 512:(rslot + 1) * 512]
            cp = self.vop(lambda e, ov=ov, bank=bank: e.tensor_copy(
                ov, self.ps_big[0:64, bank * 512:(bank + 1) * 512]), waits=[(PE, pv)])
            w = [(PE, pv)] + ([row_guard[rslot]] if rslot in row_guard else [])
            s1 = self.sop(lambda e, bank=bank, rslot=rslot: e.activation(
                rowA[0:1, rslot * 512:(rslot + 1) * 512],
                self.ps_big[64:65, bank * 512:(bank + 1) * 512], AF.Ln), waits=w)
            s2 = self.sop(lambda e, rslot=rslot: e.activation(
                rowB[0:1, rslot * 512:(rslot + 1) * 512],
                rowA[0:1, rslot * 512:(rslot + 1) * 512], AF.Exp, scale=-1.0))
            norm_fin[it] = [(VE, cp), (SE, s2)]
            wb = [(SE, s2)] + ([bc_guard[rslot]] if rslot in bc_guard else [])
            bc = self.mm(ps_bc[0:64, rslot * 512:(rslot + 1) * 512], self.ones1x64[:, :],
                         rowB[0:1, rslot * 512:(rslot + 1) * 512], start=True, stop=True,
                         inc=PE, waits=wb)
            row_guard[rslot] = (PE, bc)
            if parity == 0:
                tt = self.vop(lambda e, hp=hp, w_=w_, ov=ov, rslot=rslot: e.tensor_tensor(
                    oT[0:64, hp * S + w_ * 512: hp * S + (w_ + 1) * 512],
                    ov, ps_bc[0:64, rslot * 512:(rslot + 1) * 512], ALU.mult),
                    waits=[(PE, bc)])
                bc_guard[rslot] = (VE, tt)
            else:
                osl = (it // 2) % 2
                onv = onrm[:, osl * 512:(osl + 1) * 512]
                w2 = [(PE, bc)] + ([on_guard[osl]] if osl in on_guard else [])
                tt = self.vop(lambda e, onv=onv, ov=ov, rslot=rslot: e.tensor_tensor(
                    onv, ov, ps_bc[0:64, rslot * 512:(rslot + 1) * 512], ALU.mult), waits=w2)
                bc_guard[rslot] = (VE, tt)
                sv = self.sop(lambda e, hp=hp, w_=w_, onv=onv: e.activation(
                    oT[64:128, hp * S + w_ * 512: hp * S + (w_ + 1) * 512],
                    onv, AF.Copy), waits=[(VE, tt)])
                on_guard[osl] = (SE, sv)

        flat = []
        for ui, (w_, hp, parity) in enumerate(units):
            jlist = list(range(4, 8)) if w_ == 0 else list(range(8))
            for jn, j in enumerate(jlist):
                flat.append((ui, w_, hp, parity, j, jn == 0, jn == len(jlist) - 1))
        pend = None
        for rec in flat:
            ex = emit_S(rec[1], rec[2], rec[3], rec[4])
            if pend is not None:
                pv = emit_PV(pend[0], pend[1], pend[2], pend[3], pend[4], pend[5], pend[6],
                             pend[7])
                if pend[6]:
                    emit_norm(pend[0], pend[1], pend[2], pend[3], pv)
            pend = rec + (ex,)
        pv = emit_PV(pend[0], pend[1], pend[2], pend[3], pend[4], pend[5], pend[6], pend[7])
        emit_norm(pend[0], pend[1], pend[2], pend[3], pv)
        self.barrier()


def emit_f4(self, bi, l, oT):
    self.mark(f'F4_b{bi}l{l}')
    VE, SE, PE, LDG, STS, GE = self.VE, self.SE, self.PE, self.LDG, self.STS, self.GE
    with contextlib.ExitStack() as ph:
        self.ps_tr = self.ps("tr", [128, 1024], BF16, ph)
        self.ps_tr2 = self.ps("tr2", [128, 1024], BF16, ph)
        wo = self.sb("f4_wo", [128, 4 * D], BF16, ph)
        self.lnjunk = self.sb("f4_junk", [128, 1024], F32, ph)
        ut = self.sb("f4_u", [128, 4 * D], F32, ph)
        xft = self.sb("f4_xf", [128, 4 * D], F32, ph)
        shv4 = self.sb("f4_sh", [128, 4 * 512], F32, ph)
        h2nb = self.sb("f4_hn", [128, 2 * D], BF16, ph)
        lngb = self.sb("f4_lg", [128, D], BF16, ph)
        lnbb = self.sb("f4_lb", [128, D], BF16, ph)
        row2 = self.sb("f4_r2", [128, 4 * 1024], F32, ph)
        row3 = self.sb("f4_r3", [128, 2 * 1024], F32, ph)
        st4 = self.sb("f4_st", [128, 64], F32, ph)
        t_wo = self.dma_w3(wo[:, :], self.FWO[l], 4)
        t_lg = self.dma_g(lngb[:, :], self.FLG[l][None, :].partition_broadcast(128))
        t_lb = self.dma_g(lnbb[:, :], self.FLB[l][None, :].partition_broadcast(128))
        ve_u, st_u, se_ap, ve_hn, pe_tr, ve_ev, sh_pem, cp_sh = {}, {}, {}, {}, {}, {}, {}, {}
        for sb in range(NSB):
            par = sb % 2
            p4 = sb % 4
            wx = [(VE, ve_u[sb - 4])] if sb - 4 in ve_u else []
            ldx = self.dma_g(xft[:, p4 * D:(p4 + 1) * D],
                             self.f0_s[:, sb * D:(sb + 1) * D], waits=wx)
            psy = self.ps_big[:, par * 1024:(par + 1) * 1024]
            w = [t_wo] + ([(VE, ve_u[sb - 2])] if sb - 2 in ve_u else [])
            pem = None
            for pb in range(4):
                for nn in range(2):
                    pem = self.mm(psy[:, nn * 512:(nn + 1) * 512],
                                  oT[:, pb * S + sb * 128: pb * S + sb * 128 + 128],
                                  wo[:, pb * D + nn * 512: pb * D + (nn + 1) * 512],
                                  start=(pb == 0), stop=(pb == 3),
                                  inc=(PE if pb == 3 and nn == 1 else None),
                                  waits=(w if pb == 0 and nn == 0 else ()))
            uv = ut[:, p4 * D:(p4 + 1) * D]
            w = [(PE, pem), ldx]
            if sb - 4 in st_u:
                w += [st_u[sb - 4], (SE, se_ap[sb - 4])]
            ve_u[sb] = self.vop(lambda e, uv=uv, xf=xft[:, p4 * D:(p4 + 1) * D], psy=psy:
                                e.tensor_tensor(uv, psy, xf, ALU.add), waits=w)
            st_u[sb] = self.dma_s(self.u_s[:, sb * D:(sb + 1) * D], uv, waits=[(VE, ve_u[sb])])
            psv = self.ps_a if par == 0 else self.ps_b
            w = [(VE, ve_u[sb])] + ([(VE, cp_sh[sb - 2])] if sb - 2 in cp_sh else [])
            m1 = self.mm(psv[:, :], self.shiftT[:, :], uv[:, 512:1024],
                         start=True, stop=(sb == 0), inc=(PE if sb == 0 else None),
                         waits=w)
            if sb > 0:
                pu = ut[:, ((sb - 1) % 4) * D:((sb - 1) % 4) * D + D]
                m1 = self.mm(psv[:, :], self.e127T[:, :], pu[:, 512:1024],
                             start=False, stop=True, inc=PE)
            sh_pem[sb] = m1
            shp = shv4[:, p4 * 512:(p4 + 1) * 512]
            cp_sh[sb] = self.vop(lambda e, shp=shp, psv=psv: e.tensor_copy(shp, psv[:, :]),
                                 waits=[(PE, m1)])
            c = p4 * 16
            st = st4
            junk = self.lnjunk
            w = [(VE, cp_sh[sb])]
            self.sop(lambda e, uv=uv, c=c, j=junk: e.activation(
                j[:, 0:512], uv[:, 0:512], AF.Identity,
                accum_out=st[:, c + 6:c + 7]), waits=w)
            self.sop(lambda e, shp=shp, c=c, j=junk: e.activation(
                j[:, 0:512], shp, AF.Identity, accum_out=st[:, c + 8:c + 9]))
            self.sop(lambda e, uv=uv, c=c, j=junk: e.activation(
                j[:, 0:512], uv[:, 0:512], AF.Square,
                accum_out=st[:, c + 7:c + 8]))
            s2 = self.sop(lambda e, shp=shp, c=c, j=junk: e.activation(
                j[:, 0:512], shp, AF.Square, accum_out=st[:, c + 9:c + 10]))
            v1 = self.vop(lambda e, c=c: e.tensor_tensor(
                st[:, c + 10:c + 11], st[:, c + 6:c + 7], st[:, c + 8:c + 9], ALU.add),
                waits=[(SE, s2)])
            v1 = self.vop(lambda e, c=c: e.tensor_tensor(
                st[:, c + 11:c + 12], st[:, c + 7:c + 8], st[:, c + 9:c + 10], ALU.add),
                waits=[(VE, v1)])
            v1 = self.vop(lambda e, c=c: e.tensor_scalar(st[:, c:c + 1], st[:, c + 10:c + 11],
                                                         1.0 / 1024, None, ALU.mult),
                          waits=[(VE, v1)])
            v1 = self.vop(lambda e, c=c: e.tensor_tensor(st[:, c + 2:c + 3], st[:, c:c + 1],
                                                         st[:, c:c + 1], ALU.mult),
                          waits=[(VE, v1)])
            v1 = self.vop(lambda e, c=c: e.scalar_tensor_tensor(
                st[:, c + 1:c + 2], st[:, c + 11:c + 12], 1.0 / 1024, st[:, c + 2:c + 3],
                ALU.mult, ALU.subtract), waits=[(VE, v1)])
            s3 = self.sop(lambda e, c=c: e.activation(st[:, c + 3:c + 4], st[:, c + 1:c + 2],
                                                      AF.Ln, bias=self.cst[:, 0:1]),
                          waits=[(VE, v1)])
            s2 = self.sop(lambda e, c=c: e.activation(st[:, c + 4:c + 5], st[:, c + 3:c + 4],
                                                      AF.Exp, scale=-0.5), waits=[(SE, s3)])
            tv2 = self.vop(lambda e, c=c: e.scalar_tensor_tensor(
                st[:, c + 5:c + 6], st[:, c:c + 1], -1.0, st[:, c + 4:c + 5],
                ALU.mult, ALU.mult), waits=[(SE, s2)])
            hv = h2nb[:, par * D:(par + 1) * D]
            r2v = row2[:, p4 * D:(p4 + 1) * D]
            r3v = row3[:, par * D:(par + 1) * D]
            w = [(VE, tv2), t_lg, t_lb]
            if sb - 4 in ve_hn:
                w.append((VE, ve_hn[sb - 4]))
            self.sop(lambda e, uv=uv, c=c, r2v=r2v: e.activation(
                r2v[:, 0:512], uv[:, 0:512], AF.Identity,
                scale=st[:, c + 4:c + 5], bias=st[:, c + 5:c + 6]), waits=w)
            se_ap[sb] = self.sop(lambda e, shp=shp, c=c, r2v=r2v: e.activation(
                r2v[:, 512:1024], shp, AF.Identity,
                scale=st[:, c + 4:c + 5], bias=st[:, c + 5:c + 6]))
            self.vop(lambda e, r2v=r2v, r3v=r3v: e.tensor_tensor(
                r3v, r2v, lngb[:, :], ALU.mult), waits=[(SE, se_ap[sb])])
            ve_hn[sb] = self.vop(lambda e, hv=hv, r3v=r3v: e.tensor_tensor(
                hv, r3v, lnbb[:, :], ALU.add))
            pst = self.ps_tr if par == 0 else self.ps_tr2
            w = [(VE, ve_hn[sb])] + ([(VE, ve_ev[sb - 2])] if sb - 2 in ve_ev else [])
            for db in range(NDB):
                pe_tr[sb] = self.tr(pst[:, db * 128:(db + 1) * 128],
                                    hv[:, db * 128:(db + 1) * 128], inc=PE,
                                    waits=(w if db == 0 else ()))
            dst = self.tT[:, :].rearrange("p (db c) -> p db c", db=NDB)[:, :, 1 + sb * 128: 129 + sb * 128]
            src = pst[:, :].rearrange("p (k c) -> p k c", k=NDB)
            ve_ev[sb] = self.vop(lambda e, d=dst, s=src: e.tensor_copy(d, s),
                                 waits=[(PE, pe_tr[sb])])
        self.barrier()


def emit_f56(self, bi, l):
    self.mark(f'F5_b{bi}l{l}')
    VE, SE, PE, LDG, STS, GE = self.VE, self.SE, self.PE, self.LDG, self.STS, self.GE
    with contextlib.ExitStack() as phA:
        hT = self.sb("f5_hT", [128, NJ * S], BF16, phA)
        w2b = self.sb("f5_w2", [128, NJ * D], BF16, phA)
        b2b = self.sb("f5_b2", [128, D], F32, phA)
        ph = contextlib.ExitStack()
        ph.__enter__()
        w1a = self.sb("f5_wa", [128, 2 * NDB * 128], BF16, ph)
        w1g = self.sb("f5_wg", [128, 2 * NDB * 128], BF16, ph)
        b1a = self.sb("f5_ba", [128, NJ], F32, ph)
        b1g = self.sb("f5_bg", [128, NJ], F32, ph)
        sga = self.sb("f5_sg", [128, 2 * 512], F32, ph)
        saa = self.sb("f5_sa", [128, 2 * 512], F32, ph)

        t_ba = self.dma_g(b1a[:, 0:21], self.FB1[l][0:2688].rearrange("(j p) -> p j", p=128),
                          slow=True)
        t_ba = self.dma_g(b1a[0:42, 21:22], self.FB1[l][2688:2730][:, None], slow=True,
                          waits=[t_ba])
        t_bg = self.dma_g(b1g[:, 0:21], self.FB1[l][2730:5418].rearrange("(j p) -> p j", p=128),
                          slow=True)
        t_bg = self.dma_g(b1g[0:42, 21:22], self.FB1[l][5418:5460][:, None], slow=True,
                          waits=[t_bg])
        t_b2 = self.dma_g(b2b[:, :], self.FB2[l][None, :].partition_broadcast(128))
        t_w2 = self.dma_w3(w2b[:, 0:21 * D], self.FW2[l][0:2688], 21)
        t_w2 = self.dma_g(w2b[0:42, 21 * D:22 * D], self.FW2[l][2688:2730, :],
                          waits=[t_w2])
        w1v = self.FW1[l].rearrange("(db p) o -> p db o", p=128)
        pe_j, ve_h, t_w1 = {}, {}, {}

        def load_w1(j):
            par = j % 2
            mj = 128 if j < 21 else 42
            wt = [(PE, pe_j[j - 2])] if j - 2 in pe_j else []
            dsta = w1a[:, par * 1024:par * 1024 + 1024].rearrange("p (db o) -> p db o", db=NDB)
            dstg = w1g[:, par * 1024:par * 1024 + 1024].rearrange("p (db o) -> p db o", db=NDB)
            ta = self.dma_g(dsta[:, :, 0:mj], w1v[:, :, 128 * j:128 * j + mj], waits=wt)
            return self.dma_g(dstg[:, :, 0:mj], w1v[:, :, 2730 + 128 * j:2730 + 128 * j + mj],
                              waits=[ta])

        t_w1[0] = load_w1(0)
        for j in range(NJ):
            par = j % 2
            mj = 128 if j < 21 else 42
            if j + 1 < NJ:
                t_w1[j + 1] = load_w1(j + 1)
            for ch in range(4):
                ba = ch % 2
                psa = self.ps_big[0:128, ba * 512:(ba + 1) * 512]
                psg = self.ps_big[0:128, (2 + ba) * 512:(3 + ba) * 512]
                w = [t_w1[j], t_ba, t_bg]
                key = j * 4 + ch - 2
                if key in ve_h:
                    w.append((VE, ve_h[key]))
                for db in range(NDB):
                    self.mm(psa[0:mj, :], w1a[:, par * 1024 + db * 128: par * 1024 + db * 128 + mj],
                            self.tTv(db, ch * 512, 512), start=(db == 0), stop=(db == NDB - 1),
                            waits=(w if db == 0 else ()))
                pem = None
                for db in range(NDB):
                    pem = self.mm(psg[0:mj, :],
                                  w1g[:, par * 1024 + db * 128: par * 1024 + db * 128 + mj],
                                  self.tTv(db, ch * 512, 512), start=(db == 0),
                                  stop=(db == NDB - 1), inc=(PE if db == NDB - 1 else None))
                pe_j[j] = pem
                sgv = sga[0:mj, ba * 512:ba * 512 + 512]
                sav = saa[0:mj, ba * 512:ba * 512 + 512]
                sg_ = self.sop(lambda e, o=sgv, i=psg[0:mj, :], j=j, mj=mj: e.activation(
                    o, i, AF.Gelu, bias=b1g[0:mj, j:j + 1]), waits=[(PE, pem)])
                self.vop(lambda e, o=sav, i=psa[0:mj, :], j=j, mj=mj: e.tensor_scalar(
                    o, i, b1a[0:mj, j:j + 1], None, ALU.add), waits=[(PE, pem)])
                ve_h[j * 4 + ch] = self.vop(
                    lambda e, o=hT[0:mj, j * S + ch * 512: j * S + ch * 512 + 512],
                    a=sav, g=sgv: e.tensor_tensor(o, a, g, ALU.mult), waits=[(SE, sg_)])
        self.barrier(producers=("vector",), consumers=("tensor",))
        self.barrier()
        ph.__exit__(None, None, None)

        self.mark(f'F6_b{bi}l{l}')
        ut2 = self.sb("f6_u", [128, 2 * D], F32, phA)
        xsc = self.sb("f6_x", [128, 2 * D], F32, phA)
        xout = self.sb("f6_o", [128, 2 * D], F32, phA)
        ve_x, st_x = {}, {}
        for sb in range(NSB):
            par = sb % 2
            w = [(VE, ve_x[sb - 2])] if sb - 2 in ve_x else []
            tld = self.dma_g(ut2[:, par * D:(par + 1) * D], self.u_s[:, sb * D:(sb + 1) * D],
                             waits=w)
            psy = self.ps_big[:, par * 1024:(par + 1) * 1024]
            w = [t_w2] + ([(VE, ve_x[sb - 2])] if sb - 2 in ve_x else [])
            pem = None
            for j in range(NJ):
                mj = 128 if j < 21 else 42
                for nn in range(2):
                    pem = self.mm(psy[:, nn * 512:(nn + 1) * 512],
                                  hT[0:mj, j * S + sb * 128: j * S + sb * 128 + 128],
                                  w2b[0:mj, j * D + nn * 512: j * D + (nn + 1) * 512],
                                  start=(j == 0), stop=(j == NJ - 1),
                                  inc=(PE if j == NJ - 1 and nn == 1 else None),
                                  waits=(w if j == 0 and nn == 0 else ()))
            w = [(PE, pem), t_b2, tld]
            if sb - 2 in st_x:
                w.append(st_x[sb - 2])
            self.vop(lambda e, par=par, psy=psy: e.tensor_tensor(
                xsc[:, par * D:(par + 1) * D], psy, b2b[:, :], ALU.add),
                waits=w)
            ve_x[sb] = self.vop(lambda e, par=par: e.tensor_tensor(
                xout[:, par * D:(par + 1) * D], xsc[:, par * D:(par + 1) * D],
                ut2[:, par * D:(par + 1) * D], ALU.add))
            st_x[sb] = self.dma_s(self.f0_s[:, sb * D:(sb + 1) * D],
                                  xout[:, par * D:(par + 1) * D], waits=[(VE, ve_x[sb])])
        self.barrier()


def emit_final(self, bi):
    self.mark(f'FIN_b{bi}')
    VE, SE, PE, LDG, STS = self.VE, self.SE, self.PE, self.LDG, self.STS
    with contextlib.ExitStack() as ph:
        z1t = self.sb("fin_z1", [128, 4 * D], F32, ph)
        self.lnjunk = self.sb("fin_junk", [128, 1024], F32, ph)
        xft = self.sb("fin_xf", [128, 4 * D], F32, ph)
        ot = self.sb("fin_o", [128, 4 * D], F32, ph)
        osc = self.sb("fin_s", [128, 4 * D], F32, ph)
        st_o, ve_o, se_a = {}, {}, {}
        for sb in range(NSB):
            p4 = sb % 4
            w = [(VE, ve_o[sb - 4])] if sb - 4 in ve_o else []
            tld = self.dma_g(z1t[:, p4 * D:(p4 + 1) * D],
                             self.z1_s[:, sb * D:(sb + 1) * D], waits=w)
            w = [(SE, se_a[sb - 4])] if sb - 4 in se_a else []
            tlx = self.dma_g(xft[:, p4 * D:(p4 + 1) * D],
                             self.f0_s[:, sb * D:(sb + 1) * D], waits=w)
            c = p4 * 16
            xfv = xft[:, p4 * D:(p4 + 1) * D]
            emit_ln_stats(self, c, xfv, pe_wait=tlx)
            w = [(VE, self.VE.v)] + ([st_o[sb - 4]] if sb - 4 in st_o else [])
            se_a[sb] = self.sop(lambda e, o=osc[:, p4 * D:(p4 + 1) * D], i=xfv, c=c:
                                e.activation(o, i, AF.Identity, scale=self.stat[:, c + 4:c + 5],
                                             bias=self.stat[:, c + 5:c + 6]), waits=w)
            ve_o[sb] = self.vop(lambda e, p4=p4: e.tensor_tensor(
                ot[:, p4 * D:(p4 + 1) * D], osc[:, p4 * D:(p4 + 1) * D],
                z1t[:, p4 * D:(p4 + 1) * D], ALU.add),
                waits=[(SE, se_a[sb]), tld])
            dst = self.Y[bi].rearrange("(sb p) d -> p sb d", p=128)[:, sb, :]
            st_o[sb] = self.dma_s(dst, ot[:, p4 * D:(p4 + 1) * D],
                                  waits=[(VE, ve_o[sb])])
        self.barrier()


"""Megakernel builder for mk_sandwich_d1 (16x2048x1024 sandwich block).

Layouts:
 - normal: [128, NSB*1024], col = sb*1024 + d (token = sb*128 + p)
 - ^T:     [128, NDB*2049], col = db*2049 + 1 + s (d = db*128 + p); col db*2049
           is a zero column so a col-offset of -1 implements token-shift.
Outer attention: S^T (k on partitions), no max subtraction, exp PSUM->bf16 P^T,
softmax denominators via ones column (idx 64) in V_ext, normalization after PV
via reciprocal + split-bf16 K=1 PE broadcast. All matmuls bf16 / fp32 PSUM.
"""
import contextlib
import numpy as np
import ml_dtypes
import concourse.bass as bass
import concourse.mybir as mybir
from concourse.bass_utils import run_bass_kernel_spmd

F32 = mybir.dt.float32
BF16 = mybir.dt.bfloat16
AF = mybir.ActivationFunctionType
ALU = mybir.AluOpType
AX = mybir.AxisListType

S, D, H, HD = 2048, 1024, 16, 64
NSB, NDB, HP = 16, 8, 8
TCOL = NDB * 2049
FH, FHD, FIN, WIN = 8, 64, 512, 512
HID, HID2 = 2730, 5460
NJ = 22
SCALE = 1.0 / 8.0
QK_SCALE = 8.0
ENGINES = ("tensor", "vector", "scalar", "gpsimd", "sync")


def host_constants():
    bf = ml_dtypes.bfloat16
    c = {}
    c["IDN"] = np.eye(128, dtype=np.float32).astype(bf)
    c["ONES64"] = np.ones((128, 1), np.float32).astype(bf)
    c["ONES1x64"] = np.ones((1, 64), np.float32).astype(bf)
    sel0 = np.zeros((1, 128), np.float32); sel0[0, :64] = 1.0
    sel1 = np.zeros((1, 128), np.float32); sel1[0, 64:] = 1.0
    c["SEL0"], c["SEL1"] = sel0.astype(bf), sel1.astype(bf)
    r = np.arange(128)
    c["M_UT"] = (r[:, None] <= r[None, :]).astype(np.float32).astype(bf)
    c["M_LT"] = (r[:, None] >= r[None, :]).astype(np.float32).astype(bf)
    c["BD"] = ((r[:, None] < 64) == (r[None, :] < 64)).astype(np.float32).astype(bf)
    c["SHIFTT"] = (r[None, :] == r[:, None] + 1).astype(np.float32)
    e127 = np.zeros((128, 128), np.float32); e127[127, 0] = 1.0
    c["E127T"] = e127
    n = 2048
    half = np.arange(0, FHD, 2, dtype=np.float64)
    inv_freq = 1.0 / 10000.0 ** (half / FHD)
    t = np.arange(n, dtype=np.float64)
    freqs = np.concatenate([t[:, None] * inv_freq[None, :]] * 2, axis=-1)
    base = (half + 0.4 * FHD) / (1.4 * FHD)
    power = (t - n // 2) / float(WIN // 2)
    scale = np.concatenate([base[None, :] ** power[:, None]] * 2, axis=-1)
    cosv, sinv = np.cos(freqs), np.sin(freqs)
    sin_signed = sinv.copy()
    sin_signed[:, :32] *= -1.0
    c["RQC"] = np.tile((cosv * scale).astype(np.float32), (1, FH)).astype(bf)
    c["RQS"] = np.tile((sin_signed * scale).astype(np.float32), (1, FH)).astype(bf)
    c["RKC"] = np.tile((cosv / scale).astype(np.float32), (1, FH)).astype(bf)
    c["RKS"] = np.tile((sin_signed / scale).astype(np.float32), (1, FH)).astype(bf)
    return c


class Sem:
    def __init__(self, h):
        self.h, self.v = h, 0

    def inc(self, n=1):
        self.v += n
        return self.v


class MK:
    def __init__(self, cfg=None):
        self.cfg = cfg or {}
        self.nc = bass.Bass("TRN2", target_bir_lowering=False)
        self.ops = {e: [] for e in ENGINES}
        self.tap_t = {}

    def emit(self, eng, fn, waits=(), inc=None, inc_n=1):
        tgt = inc.inc(inc_n) if inc is not None else None
        wl = [(s.h, v) for s, v in waits if v > 0]

        def run(e, fn=fn, wl=wl, inc=inc, inc_n=inc_n):
            for h, v in wl:
                e.wait_ge(h, v)
            inst = fn(e)
            if inc is not None:
                inst.then_inc(inc.h, inc_n)
        self.ops[eng].append(run)
        return tgt

    def wait(self, eng, *sems_vals):
        wl = [(s.h, v) for s, v in sems_vals if v > 0]
        if wl:
            self.ops[eng].append(lambda e, wl=wl: [e.wait_ge(h, v) for h, v in wl])

    def _dma(self, eng, ring, idx_attr, dst, src, waits, slow=False):
        i = getattr(self, idx_attr)
        setattr(self, idx_attr, i + 1)
        s = ring[i % len(ring)]
        pre = ([(s, s.v)] if s.v > 0 else []) + [t for t in waits if t[1] > 0]
        tgt = self.emit(eng, lambda e, d=dst, sr=src, sl=slow: e.dma_start(
            out=d, in_=sr, allow_slow_non_contiguous=sl), waits=pre, inc=s, inc_n=16)
        return (s, tgt)

    def dma_g(self, dst, src, waits=(), slow=False):
        return self._dma("gpsimd", self.LDGr, "_ldg_i", dst, src, waits, slow)

    def dma_s(self, dst, src, waits=()):
        return self._dma("sync", self.STSr, "_sts_i", dst, src, waits)

    def dma_w3(self, dst, w, nk, waits=()):
        return self.dma_g(dst.rearrange("p (db o) -> p db o", db=nk),
                          w.rearrange("(db p) o -> p db o", p=128), waits=waits)

    def mark(self, name):
        if not hasattr(self, "_marks"):
            self._marks = []
            self._marks_e = []
        self._marks.append((name, getattr(self, "_mmn", 0)))
        self._marks_e.append((name, {e: len(self.ops[e]) for e in ENGINES}))

    def barrier(self, producers=ENGINES, consumers=ENGINES):
        tg = []
        for e in producers:
            if e == "sync":
                tg += [(s, s.v) for s in self.STSr]
            elif e == "gpsimd":
                tg.append((self.GE, self.GE.v))
                tg += [(s, s.v) for s in self.LDGr]
            else:
                tg.append((self.SEMS[e], self.SEMS[e].v))
        for e in consumers:
            self.wait(e, *tg)

    def _count_mm(self):
        self._mmn = getattr(self, "_mmn", 0) + 1

    def tr(self, out, in_, inc=None, waits=()):
        self._count_mm()
        return self.emit(
            "tensor",
            lambda e, o=out, i=in_: e.transpose(o, i, self.idn[:, :]),
            waits=waits, inc=inc)

    def mm(self, out, lhsT, rhs, start, stop, inc=None, tile_position=None, waits=()):
        self._count_mm()
        return self.emit(
            "tensor",
            lambda e, o=out, l=lhsT, r=rhs, st=start, sp=stop, tp=tile_position:
                e.matmul(o, l, r, start=st, stop=sp, tile_position=tp, skip_group_check=True),
            waits=waits, inc=inc)

    def vop(self, fn, waits=(), inc=None):
        sem = inc if inc is not None else self.VE
        w = ([(self.VE, self.VE.v)] if self.VE.v > 0 else []) + list(waits)
        return self.emit("vector", fn, waits=w, inc=sem)

    def sop(self, fn, waits=(), inc=None):
        sem = inc if inc is not None else self.SE
        w = ([(self.SE, self.SE.v)] if self.SE.v > 0 else []) + list(waits)
        return self.emit("scalar", fn, waits=w, inc=sem)

    def gop(self, fn, waits=(), inc=None):
        sem = inc if inc is not None else self.GE
        w = ([(self.GE, self.GE.v)] if self.GE.v > 0 else []) + list(waits)
        return self.emit("gpsimd", fn, waits=w, inc=sem)

    def tap(self, name, sbuf_ap, waits=()):
        if name in self.tap_t:
            self.dma_s(self.tap_t[name], sbuf_ap, waits=waits)

    def build(self):
        nc = self.nc
        cfg = self.cfg
        self.batches = cfg.get("batches", [0, 1])

        def din(name, shape, dt=F32):
            return nc.dram_tensor(name, list(shape), dt, kind="ExternalInput").ap()

        self.X = din("x", [2, S, D])
        self.Wq, self.Wk, self.Wv, self.Wo = (din("wq", [D, D]), din("wk", [D, D]),
                                              din("wv", [D, D]), din("wo", [D, D]))
        self.Wqp, self.Wkp = din("wqp", [D, D]), din("wkp", [D, D])
        self.LRAW = din("lraw", [1, H])
        self.FWQKV = din("fwqkv", [2, D, 3 * FIN])
        self.FQS, self.FKS = din("fqs", [2, FHD]), din("fks", [2, FHD])
        self.FWO = din("fwo", [2, FIN, D])
        self.FLG, self.FLB = din("flg", [2, D]), din("flb", [2, D])
        self.FW1, self.FB1 = din("fw1", [2, D, HID2]), din("fb1", [2, HID2])
        self.FW2, self.FB2 = din("fw2", [2, HID, D]), din("fb2", [2, D])
        for cname, cshape in (("IDN", [128, 128]), ("ONES64", [128, 1]), ("ONES1x64", [1, 64]),
                              ("SEL0", [1, 128]), ("SEL1", [1, 128]),
                              ("M_UT", [128, 128]), ("M_LT", [128, 128]),
                              ("BD", [128, 128]),
                              ("RQC", [S, 512]), ("RQS", [S, 512]),
                              ("RKC", [S, 512]), ("RKS", [S, 512])):
            setattr(self, "c" + cname, din(cname, cshape, BF16))
        self.cSHIFTT = din("SHIFTT", [128, 128], F32)
        self.cE127T = din("E127T", [128, 128], F32)
        self.Y = nc.dram_tensor("y", [2, S, D], F32, kind="ExternalOutput").ap()
        for tname, tshape, tdt in cfg.get("taps", []):
            self.tap_t[tname] = nc.dram_tensor("tap_" + tname, list(tshape), tdt,
                                               kind="ExternalOutput").ap()

        def scr(name, shape, dt):
            kind = "ExternalOutput" if name in cfg.get("dbg_out", ()) else "Internal"
            return nc.dram_tensor(name, list(shape), dt, kind=kind).ap()

        self.lam_d = scr("lam_d", [1, H], F32)
        self.qd_s = scr("qd_s", [128, HP * S], BF16)
        self.qc_s = scr("qc_s", [128, HP * S], BF16)
        self.qs_s = scr("qs_s", [128, HP * S], BF16)
        self.kd_s = scr("kd_s", [128, HP * S], BF16)
        self.kc_s = scr("kc_s", [128, HP * S], BF16)
        self.ks_s = scr("ks_s", [128, HP * S], BF16)
        self.vx_s = scr("vx_s", [128, NSB * H * 65], BF16)
        self.z1_s = scr("z1_s", [128, NSB * D], F32)
        self.f0_s = scr("f0_s", [128, NSB * D], F32)
        self.rbc_d = scr("rbc_d", [1, 4096], F32)
        self.u_s = scr("u_s", [128, NSB * D], F32)

        es = contextlib.ExitStack()
        self.es = es
        with es:
            self._nm = [0]
            def _sb(name, shape, dt=F32, stack=es):
                self._nm[0] += 1
                return stack.enter_context(
                    nc.sbuf_tensor(f"{name}_{self._nm[0]}", list(shape), dt))
            self.sb = _sb
            def _ps(name, shape, dt=F32, stack=es):
                self._nm[0] += 1
                return stack.enter_context(
                    nc.psum_tensor(f"{name}_{self._nm[0]}", list(shape), dt))
            self.ps = _ps
            self.ps_big = es.enter_context(nc.psum_tensor("ps_big", [128, 2048], F32))
            self.LDGr = [Sem(es.enter_context(nc.semaphore(f"ldg{i}"))) for i in range(8)]
            self.STSr = [Sem(es.enter_context(nc.semaphore(f"sts{i}"))) for i in range(8)]
            self._ldg_i = 0
            self._sts_i = 0
            self.LDG = self.LDGr[0]
            self.STS = self.STSr[0]
            self.PE = Sem(es.enter_context(nc.semaphore("pe")))
            self.VE = Sem(es.enter_context(nc.semaphore("ve")))
            self.SE = Sem(es.enter_context(nc.semaphore("se")))
            self.GE = Sem(es.enter_context(nc.semaphore("ge")))
            self.SEMS = dict(tensor=self.PE, vector=self.VE, scalar=self.SE,
                             gpsimd=self.GE)

            self.alloc_persistent()
            self.prologue()
            for bi in self.batches:
                self.emit_attn(bi)
                if not cfg.get("attn_only"):
                    for l in (0, 1):
                        self.emit_ffn(bi, l)
                    self.emit_final(bi)
            self.barrier(producers=ENGINES, consumers=("sync",))
            self.wait("sync", *[(s, s.v) for s in self.STSr])

            block = es.enter_context(nc.Block())
            for eng in ENGINES:
                def _replay(e, eng=eng):
                    ops = self.ops[eng]
                    bounds = [(nm, d[eng]) for nm, d in getattr(self, "_marks_e", [])]
                    idx = 0
                    for k, (nm, start) in enumerate(bounds):
                        while idx < start:
                            ops[idx](e)
                            idx += 1
                        end = bounds[k + 1][1] if k + 1 < len(bounds) else len(ops)
                        with nc.named_scope(nm):
                            while idx < end:
                                ops[idx](e)
                                idx += 1
                    while idx < len(ops):
                        ops[idx](e)
                        idx += 1
                getattr(block, eng)(_replay)
        return nc

    def alloc_persistent(self):
        sb = self.sb
        self.idn = sb("idn", [128, 128], BF16)
        self.ones64 = sb("ones64", [128, 1], BF16)
        self.ones1x64 = sb("ones1x64", [1, 64], BF16)
        self.sel0 = sb("sel0", [1, 128], BF16)
        self.sel1 = sb("sel1", [1, 128], BF16)
        self.m_ut = sb("m_ut", [128, 128], BF16)
        self.m_lt = sb("m_lt", [128, 128], BF16)
        self.bd = sb("bd", [128, 128], BF16)
        self.shiftT = sb("shiftT", [128, 128], F32)
        self.e127T = sb("e127T", [128, 128], F32)
        self.lamcol = sb("lamcol", [128, 8])
        self.lamq = sb("lamq", [1, H])
        self.cst = sb("cst", [128, 4])
        self.lnjunk = None
        self.stat = sb("stat", [128, 96])
        self.row = None
        self.rowb = None
        self.tT = None

    def tTv(self, db, s0, n):
        return self.tT[:, db * 2049 + 1 + s0: db * 2049 + 1 + s0 + n]

    def tTs(self, db, s0, n):
        return self.tT[:, db * 2049 + s0: db * 2049 + s0 + n]

    def prologue(self):
        sb, nc = self.sb, self.nc
        import math as _m
        self.vop(lambda e: e.memset(self.cst[:, 0:1], 1e-5))
        self.vop(lambda e: e.memset(self.cst[:, 1:2], 1.0))
        self.vop(lambda e: e.memset(self.cst[:, 2:3], _m.pi / 2))
        self.vop(lambda e: e.memset(self.cst[:, 3:4], 1e-6))
        for dst, src in ((self.idn, self.cIDN), (self.ones64, self.cONES64),
                         (self.ones1x64, self.cONES1x64), (self.sel0, self.cSEL0),
                         (self.sel1, self.cSEL1), (self.m_ut, self.cM_UT),
                         (self.m_lt, self.cM_LT), (self.bd, self.cBD),
                         (self.shiftT, self.cSHIFTT), (self.e127T, self.cE127T)):
            self.dma_g(dst[:, :], src)

        with contextlib.ExitStack() as ph:
            lr = self.sb("lr", [1, H], F32, ph)
            lr2 = self.sb("lr2", [1, H], F32, ph)
            t_ld = self.dma_g(lr[:, :], self.LRAW)
            t_tan = self.sop(lambda e: e.activation(lr2[:, :], lr[:, :], AF.Tanh, scale=1.0),
                             waits=[t_ld])
            t_lv = self.vop(lambda e: e.tensor_scalar(self.lamq[:, :], lr2[:, :], 0.5,
                                                      None, ALU.mult),
                            waits=[(self.SE, t_tan)])
            t_ls = self.dma_s(self.lam_d, self.lamq[:, :], waits=[(self.VE, t_lv)])
            lam2d = self.lam_d.rearrange("one (ob two) -> one ob two", two=2)
            t_l1 = self.dma_g(self.lamcol[0:64, :], lam2d[0:1, :, 0].partition_broadcast(64),
                              waits=[t_ls], slow=True)
            self.dma_g(self.lamcol[64:128, :], lam2d[0:1, :, 1].partition_broadcast(64),
                       waits=[t_l1], slow=True)

            self.barrier(producers=("sync", "gpsimd", "vector", "scalar"), consumers=ENGINES)



def shard_inputs(inputs):
    c = host_constants()
    xs = np.ascontiguousarray(inputs["L0_attn_x"])
    maps = []
    for core in range(8):
        m = dict(
            x=xs[2 * core:2 * core + 2],
            wq=inputs["L0_attn_Wq"], wk=inputs["L0_attn_Wk"], wv=inputs["L0_attn_Wv"],
            wo=inputs["L0_attn_Wo"], wqp=inputs["L0_attn_Wqp"], wkp=inputs["L0_attn_Wkp"],
            lraw=inputs["L0_attn_lambda_raw"].reshape(1, H),
            fwqkv=inputs["L0_ffn_wqkv"], fqs=inputs["L0_ffn_q_scale"],
            fks=inputs["L0_ffn_k_scale"], fwo=inputs["L0_ffn_wo"],
            flg=inputs["L0_ffn_ln_g"], flb=inputs["L0_ffn_ln_b"],
            fw1=inputs["L0_ffn_w1"], fb1=inputs["L0_ffn_b1"],
            fw2=inputs["L0_ffn_w2"], fb2=inputs["L0_ffn_b2"],
        )
        m.update(c)
        maps.append(m)
    return maps


def kernel(**inputs):
    mk = MK()
    nc = mk.build()
    maps = shard_inputs(inputs)
    res = run_bass_kernel_spmd(nc, maps, core_ids=list(range(8)))
    return np.concatenate([r["y"] for r in res.results], axis=0)


MK.emit_attn = emit_attn
MK.emit_ffn = emit_ffn
MK.emit_final = emit_final


# version 13
# speedup vs baseline: 1.0503x; 1.0032x over previous
import contextlib
import math
import concourse.mybir as mybir

F32 = mybir.dt.float32
BF16 = mybir.dt.bfloat16
AF = mybir.ActivationFunctionType
ALU = mybir.AluOpType
AX = mybir.AxisListType

S, D, H, HD = 2048, 1024, 16, 64
NSB, NDB, HP = 16, 8, 8
PI = math.pi


def x_view(self, bi, sb):
    return self.X[bi].rearrange("(sb p) d -> p sb d", p=128)[:, sb, :]


def emit_ln_stats(self, c, src_ap, pe_wait=None):
    st = self.stat
    VE, SE = self.VE, self.SE
    junk = self.lnjunk
    w = [pe_wait] if pe_wait else []
    self.sop(lambda e, j=junk: e.activation(j[:, :], src_ap, AF.Identity,
                                            accum_out=st[:, c + 6:c + 7]), waits=w)
    s2 = self.sop(lambda e, j=junk: e.activation(j[:, :], src_ap, AF.Square,
                                                 accum_out=st[:, c + 7:c + 8]))
    v1 = self.vop(lambda e: e.tensor_scalar(st[:, c:c + 1], st[:, c + 6:c + 7],
                                            1.0 / 1024, None, ALU.mult), waits=[(SE, s2)])
    v2 = self.vop(lambda e: e.tensor_tensor(st[:, c + 2:c + 3], st[:, c:c + 1],
                                            st[:, c:c + 1], ALU.mult), waits=[(VE, v1)])
    v3 = self.vop(lambda e: e.scalar_tensor_tensor(st[:, c + 1:c + 2], st[:, c + 7:c + 8],
                                                   1.0 / 1024, st[:, c + 2:c + 3],
                                                   ALU.mult, ALU.subtract), waits=[(VE, v2)])
    s3 = self.sop(lambda e: e.activation(st[:, c + 3:c + 4], st[:, c + 1:c + 2], AF.Ln,
                                         bias=self.cst[:, 0:1]), waits=[(VE, v3)])
    s4 = self.sop(lambda e: e.activation(st[:, c + 4:c + 5], st[:, c + 3:c + 4],
                                         AF.Exp, scale=-0.5), waits=[(SE, s3)])
    tv2 = self.vop(lambda e: e.scalar_tensor_tensor(
        st[:, c + 5:c + 6], st[:, c:c + 1], -1.0, st[:, c + 4:c + 5], ALU.mult, ALU.mult),
        waits=[(SE, s4)])
    return tv2


def emit_attn(self, bi):
    VE, SE, PE, LDG, STS = self.VE, self.SE, self.PE, self.LDG, self.STS
    with contextlib.ExitStack() as tstack:
        self.tT = self.sb("tT_a", [128, 8 * 2049], BF16, tstack)
        self.ps_tr = self.ps("tr", [128, 1024], BF16, tstack)
        self.ps_tr2 = self.ps("tr2", [128, 1024], BF16, tstack)
        self.ps_a = self.ps("pa", [128, 512], F32, tstack)
        self.ps_b = self.ps("pb", [128, 512], F32, tstack)
        tT_now = self.tT
        mz_t = self.vop(lambda e, t=tT_now: e.memset(t[:, :], 0.0))
        emit_attn_inner(self, bi, mz_t)
    self.tT = None
    if self.cfg.get("upto") in ("a0", "a1q", "a1"):
        return
    emit_attn2(self, bi)


def emit_attn_inner(self, bi, mz_t):
    VE, SE, PE, LDG, STS = self.VE, self.SE, self.PE, self.LDG, self.STS

    self.mark(f'A0_b{bi}')
    with contextlib.ExitStack() as ph:
        x0t = self.sb("a0_x0", [128, 4 * D], F32, ph)
        xnb = self.sb("a0_xn", [128, 4 * D], BF16, ph)
        self.lnjunk = self.sb("a0_junk", [128, 1024], F32, ph)
        se_app, ve_ev, pe_tr = {}, {}, {}
        for sb in range(NSB):
            par = sb % 2
            p4 = sb % 4
            x0 = x0t[:, p4 * D:(p4 + 1) * D]
            xn = xnb[:, p4 * D:(p4 + 1) * D]
            w = [(SE, se_app[sb - 4])] if sb - 4 in se_app else []
            tld = self.dma_g(x0, x_view(self, bi, sb), waits=w)
            c = p4 * 16
            emit_ln_stats(self, c, x0, pe_wait=tld)
            w = [(VE, self.VE.v)] + ([(PE, pe_tr[sb - 4])] if sb - 4 in pe_tr else [])
            se_app[sb] = self.sop(lambda e, xn=xn, x0=x0, c=c: e.activation(
                xn, x0, AF.Identity, scale=self.stat[:, c + 4:c + 5],
                bias=self.stat[:, c + 5:c + 6]), waits=w)
            pst = self.ps_tr if par == 0 else self.ps_tr2
            w = [(SE, se_app[sb])] + ([(VE, ve_ev[sb - 2])] if sb - 2 in ve_ev else [])
            for db in range(NDB):
                pe_tr[sb] = self.tr(pst[:, db * 128:(db + 1) * 128],
                                    xn[:, db * 128:(db + 1) * 128], inc=PE,
                                    waits=(w if db == 0 else ()))
            dst = self.tT[:, :].rearrange("p (db c) -> p db c", db=NDB)[:, :, 1 + sb * 128: 129 + sb * 128]
            src = pst[:, :].rearrange("p (k c) -> p k c", k=NDB)
            ve_ev[sb] = self.vop(lambda e, d=dst, s=src: e.tensor_copy(d, s),
                                 waits=[(PE, pe_tr[sb])] + ([(VE, mz_t)] if sb == 0 else []))
        self.barrier()
        self.tap("xnT", self.tT[:, :])
    if self.cfg.get("upto") == "a0":
        return

    self.mark(f'A1_b{bi}')
    with contextlib.ExitStack() as ph:
        wp = self.sb("a1_wp", [128, NDB * D], BF16, ph)
        wd = self.sb("a1_wd", [128, NDB * D], BF16, ph)
        cosP = self.sb("a1_cos", [128, NDB * S], BF16, ph)
        sinP = self.sb("a1_sin", [128, NDB * S], BF16, ph)
        sc1 = self.sb("a1_sc1", [128, 3 * 512], F32, ph)
        sc2 = self.sb("a1_sc2", [128, 2 * 512], F32, ph)
        asqb = self.sb("a1_asq", [128, 2 * 512], BF16, ph)
        rln = self.sb("a1_rln", [128, 512], F32, ph)
        rinvf = self.sb("a1_rinv", [128, 2 * 512], F32, ph)
        ampv = self.sb("a1_ampv", [128, 512], F32, ph)
        dstg = self.sb("a1_dst", [128, 2 * S], BF16, ph)
        cstg = self.sb("a1_cst", [128, 2 * S], BF16, ph)
        sstg = self.sb("a1_sst", [128, 2 * S], BF16, ph)

        pe_last_p = [0]
        pe_last_a = [0]
        for side in ("q", "k"):
            w_p = self.Wqp if side == "q" else self.Wkp
            w_d = self.Wq if side == "q" else self.Wk
            d_s = self.qd_s if side == "q" else self.kd_s
            c_s = self.qc_s if side == "q" else self.kc_s
            s_s = self.qs_s if side == "q" else self.ks_s
            t_wp = self.dma_w3(wp[:, :], w_p, NDB, waits=[(PE, pe_last_p[0])])
            t_wd = self.dma_w3(wd[:, :], w_d, NDB, waits=[(PE, pe_last_a[0])])


            se_p = {}
            for it in range(int(self.cfg.get('a1n', 32))):
                ob, ch = divmod(it, 4)
                par = it % 2
                pss = self.ps_a if par == 0 else self.ps_b
                w = [t_wp] + ([(SE, se_p[it - 2])] if it - 2 in se_p else [])
                for db in range(NDB):
                    pem = self.mm(pss[:, :],
                                  wp[:, db * D + ob * 128: db * D + ob * 128 + 128],
                                  self.tTv(db, ch * 512, 512),
                                  start=(db == 0), stop=(db == NDB - 1),
                                  inc=(PE if db == NDB - 1 else None),
                                  waits=(w if db == 0 else ()))
                s1v = sc1[:, par * 512:(par + 1) * 512]
                s2v = sc2[:, par * 512:(par + 1) * 512]
                cpv = cosP[:, ob * S + ch * 512: ob * S + ch * 512 + 512]
                spv = sinP[:, ob * S + ch * 512: ob * S + ch * 512 + 512]
                self.sop(lambda e, o=s1v, i=pss: e.activation(o, i[:, :], AF.Tanh),
                         waits=[(PE, pem)])
                self.sop(lambda e, o=s2v, i=s1v: e.activation(o, i, AF.Abs))
                self.sop(lambda e, o=cpv, i=s2v: e.activation(o, i, AF.Sin,
                                                              scale=-PI, bias=self.cst[:, 2:3]))
                se_p[it] = self.sop(lambda e, o=spv, i=s1v: e.activation(o, i, AF.Sin,
                                                                         scale=PI))
            pe_last_p[0] = PE.v
            self.barrier(producers=("scalar",), consumers=("tensor",))
            if self.cfg.get("upto") == "a1p":
                self.barrier()
                return

            se_ex, se_sq, se_ri, ve_d, ve_f, rms_pe, st_s = {}, {}, {}, {}, {}, {}, {}

            def projA(it):
                ob, ch = divmod(it, 4)
                par = it % 2
                sbp = ob % 2
                pss = self.ps_a if par == 0 else self.ps_b
                w = [t_wd]
                if it - 2 in se_ex:
                    w += [(SE, se_ex[it - 2]), (VE, ve_d[it - 2])]
                pem = None
                for db in range(NDB):
                    pem = self.mm(pss[:, :],
                                  wd[:, db * D + ob * 128: db * D + ob * 128 + 128],
                                  self.tTv(db, ch * 512, 512),
                                  start=(db == 0), stop=(db == NDB - 1),
                                  inc=(PE if db == NDB - 1 else None),
                                  waits=(w if db == 0 else ()))
                s1v = sc1[:, (it % 3) * 512:(it % 3 + 1) * 512]
                s2v = sc2[:, par * 512:(par + 1) * 512]
                aqv = asqb[:, par * 512:(par + 1) * 512]
                ddst = dstg[:, sbp * S + ch * 512: sbp * S + ch * 512 + 512]
                w = [(PE, pem)]
                if ch == 0 and (ob - 2) in st_s:
                    w.append(st_s[ob - 2][0])
                ve_d[it] = self.vop(lambda e, o=ddst, i=pss: e.tensor_copy(o, i[:, :]), waits=w)
                se_ex[it] = self.sop(lambda e, o=s2v, i=pss: e.activation(o, i[:, :], AF.Exp),
                                     waits=[(PE, pem)])
                w = [(VE, ve_f[it - 3])] if it - 3 in ve_f else []
                self.sop(lambda e, o=s1v, i=s2v: e.activation(o, i, AF.Ln,
                                                              bias=self.cst[:, 1:2]), waits=w)
                w = [(PE, rms_pe[it - 2])] if it - 2 in rms_pe else []
                se_sq[it] = self.sop(lambda e, o=aqv, i=s1v: e.activation(o, i, AF.Square),
                                     waits=w)

            def postA(it):
                ob, ch = divmod(it, 4)
                par = it % 2
                sbp = ob % 2
                s1v = sc1[:, (it % 3) * 512:(it % 3 + 1) * 512]
                aqv = asqb[:, par * 512:(par + 1) * 512]
                riv = rinvf[:, par * 512:(par + 1) * 512]
                cdst = cstg[:, sbp * S + ch * 512: sbp * S + ch * 512 + 512]
                sdst = sstg[:, sbp * S + ch * 512: sbp * S + ch * 512 + 512]
                cpv = cosP[:, ob * S + ch * 512: ob * S + ch * 512 + 512]
                spv = sinP[:, ob * S + ch * 512: ob * S + ch * 512 + 512]
                prs = self.ps_big[:, (2 + par) * 512:(3 + par) * 512]
                w = [(SE, se_sq[it])]
                if it - 2 in se_ri:
                    w.append((SE, se_ri[it - 2]))
                rms_pe[it] = self.mm(prs, self.bd[:, :], aqv, start=True, stop=True,
                                     inc=PE, waits=w)
                self.sop(lambda e, i=prs: e.activation(rln[:, :], i[:, :] if False else i, AF.Ln,
                                                       scale=1.0 / 64, bias=self.cst[:, 3:4]),
                         waits=[(PE, rms_pe[it])])
                wri = [(VE, ve_f[it - 2])] if it - 2 in ve_f else []
                se_ri[it] = self.sop(lambda e, o=riv: e.activation(o, rln[:, :], AF.Exp,
                                                                   scale=-0.5), waits=wri)
                if side == "q":
                    self.vop(lambda e, o=ampv[:, :], a=s1v, r=riv, ob=ob: e.scalar_tensor_tensor(
                        o, a, self.lamcol[:, ob:ob + 1], r, ALU.mult, ALU.mult),
                        waits=[(SE, se_ri[it])])
                else:
                    self.vop(lambda e, o=ampv[:, :], a=s1v, r=riv: e.tensor_tensor(
                        o, a, r, ALU.mult), waits=[(SE, se_ri[it])])
                wc = [st_s[ob - 2][1]] if (ch == 0 and (ob - 2) in st_s) else []
                self.vop(lambda e, o=cdst, c=cpv, am=ampv[:, :]: e.tensor_tensor(o, am, c, ALU.mult),
                         waits=wc)
                ws = [st_s[ob - 2][2]] if (ch == 0 and (ob - 2) in st_s) else []
                ve_f[it] = self.vop(lambda e, o=sdst, s=spv, am=ampv[:, :]: e.tensor_tensor(
                    o, am, s, ALU.mult), waits=ws)
                if ch == 3:
                    w = [(VE, ve_f[it]), (VE, ve_d[it])]
                    t_d = self.dma_s(d_s[:, ob * S:(ob + 1) * S],
                                     dstg[:, sbp * S:(sbp + 1) * S], waits=w)
                    t_c = self.dma_s(c_s[:, ob * S:(ob + 1) * S], cstg[:, sbp * S:(sbp + 1) * S])
                    t_s2 = self.dma_s(s_s[:, ob * S:(ob + 1) * S], sstg[:, sbp * S:(sbp + 1) * S])
                    st_s[ob] = (t_d, t_c, t_s2)

            nA = int(self.cfg.get("a1an", 32))
            for it in range(nA + 1):
                if it < nA:
                    projA(it)
                if it >= 1:
                    postA(it - 1)
            pe_last_a[0] = PE.v
            self.barrier()
            if self.cfg.get("upto") == "a1q":
                return

        self.mark(f'A1V_b{bi}')
        t_wv = self.dma_w3(wp[:, :], self.Wv, NDB)
        vst = self.sb("a1_vst", [128, 2 * 1040], BF16, ph)
        mv = self.vop(lambda e: e.memset(vst[:, :], 1.0))
        ve_v, st_v = {}, {}
        for sb in range(NSB):
            par = sb % 2
            psy = self.ps_big[:, par * 1024:(par + 1) * 1024]
            w = [t_wv, (VE, mv)] + ([(VE, ve_v[sb - 2])] if sb - 2 in ve_v else [])
            for db in range(NDB):
                for nn in range(2):
                    pem = self.mm(psy[:, nn * 512:(nn + 1) * 512],
                                  self.tTv(db, sb * 128, 128),
                                  wp[:, db * D + nn * 512: db * D + (nn + 1) * 512],
                                  start=(db == 0), stop=(db == NDB - 1),
                                  inc=(PE if db == NDB - 1 and nn == 1 else None),
                                  waits=(w if db == 0 and nn == 0 else ()))
            dst = vst[:, par * 1040:(par + 1) * 1040].rearrange("p (h c) -> p h c", h=H)[:, :, 0:64]
            src = psy.rearrange("p (h c) -> p h c", h=H)
            w = [(PE, pem)] + ([st_v[sb - 2]] if sb - 2 in st_v else [])
            ve_v[sb] = self.vop(lambda e, d=dst, s=src: e.tensor_copy(d, s), waits=w)
            st_v[sb] = self.dma_s(self.vx_s[:, sb * 1040:(sb + 1) * 1040],
                                  vst[:, par * 1040:(par + 1) * 1040],
                                  waits=[(VE, ve_v[sb])])
        self.barrier()


def emit_cast_fw12(self, ph):
    wbufs = [self.sb("cwA", [128, 2730], BF16, ph),
             self.sb("cwB", [128, 2730], BF16, ph)]
    st = {"i": 0, "hist": {}}

    def emit_piece(src, dst, cols, db=None, np_=128):
        i = st["i"]
        st["i"] += 1
        wbuf = wbufs[i % 2]
        wt = [st["hist"][i - 2]] if i - 2 in st["hist"] else []
        wv = wbuf[0:np_, 0:cols]
        lv = wv.rearrange("p (db o) -> p db o", db=db) if db else wv
        t1 = self.dma_g(lv, src, waits=wt)
        st["hist"][i] = self.dma_s(dst, wv, waits=[t1])

    def pf(src, dst, cols, db=None, np_=128):
        return lambda: emit_piece(src, dst, cols, db, np_)

    fns = []
    for l in (0, 1):
        for j in range(NDB):
            for hh in (0, 1):
                fns.append(pf(
                    self.FW1[l][128 * j:128 * j + 128, hh * 2730:(hh + 1) * 2730],
                    self.fw1_c[l][:, j * HID2 + hh * 2730: j * HID2 + (hh + 1) * 2730],
                    2730))
        for g in range(10):
            fns.append(pf(
                self.FW2[l][256 * g:256 * g + 256].rearrange("(db p) o -> p db o", p=128),
                self.fw2_c[l][:, g * 2 * D:(g + 1) * 2 * D], 2 * D, db=2))
        fns.append(pf(self.FW2[l][2560:2688, :], self.fw2_c[l][:, 20 * D:21 * D], D))
        fns.append(pf(self.FW2[l][2688:2730, :], self.fw2_c[l][0:42, 21 * D:22 * D],
                      D, np_=42))
    return fns


def emit_attn2(self, bi):
    self.mark(f'A2_b{bi}')
    VE, SE, PE, LDG, STS, GE = self.VE, self.SE, self.PE, self.LDG, self.STS, self.GE
    with contextlib.ExitStack() as ph:
        cast_fns = emit_cast_fw12(self, ph) if bi == self.batches[0] else []
        ps_s = [self.ps_big[:, 0:1024], self.ps_big[:, 1024:2048]]
        ps_pv = self.ps("a2_pv", [128, 1024], F32, ph)
        ps_bc = self.ps("a2_bc", [128, 1024], F32, ph)
        kdcb = self.sb("a2_kdc", [128, 4 * S], BF16, ph)
        qdcb = self.sb("a2_qdc", [128, 4 * S], BF16, ph)
        ksb = self.sb("a2_ks", [128, 2 * S], BF16, ph)
        qszb = self.sb("a2_qsz", [128, 4 * S], BF16, ph)
        vxb = self.sb("a2_vx", [128, 2 * NSB * 196], BF16, ph)
        ptb = self.sb("a2_pt", [128, 2 * 1024], BF16, ph)
        otb = self.sb("a2_ot", [128, HP * S], BF16, ph)
        opv = self.sb("a2_op", [64, 2 * 512], BF16, ph)
        onrm = self.sb("a2_on", [64, 2 * 512], BF16, ph)
        rowA = self.sb("a2_rA", [1, 2 * 512], F32, ph)
        rowB = self.sb("a2_rB", [1, 2 * 512], BF16, ph)
        csu = self.sb("a2_csu", [128, 128], BF16, ph)
        cs_v = self.vop(lambda e: e.tensor_scalar(csu[:, :], self.m_lt[:, :],
                                                  30000.0, -30000.0, ALU.mult, ALU.add))
        mz0 = self.vop(lambda e: e.memset(ps_s[0][:, :], 0.0))
        mz1 = self.vop(lambda e: e.memset(ps_s[1][:, :], 0.0))
        mzq = self.vop(lambda e: e.memset(qszb[:, :], 0.0))
        mzv = self.vop(lambda e: e.memset(vxb[:, :], 1.0))
        pair_pe_done = {}
        pv_guard = {}
        bc_guard = {}
        row_guard = {}
        on_guard = {}
        ld_hp = {}

        def load_pair(hp):
            par = hp % 2
            w = [(PE, pair_pe_done[hp - 2])] if hp - 2 in pair_pe_done else []
            toks = []
            for hl in range(2):
                for half, st in ((0, self.kd_s), (1, self.kc_s)):
                    toks.append(self.dma_g(
                        kdcb[64 * half:64 * half + 64,
                             par * 2 * S + hl * S:par * 2 * S + (hl + 1) * S],
                        st[hl * 64:hl * 64 + 64, hp * S:(hp + 1) * S], waits=w))
                    w = []
                for half, st in ((0, self.qd_s), (1, self.qc_s)):
                    toks.append(self.dma_g(
                        qdcb[64 * half:64 * half + 64,
                             par * 2 * S + hl * S:par * 2 * S + (hl + 1) * S],
                        st[hl * 64:hl * 64 + 64, hp * S:(hp + 1) * S]))
            toks.append(self.dma_g(ksb[:, par * S:(par + 1) * S],
                                   self.ks_s[:, hp * S:(hp + 1) * S]))
            wq = [(VE, mzq)] if hp < 2 else []
            for hl in range(2):
                toks.append(self.dma_g(
                    qszb[hl * 64:hl * 64 + 64,
                         par * 2 * S + hl * S:par * 2 * S + hl * S + S],
                    self.qs_s[hl * 64:hl * 64 + 64, hp * S:(hp + 1) * S], waits=wq))
                wq = []
            vsrc = self.vx_s.rearrange("p (sb c) -> p sb c", sb=NSB)[:, :, 130 * hp:130 * (hp + 1)]
            vdst = vxb[:, par * NSB * 196:(par + 1) * NSB * 196] \
                .rearrange("p (sb c) -> p sb c", sb=NSB)[:, :, 0:130]
            toks.append(self.dma_g(vdst, vsrc, waits=([(VE, mzv)] if hp < 2 else [])))
            return toks

        ld_hp[0] = load_pair(0)
        for hp in range(HP):
            par = hp % 2
            if hp + 1 < HP:
                ld_hp[hp + 1] = load_pair(hp + 1)
            for _ in range(9):
                if cast_fns:
                    cast_fns.pop(0)()
            for parity in range(2):
                h = 2 * hp + parity
                base = 64 * parity
                tp = (base, 0)
                se_ex = {}
                first_w = list(ld_hp[hp]) + [(VE, cs_v), (VE, mz0), (VE, mz1)]
                tiles = []
                for c in range(4):
                    for kb in range(4 * c + 4):
                        tiles.append((kb, c))
                pairs = [tiles[i:i + 2] for i in range(0, len(tiles), 2)]

                def emit_S_pair(pi, pr):
                    sl = pi % 2
                    pss = ps_s[sl]
                    nonlocal first_w
                    w = list(first_w)
                    first_w = []
                    if pi - 2 in se_ex:
                        w.append((SE, se_ex[pi - 2]))
                    pem = None
                    for half, (kb, c) in enumerate(pr):
                        col0 = max(kb * 128, 512 * c)
                        n = 512 * (c + 1) - col0
                        off = half * 512
                        b2 = par * 2 * S + parity * S
                        self.mm(pss[:, off:off + n],
                                kdcb[:, b2 + kb * 128: b2 + kb * 128 + 128],
                                qdcb[:, b2 + col0: b2 + col0 + n],
                                start=True, stop=False, waits=w)
                        w = []
                        diag = (col0 == kb * 128)
                        qz = par * 2 * S + parity * S
                        pem = self.mm(pss[:, off:off + n],
                                      ksb[:, par * S + kb * 128: par * S + kb * 128 + 128],
                                      qszb[:, qz + col0: qz + col0 + n],
                                      start=False, stop=not diag,
                                      inc=(None if diag else PE))
                        if diag:
                            pem = self.mm(pss[:, off:off + 128], csu[:, :], self.idn[:, :],
                                          start=False, stop=True, inc=PE)
                    kb1, c1 = pr[-1]
                    ntot = 512 + 512 * (c1 + 1) - max(kb1 * 128, 512 * c1) if len(pr) == 2 \
                        else 512
                    ex = self.sop(lambda e, o=ptb[:, sl * 1024: sl * 1024 + ntot],
                                  i=pss, ntot=ntot:
                                  e.activation(o, i[:, 0:ntot], AF.Exp, scale=0.125),
                                  waits=[(PE, pem)])
                    se_ex[pi] = ex
                    return (sl, pr, ex)

                def emit_PV_pair(p):
                    sl, pr, ex = p
                    for half, (kb, c) in enumerate(pr):
                        col0 = max(kb * 128, 512 * c)
                        n = 512 * (c + 1) - col0
                        s = c % 2
                        lv = vxb[:, par * NSB * 196:(par + 1) * NSB * 196] \
                            .rearrange("p (sb c) -> p sb c", sb=NSB)[:, kb, parity * 65:parity * 65 + 128]
                        w = [(SE, ex)]
                        if kb == 0:
                            w += pv_guard.pop(s, [])
                        pv = self.mm(ps_pv[:, s * 512 + col0 - 512 * c:
                                           s * 512 + col0 - 512 * c + n], lv,
                                     ptb[:, sl * 1024 + half * 512: sl * 1024 + half * 512 + n],
                                     start=(kb == 0), stop=(kb == 4 * c + 3),
                                     inc=PE, waits=w)
                        if kb == 4 * c + 3:
                            emit_chunk_norm(c, s, pv)

                def emit_chunk_norm(c, s, pv):
                    ov = opv[:, s * 512:(s + 1) * 512]
                    cp = self.vop(lambda e, ov=ov, s=s: e.tensor_copy(
                        ov, ps_pv[0:64, s * 512:(s + 1) * 512]), waits=[(PE, pv)])
                    wl = [(PE, pv)] + ([row_guard[s]] if s in row_guard else [])
                    s1 = self.sop(lambda e, s=s: e.activation(
                        rowA[0:1, s * 512:(s + 1) * 512],
                        ps_pv[64:65, s * 512:(s + 1) * 512], AF.Ln), waits=wl)
                    s2 = self.sop(lambda e, s=s: e.activation(
                        rowB[0:1, s * 512:(s + 1) * 512],
                        rowA[0:1, s * 512:(s + 1) * 512], AF.Exp, scale=-1.0))
                    pv_guard[s] = [(VE, cp), (SE, s2)]
                    wb = [(SE, s2)] + ([bc_guard[s]] if s in bc_guard else [])
                    bc = self.mm(ps_bc[0:64, s * 512:(s + 1) * 512], self.ones1x64[:, :],
                                 rowB[0:1, s * 512:(s + 1) * 512], start=True, stop=True,
                                 inc=PE, waits=wb)
                    row_guard[s] = (PE, bc)
                    blk = otb[:, hp * S + c * 512: hp * S + (c + 1) * 512]
                    if parity == 0:
                        tt = self.vop(lambda e, blk=blk, ov=ov, s=s: e.tensor_tensor(
                            blk[0:64, :], ov, ps_bc[0:64, s * 512:(s + 1) * 512], ALU.mult),
                            waits=[(PE, bc)])
                        bc_guard[s] = (VE, tt)
                    else:
                        osl = c % 2
                        onv = onrm[:, osl * 512:(osl + 1) * 512]
                        w2 = [(PE, bc)] + ([on_guard[osl]] if osl in on_guard else [])
                        tt = self.vop(lambda e, onv=onv, ov=ov, s=s: e.tensor_tensor(
                            onv, ov, ps_bc[0:64, s * 512:(s + 1) * 512], ALU.mult), waits=w2)
                        bc_guard[s] = (VE, tt)
                        sv = self.sop(lambda e, blk=blk, onv=onv: e.activation(
                            blk[64:128, :], onv, AF.Copy), waits=[(VE, tt)])
                        on_guard[osl] = (SE, sv)

                pend = None
                for pi, pr in enumerate(pairs):
                    cur = emit_S_pair(pi, pr)
                    if pend is not None:
                        emit_PV_pair(pend)
                    pend = cur
                emit_PV_pair(pend)
                pair_pe_done[hp] = PE.v
        while cast_fns:
            cast_fns.pop(0)()
        self.barrier()
        self.tap("otb", otb[:, :])

        self.mark(f'A2Wo_b{bi}')
        wo = self.sb("a2_wo", [128, NDB * D], BF16, ph)
        self.lnjunk = self.sb("a2_junk", [128, 1024], F32, ph)
        x0t = self.sb("a2_x0", [128, 4 * D], F32, ph)
        z1t = self.sb("a2_z1", [128, 4 * D], F32, ph)
        zsc = self.sb("a2_zs", [128, 4 * D], F32, ph)
        t_wo = self.dma_w3(wo[:, :], self.Wo, NDB)
        ve_ap, ve_z, st_z = {}, {}, {}
        for sb in range(NSB):
            par = sb % 2
            p4 = sb % 4
            w = [(VE, ve_z[sb - 4])] if sb - 4 in ve_z else []
            ldx = self.dma_g(x0t[:, p4 * D:(p4 + 1) * D], x_view(self, bi, sb), waits=w)
            psy = self.ps_big[:, par * 1024:(par + 1) * 1024]
            w = [t_wo] + ([(VE, ve_ap[sb - 2])] if sb - 2 in ve_ap else [])
            for db in range(NDB):
                for nn in range(2):
                    pem = self.mm(psy[:, nn * 512:(nn + 1) * 512],
                                  otb[:, db * S + sb * 128: db * S + sb * 128 + 128],
                                  wo[:, db * D + nn * 512: db * D + (nn + 1) * 512],
                                  start=(db == 0), stop=(db == NDB - 1),
                                  inc=(PE if db == NDB - 1 and nn == 1 else None),
                                  waits=(w if db == 0 and nn == 0 else ()))
            c = p4 * 16
            emit_ln_stats(self, c, psy, pe_wait=(PE, pem))
            zv = z1t[:, p4 * D:(p4 + 1) * D]
            zs = zsc[:, p4 * D:(p4 + 1) * D]
            w = [(VE, self.VE.v), ldx] + ([st_z[sb - 4]] if sb - 4 in st_z else [])
            ve_ap[sb] = self.vop(lambda e, zs=zs, c=c, psy=psy: e.tensor_scalar(
                zs, psy, self.stat[:, c + 4:c + 5],
                self.stat[:, c + 5:c + 6], ALU.mult, ALU.add), waits=w)
            ve_z[sb] = self.vop(lambda e, zv=zv, zs=zs, x0=x0t[:, p4 * D:(p4 + 1) * D]:
                                e.tensor_tensor(zv, zs, x0, ALU.add))
            st_z[sb] = self.dma_s(self.z1_s[:, sb * D:(sb + 1) * D], zv,
                                  waits=[(VE, ve_z[sb])])
        self.barrier()


"""FFN (2-layer local-attention transformer) phases + final combine. Mixin for MK."""
import contextlib
import concourse.mybir as mybir

F32 = mybir.dt.float32
BF16 = mybir.dt.bfloat16
AF = mybir.ActivationFunctionType
ALU = mybir.AluOpType
AX = mybir.AxisListType

S, D = 2048, 1024
NSB, NDB = 16, 8
FH, FHD, FIN, WIN = 8, 64, 512, 512
HID, HID2, NJ = 2730, 5460, 22


def emit_ffn(self, bi, l):
    with contextlib.ExitStack() as tstack:
        self.tT = self.sb("tT_f", [128, 8 * 2049], BF16, tstack)
        self.ps_a = self.ps("pa", [128, 512], F32, tstack)
        self.ps_b = self.ps("pb", [128, 512], F32, tstack)
        tT_now = self.tT
        mz_t = self.vop(lambda e, t=tT_now: e.memset(t[:, :], 0.0))
        emit_f0(self, bi, l, mz_t)
        with contextlib.ExitStack() as s1:
            vxf = self.sb("f1_vx", [128, NSB * 584], BF16, s1)
            qrTz = self.sb("f2_qTz", [128, 8 * S], BF16, s1)
            krT = self.sb("f2_kT", [128, 4 * S], BF16, s1)
            with contextlib.ExitStack() as s2:
                qrot = self.sb("f1_qr", [128, NSB * 512], BF16, s2)
                krot = self.sb("f1_kr", [128, NSB * 512], BF16, s2)
                emit_f1(self, bi, l, qrot, krot, vxf)
                emit_f2(self, qrot, krot, qrTz, krT)
            with contextlib.ExitStack() as s3:
                oT = self.sb("f3_oT", [128, 4 * S], BF16, s3)
                emit_f3(self, bi, l, qrTz, krT, vxf, oT)
                emit_f4(self, bi, l, oT)
        emit_f56(self, bi, l)
    self.tT = None


def emit_f0(self, bi, l, mz_t):
    self.mark(f'F0_b{bi}l{l}')
    """x_f stream tile-wise: l==0: LN(z1) -> f0_s spill + bf16 -> transpose;
    l==1: f0_s -> bf16 -> transpose."""
    VE, SE, PE, LDG, STS = self.VE, self.SE, self.PE, self.LDG, self.STS
    with contextlib.ExitStack() as ph:
        self.ps_tr = self.ps("tr", [128, 1024], BF16, ph)
        self.ps_tr2 = self.ps("tr2", [128, 1024], BF16, ph)
        xnb = self.sb("f0_xn", [128, 4 * D], BF16, ph)
        self.lnjunk = self.sb("f0_junk", [128, 1024], F32, ph)
        z1t = self.sb("f0_z1", [128, 4 * D], F32, ph)
        xft = self.sb("f0_xf", [128, 4 * D], F32, ph)
        se_app, ve_ev, pe_tr, st_f = {}, {}, {}, {}
        for sb in range(NSB):
            par = sb % 2
            p4 = sb % 4
            xn = xnb[:, p4 * D:(p4 + 1) * D]
            if l == 0:
                z1 = z1t[:, p4 * D:(p4 + 1) * D]
                xfv = xft[:, p4 * D:(p4 + 1) * D]
                w = [(SE, se_app[sb - 4])] if sb - 4 in se_app else []
                tld = self.dma_g(z1, self.z1_s[:, sb * D:(sb + 1) * D], waits=w)
                c = p4 * 16
                emit_ln_stats(self, c, z1, pe_wait=tld)
                w = [(VE, self.VE.v)] + ([st_f[sb - 4]] if sb - 4 in st_f else [])
                sf = self.sop(lambda e, o=xfv, i=z1, c=c: e.activation(
                    o, i, AF.Identity, scale=self.stat[:, c + 4:c + 5],
                    bias=self.stat[:, c + 5:c + 6]), waits=w)
                se_app[sb] = sf
                st_f[sb] = self.dma_s(self.f0_s[:, sb * D:(sb + 1) * D], xfv,
                                      waits=[(SE, sf)])
                w = [(SE, sf)] + ([(PE, pe_tr[sb - 4])] if sb - 4 in pe_tr else [])
                ve_c = self.vop(lambda e, o=xn, i=xfv: e.tensor_copy(o, i), waits=w)
            else:
                xfv = xft[:, p4 * D:(p4 + 1) * D]
                w = [(SE, se_app[sb - 4])] if sb - 4 in se_app else []
                tld = self.dma_g(xfv, self.f0_s[:, sb * D:(sb + 1) * D], waits=w)
                w = [tld] + ([(PE, pe_tr[sb - 4])] if sb - 4 in pe_tr else [])
                se_app[sb] = self.sop(lambda e, o=xn, i=xfv: e.activation(o, i, AF.Copy),
                                      waits=w)
                ve_c = self.VE.v
            pst = self.ps_tr if par == 0 else self.ps_tr2
            if l == 0:
                w = [(VE, ve_c)] + ([(VE, ve_ev[sb - 2])] if sb - 2 in ve_ev else [])
            else:
                w = [(SE, se_app[sb])] + ([(VE, ve_ev[sb - 2])] if sb - 2 in ve_ev else [])
            for db in range(NDB):
                pe_tr[sb] = self.tr(pst[:, db * 128:(db + 1) * 128],
                                    xn[:, db * 128:(db + 1) * 128], inc=PE,
                                    waits=(w if db == 0 else ()))
            dst = self.tT[:, :].rearrange("p (db c) -> p db c", db=NDB)[:, :, 1 + sb * 128: 129 + sb * 128]
            src = pst[:, :].rearrange("p (k c) -> p k c", k=NDB)
            ve_ev[sb] = self.vop(lambda e, d=dst, s=src: e.tensor_copy(d, s),
                                 waits=[(PE, pe_tr[sb])] + ([(VE, mz_t)] if sb == 0 else []))
        self.barrier()


def emit_f1(self, bi, l, qrot, krot, vxf):
    self.mark(f'F1_b{bi}l{l}')
    VE, SE, PE, LDG, STS, GE = self.VE, self.SE, self.PE, self.LDG, self.STS, self.GE
    with contextlib.ExitStack() as ph:
        wqkv = self.sb("f1_w", [128, NDB * 1536], BF16, ph)
        qsb_t = self.sb("f1_qs", [128, 512], BF16, ph)
        ksb_t = self.sb("f1_ks", [128, 512], BF16, ph)
        rot_t = self.sb("f1_rot", [128, 2 * 2048], BF16, ph)
        sq_t = self.sb("f1_sqt", [128, 2 * 512], F32, ph)
        qn_t = self.sb("f1_qn", [128, 1024], F32, ph)
        qn2_t = self.sb("f1_qn2", [128, 1024], F32, ph)
        r1_t = self.sb("f1_r1", [128, 1024], F32, ph)
        r2_t = self.sb("f1_r2", [128, 1024], F32, ph)
        t_w = self.dma_w3(wqkv[:, :], self.FWQKV[l], NDB)
        t_qs = t_ks = None
        for hh in range(FH):
            t_qs = self.dma_g(qsb_t[:, hh * FHD:(hh + 1) * FHD],
                              self.FQS[l][None, :].partition_broadcast(128),
                              waits=([t_qs] if t_qs else []))
            t_ks = self.dma_g(ksb_t[:, hh * FHD:(hh + 1) * FHD],
                              self.FKS[l][None, :].partition_broadcast(128),
                              waits=([t_ks] if t_ks else []))
        mv = self.vop(lambda e: e.memset(vxf[:, :], 1.0))
        ve_done, ve_rot = {}, {}
        for sb in range(NSB):
            par = sb % 2
            t_rots = []
            for ti, tab in enumerate((self.cRQC, self.cRQS, self.cRKC, self.cRKS)):
                wt = [(VE, ve_rot[sb - 2])] if sb - 2 in ve_rot else []
                t_rots.append(self.dma_g(
                    rot_t[:, par * 2048 + ti * 512: par * 2048 + (ti + 1) * 512],
                    tab[sb * 128:(sb + 1) * 128, :], waits=wt))
            if par == 0:
                psums = [self.ps_big[:, 0:512], self.ps_big[:, 512:1024],
                         self.ps_big[:, 1024:1536]]
            else:
                psums = [self.ps_big[:, 1536:2048], self.ps_a[:, :], self.ps_b[:, :]]
            w = [t_w, t_qs, t_ks, (VE, mv)]
            if sb - 2 in ve_done:
                w.append((VE, ve_done[sb - 2]))
            pem = None
            for db in range(NDB):
                lhs = self.tTv(db, sb * 128, 128) if db < 4 else self.tTs(db, sb * 128, 128)
                for nn in range(3):
                    pem = self.mm(psums[nn], lhs,
                                  wqkv[:, db * 1536 + nn * 512: db * 1536 + (nn + 1) * 512],
                                  start=(db == 0), stop=(db == NDB - 1),
                                  inc=(PE if db == NDB - 1 and nn == 2 else None),
                                  waits=(w if db == 0 and nn == 0 else ()))
            last_rot = None
            for side in range(2):
                psv = psums[side]
                sqv = sq_t[:, par * 512:(par + 1) * 512]
                wsq = [(PE, pem)] + ([(VE, ve_done[sb - 2])] if sb - 2 in ve_done else [])
                sv = self.sop(lambda e, o=sqv, i=psv: e.activation(o, i, AF.Square),
                              waits=wsq)
                c = 32 + par * 32 + side * 16
                rv = self.vop(lambda e, c=c, sqv=sqv: e.tensor_reduce(
                    self.stat[:, c:c + 8], sqv.rearrange("p (h d) -> p h d", h=FH),
                    AX.X, ALU.add), waits=[(SE, sv)])
                rv = self.vop(lambda e, c=c: e.tensor_scalar(
                    self.stat[:, c + 8:c + 16], self.stat[:, c:c + 8], 1e-24, None, ALU.max),
                    waits=[(VE, rv)])
                s_l = self.sop(lambda e, c=c: e.activation(
                    self.stat[:, c:c + 8], self.stat[:, c + 8:c + 16], AF.Ln),
                    waits=[(VE, rv)])
                sv = self.sop(lambda e, c=c: e.activation(
                    self.stat[:, c + 8:c + 16], self.stat[:, c:c + 8], AF.Exp, scale=-0.5),
                    waits=[(SE, s_l)])
                qv = qn_t[:, side * 512:(side + 1) * 512]
                q2v = qn2_t[:, side * 512:(side + 1) * 512]
                sct = qsb_t if side == 0 else ksb_t
                self.vop(lambda e, qv=qv, i=psv, s=sct: e.tensor_tensor(qv, i, s[:, :], ALU.mult))
                for hh in range(FH):
                    self.vop(lambda e, hh=hh, c=c, qv=qv, q2v=q2v: e.tensor_scalar(
                        q2v[:, hh * 64:(hh + 1) * 64], qv[:, hh * 64:(hh + 1) * 64],
                        self.stat[:, c + 8 + hh:c + 9 + hh], None, ALU.mult),
                        waits=([(SE, sv)] if hh == 0 else ()))
                ct = rot_t[:, par * 2048 + (2 * side) * 512: par * 2048 + (2 * side) * 512 + 512]
                st_ = rot_t[:, par * 2048 + (2 * side + 1) * 512: par * 2048 + (2 * side + 1) * 512 + 512]
                t1 = r1_t[:, side * 512:(side + 1) * 512]
                t2 = r2_t[:, side * 512:(side + 1) * 512]
                self.vop(lambda e, t1=t1, q2v=q2v, ct=ct: e.tensor_tensor(t1, q2v, ct, ALU.mult),
                         waits=t_rots)
                q23 = q2v.rearrange("p (h two d) -> p h two d", h=FH, two=2)
                st3 = st_.rearrange("p (h two d) -> p h two d", h=FH, two=2)
                t23 = t2.rearrange("p (h two d) -> p h two d", h=FH, two=2)
                self.vop(lambda e, t23=t23, q23=q23, st3=st3: e.tensor_tensor(
                    t23[:, :, 0, :], q23[:, :, 1, :], st3[:, :, 0, :], ALU.mult))
                self.vop(lambda e, t23=t23, q23=q23, st3=st3: e.tensor_tensor(
                    t23[:, :, 1, :], q23[:, :, 0, :], st3[:, :, 1, :], ALU.mult))
                dstq = (qrot if side == 0 else krot)[:, sb * 512:(sb + 1) * 512]
                last_rot = self.vop(lambda e, d=dstq, t1=t1, t2=t2: e.tensor_tensor(
                    d, t1, t2, ALU.add))
            ve_rot[sb] = last_rot
            dst = vxf[:, sb * 584: sb * 584 + 520].rearrange("p (h c) -> p h c", h=FH)[:, :, 0:64]
            src = psums[2].rearrange("p (h c) -> p h c", h=FH)
            ve_done[sb] = self.vop(lambda e, d=dst, s=src: e.tensor_copy(d, s))
        self.barrier()


def emit_f2(self, qrot, krot, qrTz, krT):
    self.mark('F2')
    VE, SE, PE = self.VE, self.SE, self.PE
    with contextlib.ExitStack() as ph:
        self.ps_tr = self.ps("tr", [128, 1024], BF16, ph)
        self.ps_tr2 = self.ps("tr2", [128, 1024], BF16, ph)
        mzq = self.vop(lambda e: e.memset(qrTz[:, :], 0.0))
        ve_ev2, pe_tr2 = {}, {}
        qz = qrTz[:, :].rearrange("p (pb two c) -> p pb two c", pb=4, two=2)
        for sb in range(NSB):
            par = sb % 2
            pst = self.ps_tr if par == 0 else self.ps_tr2
            w = ([(VE, ve_ev2[sb - 2])] if sb - 2 in ve_ev2 else [])
            for pb in range(4):
                self.tr(pst[:, pb * 128:(pb + 1) * 128],
                        qrot[:, sb * 512 + pb * 128: sb * 512 + (pb + 1) * 128], inc=PE,
                        waits=(w if pb == 0 else ()))
            for pb in range(4):
                pe_tr2[sb] = self.tr(pst[:, 512 + pb * 128: 512 + (pb + 1) * 128],
                                     krot[:, sb * 512 + pb * 128: sb * 512 + (pb + 1) * 128],
                                     inc=PE)
            dqe = qz[0:64, :, 0, sb * 128:(sb + 1) * 128]
            dqo = qz[64:128, :, 1, sb * 128:(sb + 1) * 128]
            dk = krT[:, :].rearrange("p (pb c) -> p pb c", pb=4)[:, :, sb * 128:(sb + 1) * 128]
            sq_ = pst[:, 0:512].rearrange("p (k c) -> p k c", k=4)
            sk_ = pst[:, 512:1024].rearrange("p (k c) -> p k c", k=4)
            self.vop(lambda e, d=dqe, s=sq_: e.tensor_copy(d, s[0:64]),
                     waits=[(PE, pe_tr2[sb])] + ([(VE, mzq)] if sb == 0 else []))
            self.vop(lambda e, d=dqo, s=sq_: e.tensor_copy(d, s[64:128]))
            ve_ev2[sb] = self.vop(lambda e, d=dk, s=sk_: e.tensor_copy(d, s))
        self.barrier()


def emit_f3(self, bi, l, qrTz, krT, vxf, oT):
    self.mark(f'F3_b{bi}l{l}')
    VE, SE, PE, LDG, STS, GE = self.VE, self.SE, self.PE, self.LDG, self.STS, self.GE
    with contextlib.ExitStack() as ph:
        ps_bc = self.ps("f3_bc", [128, 1024], F32, ph)
        ptf = self.sb("f3_pt", [128, 8 * 512], BF16, ph)
        opv = self.sb("f3_op", [64, 2 * 512], BF16, ph)
        onrm = self.sb("f3_on", [64, 2 * 512], BF16, ph)
        csl = self.sb("f3_csl", [128, 128], BF16, ph)
        csu = self.sb("f3_csu", [128, 128], BF16, ph)
        rowA = self.sb("f3_rA", [1, 2 * 512], F32, ph)
        rowB = self.sb("f3_rB", [1, 2 * 512], BF16, ph)
        mz = self.vop(lambda e: e.memset(ptf[:, :], 0.0))
        self.vop(lambda e: e.tensor_scalar(csu[:, :], self.m_lt[:, :], 30000.0, -30000.0,
                                           ALU.mult, ALU.add))
        cs_v = self.vop(lambda e: e.tensor_scalar(csl[:, :], self.m_ut[:, :],
                                                  30000.0, -30000.0, ALU.mult, ALU.add))
        norm_fin, row_guard, bc_guard, on_guard, se_ex = {}, {}, {}, {}, {}
        sl_n = [0]
        units = []
        for w_ in range(4):
            for hp in range(4):
                for parity in range(2):
                    units.append((w_, hp, parity))

        def emit_S(w_, hp, parity, j):
            h = 2 * hp + parity
            a0 = max(0, j - 4)
            a1 = min(3, j)
            col0, n = a0 * 128, (a1 - a0 + 1) * 128
            kb_abs = 4 * (w_ - 1) + j
            sn = sl_n[0]
            sl_n[0] += 1
            sl = sn % 2
            pss = self.ps_a if sl == 0 else self.ps_b
            wt = [(VE, cs_v), (VE, mz)] if sn == 0 else []
            if sn - 2 in se_ex:
                wt.append((SE, se_ex[sn - 2]))
            self.mm(pss[:, 0:n],
                    krT[:, hp * S + kb_abs * 128: hp * S + kb_abs * 128 + 128],
                    qrTz[:, h * S + w_ * 512 + col0: h * S + w_ * 512 + col0 + n],
                    start=True, stop=False, waits=wt)
            if j >= 4:
                pem = self.mm(pss[:, 0:128], csu[:, :], self.idn[:, :],
                              start=False, stop=True, inc=PE)
            else:
                off = (a1 - a0) * 128
                pem = self.mm(pss[:, off:off + 128], csl[:, :], self.idn[:, :],
                              start=False, stop=True, inc=PE)
            ex = self.sop(lambda e, o=ptf[:, j * 512 + col0: j * 512 + col0 + n],
                          i=pss, n=n: e.activation(o, i[:, 0:n], AF.Exp, scale=8.0),
                          waits=[(PE, pem)])
            se_ex[sn] = ex
            return ex

        def emit_PV(it, w_, hp, parity, j, first, last, ex):
            h = 2 * hp + parity
            bank = it % 4
            kb_abs = 4 * (w_ - 1) + j
            wv = [(SE, ex)]
            if first and it - 4 in norm_fin:
                wv += norm_fin[it - 4]
            return self.mm(self.ps_big[:, bank * 512:(bank + 1) * 512],
                           vxf[:, kb_abs * 584 + h * 65: kb_abs * 584 + h * 65 + 128],
                           ptf[:, j * 512:(j + 1) * 512],
                           start=first, stop=last, inc=PE, waits=wv)

        def emit_norm(it, w_, hp, parity, pv):
            bank = it % 4
            rslot = it % 2
            ov = opv[:, rslot * 512:(rslot + 1) * 512]
            cp = self.vop(lambda e, ov=ov, bank=bank: e.tensor_copy(
                ov, self.ps_big[0:64, bank * 512:(bank + 1) * 512]), waits=[(PE, pv)])
            w = [(PE, pv)] + ([row_guard[rslot]] if rslot in row_guard else [])
            s1 = self.sop(lambda e, bank=bank, rslot=rslot: e.activation(
                rowA[0:1, rslot * 512:(rslot + 1) * 512],
                self.ps_big[64:65, bank * 512:(bank + 1) * 512], AF.Ln), waits=w)
            s2 = self.sop(lambda e, rslot=rslot: e.activation(
                rowB[0:1, rslot * 512:(rslot + 1) * 512],
                rowA[0:1, rslot * 512:(rslot + 1) * 512], AF.Exp, scale=-1.0))
            norm_fin[it] = [(VE, cp), (SE, s2)]
            wb = [(SE, s2)] + ([bc_guard[rslot]] if rslot in bc_guard else [])
            bc = self.mm(ps_bc[0:64, rslot * 512:(rslot + 1) * 512], self.ones1x64[:, :],
                         rowB[0:1, rslot * 512:(rslot + 1) * 512], start=True, stop=True,
                         inc=PE, waits=wb)
            row_guard[rslot] = (PE, bc)
            if parity == 0:
                tt = self.vop(lambda e, hp=hp, w_=w_, ov=ov, rslot=rslot: e.tensor_tensor(
                    oT[0:64, hp * S + w_ * 512: hp * S + (w_ + 1) * 512],
                    ov, ps_bc[0:64, rslot * 512:(rslot + 1) * 512], ALU.mult),
                    waits=[(PE, bc)])
                bc_guard[rslot] = (VE, tt)
            else:
                osl = (it // 2) % 2
                onv = onrm[:, osl * 512:(osl + 1) * 512]
                w2 = [(PE, bc)] + ([on_guard[osl]] if osl in on_guard else [])
                tt = self.vop(lambda e, onv=onv, ov=ov, rslot=rslot: e.tensor_tensor(
                    onv, ov, ps_bc[0:64, rslot * 512:(rslot + 1) * 512], ALU.mult), waits=w2)
                bc_guard[rslot] = (VE, tt)
                sv = self.sop(lambda e, hp=hp, w_=w_, onv=onv: e.activation(
                    oT[64:128, hp * S + w_ * 512: hp * S + (w_ + 1) * 512],
                    onv, AF.Copy), waits=[(VE, tt)])
                on_guard[osl] = (SE, sv)

        flat = []
        for ui, (w_, hp, parity) in enumerate(units):
            jlist = list(range(4, 8)) if w_ == 0 else list(range(8))
            for jn, j in enumerate(jlist):
                flat.append((ui, w_, hp, parity, j, jn == 0, jn == len(jlist) - 1))
        pend = None
        for rec in flat:
            ex = emit_S(rec[1], rec[2], rec[3], rec[4])
            if pend is not None:
                pv = emit_PV(pend[0], pend[1], pend[2], pend[3], pend[4], pend[5], pend[6],
                             pend[7])
                if pend[6]:
                    emit_norm(pend[0], pend[1], pend[2], pend[3], pv)
            pend = rec + (ex,)
        pv = emit_PV(pend[0], pend[1], pend[2], pend[3], pend[4], pend[5], pend[6], pend[7])
        emit_norm(pend[0], pend[1], pend[2], pend[3], pv)
        self.barrier()


def emit_f4(self, bi, l, oT):
    self.mark(f'F4_b{bi}l{l}')
    VE, SE, PE, LDG, STS, GE = self.VE, self.SE, self.PE, self.LDG, self.STS, self.GE
    with contextlib.ExitStack() as ph:
        self.ps_tr = self.ps("tr", [128, 1024], BF16, ph)
        self.ps_tr2 = self.ps("tr2", [128, 1024], BF16, ph)
        wo = self.sb("f4_wo", [128, 4 * D], BF16, ph)
        self.lnjunk = self.sb("f4_junk", [128, 1024], F32, ph)
        ut = self.sb("f4_u", [128, 4 * D], F32, ph)
        xft = self.sb("f4_xf", [128, 4 * D], F32, ph)
        shv4 = self.sb("f4_sh", [128, 4 * 512], F32, ph)
        h2nb = self.sb("f4_hn", [128, 2 * D], BF16, ph)
        lngb = self.sb("f4_lg", [128, D], BF16, ph)
        lnbb = self.sb("f4_lb", [128, D], BF16, ph)
        row2 = self.sb("f4_r2", [128, 4 * 1024], F32, ph)
        row3 = self.sb("f4_r3", [128, 2 * 1024], F32, ph)
        st4 = self.sb("f4_st", [128, 64], F32, ph)
        t_wo = self.dma_w3(wo[:, :], self.FWO[l], 4)
        t_lg = self.dma_g(lngb[:, :], self.FLG[l][None, :].partition_broadcast(128))
        t_lb = self.dma_g(lnbb[:, :], self.FLB[l][None, :].partition_broadcast(128))
        ve_u, st_u, se_ap, ve_hn, pe_tr, ve_ev, sh_pem, cp_sh = {}, {}, {}, {}, {}, {}, {}, {}
        for sb in range(NSB):
            par = sb % 2
            p4 = sb % 4
            wx = [(VE, ve_u[sb - 4])] if sb - 4 in ve_u else []
            ldx = self.dma_g(xft[:, p4 * D:(p4 + 1) * D],
                             self.f0_s[:, sb * D:(sb + 1) * D], waits=wx)
            psy = self.ps_big[:, par * 1024:(par + 1) * 1024]
            w = [t_wo] + ([(VE, ve_u[sb - 2])] if sb - 2 in ve_u else [])
            pem = None
            for pb in range(4):
                for nn in range(2):
                    pem = self.mm(psy[:, nn * 512:(nn + 1) * 512],
                                  oT[:, pb * S + sb * 128: pb * S + sb * 128 + 128],
                                  wo[:, pb * D + nn * 512: pb * D + (nn + 1) * 512],
                                  start=(pb == 0), stop=(pb == 3),
                                  inc=(PE if pb == 3 and nn == 1 else None),
                                  waits=(w if pb == 0 and nn == 0 else ()))
            uv = ut[:, p4 * D:(p4 + 1) * D]
            w = [(PE, pem), ldx]
            if sb - 4 in st_u:
                w += [st_u[sb - 4], (SE, se_ap[sb - 4])]
            ve_u[sb] = self.vop(lambda e, uv=uv, xf=xft[:, p4 * D:(p4 + 1) * D], psy=psy:
                                e.tensor_tensor(uv, psy, xf, ALU.add), waits=w)
            st_u[sb] = self.dma_s(self.u_s[:, sb * D:(sb + 1) * D], uv, waits=[(VE, ve_u[sb])])
            psv = self.ps_a if par == 0 else self.ps_b
            w = [(VE, ve_u[sb])] + ([(VE, cp_sh[sb - 2])] if sb - 2 in cp_sh else [])
            m1 = self.mm(psv[:, :], self.shiftT[:, :], uv[:, 512:1024],
                         start=True, stop=(sb == 0), inc=(PE if sb == 0 else None),
                         waits=w)
            if sb > 0:
                pu = ut[:, ((sb - 1) % 4) * D:((sb - 1) % 4) * D + D]
                m1 = self.mm(psv[:, :], self.e127T[:, :], pu[:, 512:1024],
                             start=False, stop=True, inc=PE)
            sh_pem[sb] = m1
            shp = shv4[:, p4 * 512:(p4 + 1) * 512]
            cp_sh[sb] = self.vop(lambda e, shp=shp, psv=psv: e.tensor_copy(shp, psv[:, :]),
                                 waits=[(PE, m1)])
            c = p4 * 16
            st = st4
            junk = self.lnjunk
            w = [(VE, cp_sh[sb])]
            self.sop(lambda e, uv=uv, c=c, j=junk: e.activation(
                j[:, 0:512], uv[:, 0:512], AF.Identity,
                accum_out=st[:, c + 6:c + 7]), waits=w)
            self.sop(lambda e, shp=shp, c=c, j=junk: e.activation(
                j[:, 0:512], shp, AF.Identity, accum_out=st[:, c + 8:c + 9]))
            self.sop(lambda e, uv=uv, c=c, j=junk: e.activation(
                j[:, 0:512], uv[:, 0:512], AF.Square,
                accum_out=st[:, c + 7:c + 8]))
            s2 = self.sop(lambda e, shp=shp, c=c, j=junk: e.activation(
                j[:, 0:512], shp, AF.Square, accum_out=st[:, c + 9:c + 10]))
            v1 = self.vop(lambda e, c=c: e.tensor_tensor(
                st[:, c + 10:c + 11], st[:, c + 6:c + 7], st[:, c + 8:c + 9], ALU.add),
                waits=[(SE, s2)])
            v1 = self.vop(lambda e, c=c: e.tensor_tensor(
                st[:, c + 11:c + 12], st[:, c + 7:c + 8], st[:, c + 9:c + 10], ALU.add),
                waits=[(VE, v1)])
            v1 = self.vop(lambda e, c=c: e.tensor_scalar(st[:, c:c + 1], st[:, c + 10:c + 11],
                                                         1.0 / 1024, None, ALU.mult),
                          waits=[(VE, v1)])
            v1 = self.vop(lambda e, c=c: e.tensor_tensor(st[:, c + 2:c + 3], st[:, c:c + 1],
                                                         st[:, c:c + 1], ALU.mult),
                          waits=[(VE, v1)])
            v1 = self.vop(lambda e, c=c: e.scalar_tensor_tensor(
                st[:, c + 1:c + 2], st[:, c + 11:c + 12], 1.0 / 1024, st[:, c + 2:c + 3],
                ALU.mult, ALU.subtract), waits=[(VE, v1)])
            s3 = self.sop(lambda e, c=c: e.activation(st[:, c + 3:c + 4], st[:, c + 1:c + 2],
                                                      AF.Ln, bias=self.cst[:, 0:1]),
                          waits=[(VE, v1)])
            s2 = self.sop(lambda e, c=c: e.activation(st[:, c + 4:c + 5], st[:, c + 3:c + 4],
                                                      AF.Exp, scale=-0.5), waits=[(SE, s3)])
            tv2 = self.vop(lambda e, c=c: e.scalar_tensor_tensor(
                st[:, c + 5:c + 6], st[:, c:c + 1], -1.0, st[:, c + 4:c + 5],
                ALU.mult, ALU.mult), waits=[(SE, s2)])
            hv = h2nb[:, par * D:(par + 1) * D]
            r2v = row2[:, p4 * D:(p4 + 1) * D]
            r3v = row3[:, par * D:(par + 1) * D]
            w = [(VE, tv2), t_lg, t_lb]
            if sb - 4 in ve_hn:
                w.append((VE, ve_hn[sb - 4]))
            self.sop(lambda e, uv=uv, c=c, r2v=r2v: e.activation(
                r2v[:, 0:512], uv[:, 0:512], AF.Identity,
                scale=st[:, c + 4:c + 5], bias=st[:, c + 5:c + 6]), waits=w)
            se_ap[sb] = self.sop(lambda e, shp=shp, c=c, r2v=r2v: e.activation(
                r2v[:, 512:1024], shp, AF.Identity,
                scale=st[:, c + 4:c + 5], bias=st[:, c + 5:c + 6]))
            self.vop(lambda e, r2v=r2v, r3v=r3v: e.tensor_tensor(
                r3v, r2v, lngb[:, :], ALU.mult), waits=[(SE, se_ap[sb])])
            ve_hn[sb] = self.vop(lambda e, hv=hv, r3v=r3v: e.tensor_tensor(
                hv, r3v, lnbb[:, :], ALU.add))
            pst = self.ps_tr if par == 0 else self.ps_tr2
            w = [(VE, ve_hn[sb])] + ([(VE, ve_ev[sb - 2])] if sb - 2 in ve_ev else [])
            for db in range(NDB):
                pe_tr[sb] = self.tr(pst[:, db * 128:(db + 1) * 128],
                                    hv[:, db * 128:(db + 1) * 128], inc=PE,
                                    waits=(w if db == 0 else ()))
            dst = self.tT[:, :].rearrange("p (db c) -> p db c", db=NDB)[:, :, 1 + sb * 128: 129 + sb * 128]
            src = pst[:, :].rearrange("p (k c) -> p k c", k=NDB)
            ve_ev[sb] = self.vop(lambda e, d=dst, s=src: e.tensor_copy(d, s),
                                 waits=[(PE, pe_tr[sb])])
        self.barrier()


def emit_f56(self, bi, l):
    self.mark(f'F5_b{bi}l{l}')
    VE, SE, PE, LDG, STS, GE = self.VE, self.SE, self.PE, self.LDG, self.STS, self.GE
    with contextlib.ExitStack() as phA:
        hT = self.sb("f5_hT", [128, NJ * S], BF16, phA)
        w2b = self.sb("f5_w2", [128, NJ * D], BF16, phA)
        b2b = self.sb("f5_b2", [128, D], F32, phA)
        ph = contextlib.ExitStack()
        ph.__enter__()
        w1a = self.sb("f5_wa", [128, 2 * NDB * 128], BF16, ph)
        w1g = self.sb("f5_wg", [128, 2 * NDB * 128], BF16, ph)
        b1a = self.sb("f5_ba", [128, NJ], F32, ph)
        b1g = self.sb("f5_bg", [128, NJ], F32, ph)
        sga = self.sb("f5_sg", [128, 2 * 512], F32, ph)
        saa = self.sb("f5_sa", [128, 2 * 512], F32, ph)

        t_ba = self.dma_g(b1a[:, 0:21], self.FB1[l][0:2688].rearrange("(j p) -> p j", p=128),
                          slow=True)
        t_ba = self.dma_g(b1a[0:42, 21:22], self.FB1[l][2688:2730][:, None], slow=True,
                          waits=[t_ba])
        t_bg = self.dma_g(b1g[:, 0:21], self.FB1[l][2730:5418].rearrange("(j p) -> p j", p=128),
                          slow=True)
        t_bg = self.dma_g(b1g[0:42, 21:22], self.FB1[l][5418:5460][:, None], slow=True,
                          waits=[t_bg])
        t_b2 = self.dma_g(b2b[:, :], self.FB2[l][None, :].partition_broadcast(128))
        t_w2 = self.dma_g(w2b[:, 0:21 * D], self.fw2_c[l][:, 0:21 * D])
        t_w2 = self.dma_g(w2b[0:42, 21 * D:22 * D], self.fw2_c[l][0:42, 21 * D:22 * D],
                          waits=[t_w2])
        w1v = self.fw1_c[l].rearrange("p (db o) -> p db o", db=NDB)
        pe_j, ve_h, t_w1 = {}, {}, {}

        def load_w1(j):
            par = j % 2
            mj = 128 if j < 21 else 42
            wt = [(PE, pe_j[j - 2])] if j - 2 in pe_j else []
            dsta = w1a[:, par * 1024:par * 1024 + 1024].rearrange("p (db o) -> p db o", db=NDB)
            dstg = w1g[:, par * 1024:par * 1024 + 1024].rearrange("p (db o) -> p db o", db=NDB)
            ta = self.dma_g(dsta[:, :, 0:mj], w1v[:, :, 128 * j:128 * j + mj], waits=wt)
            return self.dma_g(dstg[:, :, 0:mj], w1v[:, :, 2730 + 128 * j:2730 + 128 * j + mj],
                              waits=[ta])

        t_w1[0] = load_w1(0)
        for j in range(NJ):
            par = j % 2
            mj = 128 if j < 21 else 42
            if j + 1 < NJ:
                t_w1[j + 1] = load_w1(j + 1)
            for ch in range(4):
                ba = ch % 2
                psa = self.ps_big[0:128, ba * 512:(ba + 1) * 512]
                psg = self.ps_big[0:128, (2 + ba) * 512:(3 + ba) * 512]
                w = [t_w1[j], t_ba, t_bg]
                key = j * 4 + ch - 2
                if key in ve_h:
                    w.append((VE, ve_h[key]))
                for db in range(NDB):
                    self.mm(psa[0:mj, :], w1a[:, par * 1024 + db * 128: par * 1024 + db * 128 + mj],
                            self.tTv(db, ch * 512, 512), start=(db == 0), stop=(db == NDB - 1),
                            waits=(w if db == 0 else ()))
                pem = None
                for db in range(NDB):
                    pem = self.mm(psg[0:mj, :],
                                  w1g[:, par * 1024 + db * 128: par * 1024 + db * 128 + mj],
                                  self.tTv(db, ch * 512, 512), start=(db == 0),
                                  stop=(db == NDB - 1), inc=(PE if db == NDB - 1 else None))
                pe_j[j] = pem
                sgv = sga[0:mj, ba * 512:ba * 512 + 512]
                sav = saa[0:mj, ba * 512:ba * 512 + 512]
                sg_ = self.sop(lambda e, o=sgv, i=psg[0:mj, :], j=j, mj=mj: e.activation(
                    o, i, AF.Gelu, bias=b1g[0:mj, j:j + 1]), waits=[(PE, pem)])
                self.vop(lambda e, o=sav, i=psa[0:mj, :], j=j, mj=mj: e.tensor_scalar(
                    o, i, b1a[0:mj, j:j + 1], None, ALU.add), waits=[(PE, pem)])
                ve_h[j * 4 + ch] = self.vop(
                    lambda e, o=hT[0:mj, j * S + ch * 512: j * S + ch * 512 + 512],
                    a=sav, g=sgv: e.tensor_tensor(o, a, g, ALU.mult), waits=[(SE, sg_)])
        self.barrier(producers=("vector",), consumers=("tensor",))
        self.barrier()
        ph.__exit__(None, None, None)

        self.mark(f'F6_b{bi}l{l}')
        ut2 = self.sb("f6_u", [128, 2 * D], F32, phA)
        xsc = self.sb("f6_x", [128, 2 * D], F32, phA)
        xout = self.sb("f6_o", [128, 2 * D], F32, phA)
        ve_x, st_x = {}, {}
        for sb in range(NSB):
            par = sb % 2
            w = [(VE, ve_x[sb - 2])] if sb - 2 in ve_x else []
            tld = self.dma_g(ut2[:, par * D:(par + 1) * D], self.u_s[:, sb * D:(sb + 1) * D],
                             waits=w)
            psy = self.ps_big[:, par * 1024:(par + 1) * 1024]
            w = [t_w2] + ([(VE, ve_x[sb - 2])] if sb - 2 in ve_x else [])
            pem = None
            for j in range(NJ):
                mj = 128 if j < 21 else 42
                for nn in range(2):
                    pem = self.mm(psy[:, nn * 512:(nn + 1) * 512],
                                  hT[0:mj, j * S + sb * 128: j * S + sb * 128 + 128],
                                  w2b[0:mj, j * D + nn * 512: j * D + (nn + 1) * 512],
                                  start=(j == 0), stop=(j == NJ - 1),
                                  inc=(PE if j == NJ - 1 and nn == 1 else None),
                                  waits=(w if j == 0 and nn == 0 else ()))
            w = [(PE, pem), t_b2, tld]
            if sb - 2 in st_x:
                w.append(st_x[sb - 2])
            self.vop(lambda e, par=par, psy=psy: e.tensor_tensor(
                xsc[:, par * D:(par + 1) * D], psy, b2b[:, :], ALU.add),
                waits=w)
            ve_x[sb] = self.vop(lambda e, par=par: e.tensor_tensor(
                xout[:, par * D:(par + 1) * D], xsc[:, par * D:(par + 1) * D],
                ut2[:, par * D:(par + 1) * D], ALU.add))
            st_x[sb] = self.dma_s(self.f0_s[:, sb * D:(sb + 1) * D],
                                  xout[:, par * D:(par + 1) * D], waits=[(VE, ve_x[sb])])
        self.barrier()


def emit_final(self, bi):
    self.mark(f'FIN_b{bi}')
    VE, SE, PE, LDG, STS = self.VE, self.SE, self.PE, self.LDG, self.STS
    with contextlib.ExitStack() as ph:
        z1t = self.sb("fin_z1", [128, 4 * D], F32, ph)
        self.lnjunk = self.sb("fin_junk", [128, 1024], F32, ph)
        xft = self.sb("fin_xf", [128, 4 * D], F32, ph)
        ot = self.sb("fin_o", [128, 4 * D], F32, ph)
        osc = self.sb("fin_s", [128, 4 * D], F32, ph)
        st_o, ve_o, se_a = {}, {}, {}
        for sb in range(NSB):
            p4 = sb % 4
            w = [(VE, ve_o[sb - 4])] if sb - 4 in ve_o else []
            tld = self.dma_g(z1t[:, p4 * D:(p4 + 1) * D],
                             self.z1_s[:, sb * D:(sb + 1) * D], waits=w)
            w = [(SE, se_a[sb - 4])] if sb - 4 in se_a else []
            tlx = self.dma_g(xft[:, p4 * D:(p4 + 1) * D],
                             self.f0_s[:, sb * D:(sb + 1) * D], waits=w)
            c = p4 * 16
            xfv = xft[:, p4 * D:(p4 + 1) * D]
            emit_ln_stats(self, c, xfv, pe_wait=tlx)
            w = [(VE, self.VE.v)] + ([st_o[sb - 4]] if sb - 4 in st_o else [])
            se_a[sb] = self.sop(lambda e, o=osc[:, p4 * D:(p4 + 1) * D], i=xfv, c=c:
                                e.activation(o, i, AF.Identity, scale=self.stat[:, c + 4:c + 5],
                                             bias=self.stat[:, c + 5:c + 6]), waits=w)
            ve_o[sb] = self.vop(lambda e, p4=p4: e.tensor_tensor(
                ot[:, p4 * D:(p4 + 1) * D], osc[:, p4 * D:(p4 + 1) * D],
                z1t[:, p4 * D:(p4 + 1) * D], ALU.add),
                waits=[(SE, se_a[sb]), tld])
            dst = self.Y[bi].rearrange("(sb p) d -> p sb d", p=128)[:, sb, :]
            st_o[sb] = self.dma_s(dst, ot[:, p4 * D:(p4 + 1) * D],
                                  waits=[(VE, ve_o[sb])])
        self.barrier()


"""Megakernel builder for mk_sandwich_d1 (16x2048x1024 sandwich block).

Layouts:
 - normal: [128, NSB*1024], col = sb*1024 + d (token = sb*128 + p)
 - ^T:     [128, NDB*2049], col = db*2049 + 1 + s (d = db*128 + p); col db*2049
           is a zero column so a col-offset of -1 implements token-shift.
Outer attention: S^T (k on partitions), no max subtraction, exp PSUM->bf16 P^T,
softmax denominators via ones column (idx 64) in V_ext, normalization after PV
via reciprocal + split-bf16 K=1 PE broadcast. All matmuls bf16 / fp32 PSUM.
"""
import contextlib
import numpy as np
import ml_dtypes
import concourse.bass as bass
import concourse.mybir as mybir
from concourse.bass_utils import run_bass_kernel_spmd

F32 = mybir.dt.float32
BF16 = mybir.dt.bfloat16
AF = mybir.ActivationFunctionType
ALU = mybir.AluOpType
AX = mybir.AxisListType

S, D, H, HD = 2048, 1024, 16, 64
NSB, NDB, HP = 16, 8, 8
TCOL = NDB * 2049
FH, FHD, FIN, WIN = 8, 64, 512, 512
HID, HID2 = 2730, 5460
NJ = 22
SCALE = 1.0 / 8.0
QK_SCALE = 8.0
ENGINES = ("tensor", "vector", "scalar", "gpsimd", "sync")


def host_constants():
    bf = ml_dtypes.bfloat16
    c = {}
    c["IDN"] = np.eye(128, dtype=np.float32).astype(bf)
    c["ONES64"] = np.ones((128, 1), np.float32).astype(bf)
    c["ONES1x64"] = np.ones((1, 64), np.float32).astype(bf)
    sel0 = np.zeros((1, 128), np.float32); sel0[0, :64] = 1.0
    sel1 = np.zeros((1, 128), np.float32); sel1[0, 64:] = 1.0
    c["SEL0"], c["SEL1"] = sel0.astype(bf), sel1.astype(bf)
    r = np.arange(128)
    c["M_UT"] = (r[:, None] <= r[None, :]).astype(np.float32).astype(bf)
    c["M_LT"] = (r[:, None] >= r[None, :]).astype(np.float32).astype(bf)
    c["BD"] = ((r[:, None] < 64) == (r[None, :] < 64)).astype(np.float32).astype(bf)
    c["SHIFTT"] = (r[None, :] == r[:, None] + 1).astype(np.float32)
    e127 = np.zeros((128, 128), np.float32); e127[127, 0] = 1.0
    c["E127T"] = e127
    n = 2048
    half = np.arange(0, FHD, 2, dtype=np.float64)
    inv_freq = 1.0 / 10000.0 ** (half / FHD)
    t = np.arange(n, dtype=np.float64)
    freqs = np.concatenate([t[:, None] * inv_freq[None, :]] * 2, axis=-1)
    base = (half + 0.4 * FHD) / (1.4 * FHD)
    power = (t - n // 2) / float(WIN // 2)
    scale = np.concatenate([base[None, :] ** power[:, None]] * 2, axis=-1)
    cosv, sinv = np.cos(freqs), np.sin(freqs)
    sin_signed = sinv.copy()
    sin_signed[:, :32] *= -1.0
    c["RQC"] = np.tile((cosv * scale).astype(np.float32), (1, FH)).astype(bf)
    c["RQS"] = np.tile((sin_signed * scale).astype(np.float32), (1, FH)).astype(bf)
    c["RKC"] = np.tile((cosv / scale).astype(np.float32), (1, FH)).astype(bf)
    c["RKS"] = np.tile((sin_signed / scale).astype(np.float32), (1, FH)).astype(bf)
    return c


class Sem:
    def __init__(self, h):
        self.h, self.v = h, 0

    def inc(self, n=1):
        self.v += n
        return self.v


class MK:
    def __init__(self, cfg=None):
        self.cfg = cfg or {}
        self.nc = bass.Bass("TRN2", target_bir_lowering=False)
        self.ops = {e: [] for e in ENGINES}
        self.tap_t = {}

    def emit(self, eng, fn, waits=(), inc=None, inc_n=1):
        tgt = inc.inc(inc_n) if inc is not None else None
        wl = [(s.h, v) for s, v in waits if v > 0]

        def run(e, fn=fn, wl=wl, inc=inc, inc_n=inc_n):
            for h, v in wl:
                e.wait_ge(h, v)
            inst = fn(e)
            if inc is not None:
                inst.then_inc(inc.h, inc_n)
        self.ops[eng].append(run)
        return tgt

    def wait(self, eng, *sems_vals):
        wl = [(s.h, v) for s, v in sems_vals if v > 0]
        if wl:
            self.ops[eng].append(lambda e, wl=wl: [e.wait_ge(h, v) for h, v in wl])

    def _dma(self, eng, ring, idx_attr, dst, src, waits, slow=False):
        i = getattr(self, idx_attr)
        setattr(self, idx_attr, i + 1)
        s = ring[i % len(ring)]
        pre = ([(s, s.v)] if s.v > 0 else []) + [t for t in waits if t[1] > 0]
        tgt = self.emit(eng, lambda e, d=dst, sr=src, sl=slow: e.dma_start(
            out=d, in_=sr, allow_slow_non_contiguous=sl), waits=pre, inc=s, inc_n=16)
        return (s, tgt)

    def dma_g(self, dst, src, waits=(), slow=False):
        return self._dma("gpsimd", self.LDGr, "_ldg_i", dst, src, waits, slow)

    def dma_s(self, dst, src, waits=()):
        return self._dma("sync", self.STSr, "_sts_i", dst, src, waits)

    def dma_w3(self, dst, w, nk, waits=()):
        return self.dma_g(dst.rearrange("p (db o) -> p db o", db=nk),
                          w.rearrange("(db p) o -> p db o", p=128), waits=waits)

    def mark(self, name):
        if not hasattr(self, "_marks"):
            self._marks = []
            self._marks_e = []
        self._marks.append((name, getattr(self, "_mmn", 0)))
        self._marks_e.append((name, {e: len(self.ops[e]) for e in ENGINES}))

    def barrier(self, producers=ENGINES, consumers=ENGINES):
        tg = []
        for e in producers:
            if e == "sync":
                tg += [(s, s.v) for s in self.STSr]
            elif e == "gpsimd":
                tg.append((self.GE, self.GE.v))
                tg += [(s, s.v) for s in self.LDGr]
            else:
                tg.append((self.SEMS[e], self.SEMS[e].v))
        for e in consumers:
            self.wait(e, *tg)

    def _count_mm(self):
        self._mmn = getattr(self, "_mmn", 0) + 1

    def tr(self, out, in_, inc=None, waits=()):
        self._count_mm()
        return self.emit(
            "tensor",
            lambda e, o=out, i=in_: e.transpose(o, i, self.idn[:, :]),
            waits=waits, inc=inc)

    def mm(self, out, lhsT, rhs, start, stop, inc=None, tile_position=None, waits=()):
        self._count_mm()
        return self.emit(
            "tensor",
            lambda e, o=out, l=lhsT, r=rhs, st=start, sp=stop, tp=tile_position:
                e.matmul(o, l, r, start=st, stop=sp, tile_position=tp, skip_group_check=True),
            waits=waits, inc=inc)

    def vop(self, fn, waits=(), inc=None):
        sem = inc if inc is not None else self.VE
        w = ([(self.VE, self.VE.v)] if self.VE.v > 0 else []) + list(waits)
        return self.emit("vector", fn, waits=w, inc=sem)

    def sop(self, fn, waits=(), inc=None):
        sem = inc if inc is not None else self.SE
        w = ([(self.SE, self.SE.v)] if self.SE.v > 0 else []) + list(waits)
        return self.emit("scalar", fn, waits=w, inc=sem)

    def gop(self, fn, waits=(), inc=None):
        sem = inc if inc is not None else self.GE
        w = ([(self.GE, self.GE.v)] if self.GE.v > 0 else []) + list(waits)
        return self.emit("gpsimd", fn, waits=w, inc=sem)

    def tap(self, name, sbuf_ap, waits=()):
        if name in self.tap_t:
            self.dma_s(self.tap_t[name], sbuf_ap, waits=waits)

    def build(self):
        nc = self.nc
        cfg = self.cfg
        self.batches = cfg.get("batches", [0, 1])

        def din(name, shape, dt=F32):
            return nc.dram_tensor(name, list(shape), dt, kind="ExternalInput").ap()

        self.X = din("x", [2, S, D])
        self.Wq, self.Wk, self.Wv, self.Wo = (din("wq", [D, D]), din("wk", [D, D]),
                                              din("wv", [D, D]), din("wo", [D, D]))
        self.Wqp, self.Wkp = din("wqp", [D, D]), din("wkp", [D, D])
        self.LRAW = din("lraw", [1, H])
        self.FWQKV = din("fwqkv", [2, D, 3 * FIN])
        self.FQS, self.FKS = din("fqs", [2, FHD]), din("fks", [2, FHD])
        self.FWO = din("fwo", [2, FIN, D])
        self.FLG, self.FLB = din("flg", [2, D]), din("flb", [2, D])
        self.FW1, self.FB1 = din("fw1", [2, D, HID2]), din("fb1", [2, HID2])
        self.FW2, self.FB2 = din("fw2", [2, HID, D]), din("fb2", [2, D])
        for cname, cshape in (("IDN", [128, 128]), ("ONES64", [128, 1]), ("ONES1x64", [1, 64]),
                              ("SEL0", [1, 128]), ("SEL1", [1, 128]),
                              ("M_UT", [128, 128]), ("M_LT", [128, 128]),
                              ("BD", [128, 128]),
                              ("RQC", [S, 512]), ("RQS", [S, 512]),
                              ("RKC", [S, 512]), ("RKS", [S, 512])):
            setattr(self, "c" + cname, din(cname, cshape, BF16))
        self.cSHIFTT = din("SHIFTT", [128, 128], F32)
        self.cE127T = din("E127T", [128, 128], F32)
        self.Y = nc.dram_tensor("y", [2, S, D], F32, kind="ExternalOutput").ap()
        for tname, tshape, tdt in cfg.get("taps", []):
            self.tap_t[tname] = nc.dram_tensor("tap_" + tname, list(tshape), tdt,
                                               kind="ExternalOutput").ap()

        def scr(name, shape, dt):
            kind = "ExternalOutput" if name in cfg.get("dbg_out", ()) else "Internal"
            return nc.dram_tensor(name, list(shape), dt, kind=kind).ap()

        self.fw1_c = scr("fw1_c", [2, 128, NDB * HID2], BF16)
        self.fw2_c = scr("fw2_c", [2, 128, NJ * D], BF16)
        self.lam_d = scr("lam_d", [1, H], F32)
        self.qd_s = scr("qd_s", [128, HP * S], BF16)
        self.qc_s = scr("qc_s", [128, HP * S], BF16)
        self.qs_s = scr("qs_s", [128, HP * S], BF16)
        self.kd_s = scr("kd_s", [128, HP * S], BF16)
        self.kc_s = scr("kc_s", [128, HP * S], BF16)
        self.ks_s = scr("ks_s", [128, HP * S], BF16)
        self.vx_s = scr("vx_s", [128, NSB * H * 65], BF16)
        self.z1_s = scr("z1_s", [128, NSB * D], F32)
        self.f0_s = scr("f0_s", [128, NSB * D], F32)
        self.rbc_d = scr("rbc_d", [1, 4096], F32)
        self.u_s = scr("u_s", [128, NSB * D], F32)

        es = contextlib.ExitStack()
        self.es = es
        with es:
            self._nm = [0]
            def _sb(name, shape, dt=F32, stack=es):
                self._nm[0] += 1
                return stack.enter_context(
                    nc.sbuf_tensor(f"{name}_{self._nm[0]}", list(shape), dt))
            self.sb = _sb
            def _ps(name, shape, dt=F32, stack=es):
                self._nm[0] += 1
                return stack.enter_context(
                    nc.psum_tensor(f"{name}_{self._nm[0]}", list(shape), dt))
            self.ps = _ps
            self.ps_big = es.enter_context(nc.psum_tensor("ps_big", [128, 2048], F32))
            self.LDGr = [Sem(es.enter_context(nc.semaphore(f"ldg{i}"))) for i in range(8)]
            self.STSr = [Sem(es.enter_context(nc.semaphore(f"sts{i}"))) for i in range(8)]
            self._ldg_i = 0
            self._sts_i = 0
            self.LDG = self.LDGr[0]
            self.STS = self.STSr[0]
            self.PE = Sem(es.enter_context(nc.semaphore("pe")))
            self.VE = Sem(es.enter_context(nc.semaphore("ve")))
            self.SE = Sem(es.enter_context(nc.semaphore("se")))
            self.GE = Sem(es.enter_context(nc.semaphore("ge")))
            self.SEMS = dict(tensor=self.PE, vector=self.VE, scalar=self.SE,
                             gpsimd=self.GE)

            self.alloc_persistent()
            self.prologue()
            for bi in self.batches:
                self.emit_attn(bi)
                if not cfg.get("attn_only"):
                    for l in (0, 1):
                        self.emit_ffn(bi, l)
                    self.emit_final(bi)
            self.barrier(producers=ENGINES, consumers=("sync",))
            self.wait("sync", *[(s, s.v) for s in self.STSr])

            block = es.enter_context(nc.Block())
            for eng in ENGINES:
                def _replay(e, eng=eng):
                    ops = self.ops[eng]
                    bounds = [(nm, d[eng]) for nm, d in getattr(self, "_marks_e", [])]
                    idx = 0
                    for k, (nm, start) in enumerate(bounds):
                        while idx < start:
                            ops[idx](e)
                            idx += 1
                        end = bounds[k + 1][1] if k + 1 < len(bounds) else len(ops)
                        with nc.named_scope(nm):
                            while idx < end:
                                ops[idx](e)
                                idx += 1
                    while idx < len(ops):
                        ops[idx](e)
                        idx += 1
                getattr(block, eng)(_replay)
        return nc

    def alloc_persistent(self):
        sb = self.sb
        self.idn = sb("idn", [128, 128], BF16)
        self.ones64 = sb("ones64", [128, 1], BF16)
        self.ones1x64 = sb("ones1x64", [1, 64], BF16)
        self.sel0 = sb("sel0", [1, 128], BF16)
        self.sel1 = sb("sel1", [1, 128], BF16)
        self.m_ut = sb("m_ut", [128, 128], BF16)
        self.m_lt = sb("m_lt", [128, 128], BF16)
        self.bd = sb("bd", [128, 128], BF16)
        self.shiftT = sb("shiftT", [128, 128], F32)
        self.e127T = sb("e127T", [128, 128], F32)
        self.lamcol = sb("lamcol", [128, 8])
        self.lamq = sb("lamq", [1, H])
        self.cst = sb("cst", [128, 4])
        self.lnjunk = None
        self.stat = sb("stat", [128, 96])
        self.row = None
        self.rowb = None
        self.tT = None

    def tTv(self, db, s0, n):
        return self.tT[:, db * 2049 + 1 + s0: db * 2049 + 1 + s0 + n]

    def tTs(self, db, s0, n):
        return self.tT[:, db * 2049 + s0: db * 2049 + s0 + n]

    def prologue(self):
        sb, nc = self.sb, self.nc
        import math as _m
        self.vop(lambda e: e.memset(self.cst[:, 0:1], 1e-5))
        self.vop(lambda e: e.memset(self.cst[:, 1:2], 1.0))
        self.vop(lambda e: e.memset(self.cst[:, 2:3], _m.pi / 2))
        self.vop(lambda e: e.memset(self.cst[:, 3:4], 1e-6))
        for dst, src in ((self.idn, self.cIDN), (self.ones64, self.cONES64),
                         (self.ones1x64, self.cONES1x64), (self.sel0, self.cSEL0),
                         (self.sel1, self.cSEL1), (self.m_ut, self.cM_UT),
                         (self.m_lt, self.cM_LT), (self.bd, self.cBD),
                         (self.shiftT, self.cSHIFTT), (self.e127T, self.cE127T)):
            self.dma_g(dst[:, :], src)

        with contextlib.ExitStack() as ph:
            lr = self.sb("lr", [1, H], F32, ph)
            lr2 = self.sb("lr2", [1, H], F32, ph)
            t_ld = self.dma_g(lr[:, :], self.LRAW)
            t_tan = self.sop(lambda e: e.activation(lr2[:, :], lr[:, :], AF.Tanh, scale=1.0),
                             waits=[t_ld])
            t_lv = self.vop(lambda e: e.tensor_scalar(self.lamq[:, :], lr2[:, :], 0.5,
                                                      None, ALU.mult),
                            waits=[(self.SE, t_tan)])
            t_ls = self.dma_s(self.lam_d, self.lamq[:, :], waits=[(self.VE, t_lv)])
            lam2d = self.lam_d.rearrange("one (ob two) -> one ob two", two=2)
            t_l1 = self.dma_g(self.lamcol[0:64, :], lam2d[0:1, :, 0].partition_broadcast(64),
                              waits=[t_ls], slow=True)
            self.dma_g(self.lamcol[64:128, :], lam2d[0:1, :, 1].partition_broadcast(64),
                       waits=[t_l1], slow=True)

            self.barrier(producers=("sync", "gpsimd", "vector", "scalar"), consumers=ENGINES)



def shard_inputs(inputs):
    c = host_constants()
    xs = np.ascontiguousarray(inputs["L0_attn_x"])
    maps = []
    for core in range(8):
        m = dict(
            x=xs[2 * core:2 * core + 2],
            wq=inputs["L0_attn_Wq"], wk=inputs["L0_attn_Wk"], wv=inputs["L0_attn_Wv"],
            wo=inputs["L0_attn_Wo"], wqp=inputs["L0_attn_Wqp"], wkp=inputs["L0_attn_Wkp"],
            lraw=inputs["L0_attn_lambda_raw"].reshape(1, H),
            fwqkv=inputs["L0_ffn_wqkv"], fqs=inputs["L0_ffn_q_scale"],
            fks=inputs["L0_ffn_k_scale"], fwo=inputs["L0_ffn_wo"],
            flg=inputs["L0_ffn_ln_g"], flb=inputs["L0_ffn_ln_b"],
            fw1=inputs["L0_ffn_w1"], fb1=inputs["L0_ffn_b1"],
            fw2=inputs["L0_ffn_w2"], fb2=inputs["L0_ffn_b2"],
        )
        m.update(c)
        maps.append(m)
    return maps


def kernel(**inputs):
    mk = MK()
    nc = mk.build()
    maps = shard_inputs(inputs)
    res = run_bass_kernel_spmd(nc, maps, core_ids=list(range(8)))
    return np.concatenate([r["y"] for r in res.results], axis=0)


MK.emit_attn = emit_attn
MK.emit_ffn = emit_ffn
MK.emit_final = emit_final


# version 14 (final)
# speedup vs baseline: 1.0752x; 1.0237x over previous
import contextlib
import math
import concourse.mybir as mybir

F32 = mybir.dt.float32
BF16 = mybir.dt.bfloat16
AF = mybir.ActivationFunctionType
ALU = mybir.AluOpType
AX = mybir.AxisListType

S, D, H, HD = 2048, 1024, 16, 64
NSB, NDB, HP = 16, 8, 8
PI = math.pi


def x_view(self, bi, sb):
    return self.X[bi].rearrange("(sb p) d -> p sb d", p=128)[:, sb, :]


def emit_ln_stats(self, c, src_ap, pe_wait=None):
    st = self.stat
    VE, SE = self.VE, self.SE
    junk = self.lnjunk
    w = [pe_wait] if pe_wait else []
    self.sop(lambda e, j=junk: e.activation(j[:, :], src_ap, AF.Identity,
                                            accum_out=st[:, c + 6:c + 7]), waits=w)
    s2 = self.sop(lambda e, j=junk: e.activation(j[:, :], src_ap, AF.Square,
                                                 accum_out=st[:, c + 7:c + 8]))
    v1 = self.vop(lambda e: e.tensor_scalar(st[:, c:c + 1], st[:, c + 6:c + 7],
                                            1.0 / 1024, None, ALU.mult), waits=[(SE, s2)])
    v2 = self.vop(lambda e: e.tensor_tensor(st[:, c + 2:c + 3], st[:, c:c + 1],
                                            st[:, c:c + 1], ALU.mult), waits=[(VE, v1)])
    v3 = self.vop(lambda e: e.scalar_tensor_tensor(st[:, c + 1:c + 2], st[:, c + 7:c + 8],
                                                   1.0 / 1024, st[:, c + 2:c + 3],
                                                   ALU.mult, ALU.subtract), waits=[(VE, v2)])
    s3 = self.sop(lambda e: e.activation(st[:, c + 3:c + 4], st[:, c + 1:c + 2], AF.Ln,
                                         bias=self.cst[:, 0:1]), waits=[(VE, v3)])
    s4 = self.sop(lambda e: e.activation(st[:, c + 4:c + 5], st[:, c + 3:c + 4],
                                         AF.Exp, scale=-0.5), waits=[(SE, s3)])
    tv2 = self.vop(lambda e: e.scalar_tensor_tensor(
        st[:, c + 5:c + 6], st[:, c:c + 1], -1.0, st[:, c + 4:c + 5], ALU.mult, ALU.mult),
        waits=[(SE, s4)])
    return tv2


def emit_attn(self, bi):
    VE, SE, PE, LDG, STS = self.VE, self.SE, self.PE, self.LDG, self.STS
    with contextlib.ExitStack() as tstack:
        self.tT = self.sb("tT_a", [128, 8 * 2049], BF16, tstack)
        self.ps_tr = self.ps("tr", [128, 1024], BF16, tstack)
        self.ps_tr2 = self.ps("tr2", [128, 1024], BF16, tstack)
        self.ps_a = self.ps("pa", [128, 512], F32, tstack)
        self.ps_b = self.ps("pb", [128, 512], F32, tstack)
        tT_now = self.tT
        mz_t = self.vop(lambda e, t=tT_now: e.memset(t[:, :], 0.0))
        emit_attn_inner(self, bi, mz_t)
    self.tT = None
    if self.cfg.get("upto") in ("a0", "a1q", "a1"):
        return
    emit_attn2(self, bi)


def emit_attn_inner(self, bi, mz_t):
    VE, SE, PE, LDG, STS = self.VE, self.SE, self.PE, self.LDG, self.STS

    self.mark(f'A0_b{bi}')
    with contextlib.ExitStack() as ph:
        x0t = self.sb("a0_x0", [128, 4 * D], F32, ph)
        xnb = self.sb("a0_xn", [128, 4 * D], BF16, ph)
        self.lnjunk = self.sb("a0_junk", [128, 1024], F32, ph)
        se_app, ve_ev, pe_tr = {}, {}, {}
        for sb in range(NSB):
            par = sb % 2
            p4 = sb % 4
            x0 = x0t[:, p4 * D:(p4 + 1) * D]
            xn = xnb[:, p4 * D:(p4 + 1) * D]
            w = [(SE, se_app[sb - 4])] if sb - 4 in se_app else []
            tld = self.dma_g(x0, x_view(self, bi, sb), waits=w)
            c = p4 * 16
            emit_ln_stats(self, c, x0, pe_wait=tld)
            w = [(VE, self.VE.v)] + ([(PE, pe_tr[sb - 4])] if sb - 4 in pe_tr else [])
            se_app[sb] = self.sop(lambda e, xn=xn, x0=x0, c=c: e.activation(
                xn, x0, AF.Identity, scale=self.stat[:, c + 4:c + 5],
                bias=self.stat[:, c + 5:c + 6]), waits=w)
            pst = self.ps_tr if par == 0 else self.ps_tr2
            w = [(SE, se_app[sb])] + ([(VE, ve_ev[sb - 2])] if sb - 2 in ve_ev else [])
            for db in range(NDB):
                pe_tr[sb] = self.tr(pst[:, db * 128:(db + 1) * 128],
                                    xn[:, db * 128:(db + 1) * 128], inc=PE,
                                    waits=(w if db == 0 else ()))
            dst = self.tT[:, :].rearrange("p (db c) -> p db c", db=NDB)[:, :, 1 + sb * 128: 129 + sb * 128]
            src = pst[:, :].rearrange("p (k c) -> p k c", k=NDB)
            ve_ev[sb] = self.vop(lambda e, d=dst, s=src: e.tensor_copy(d, s),
                                 waits=[(PE, pe_tr[sb])] + ([(VE, mz_t)] if sb == 0 else []))
        self.barrier()
        self.tap("xnT", self.tT[:, :])
    if self.cfg.get("upto") == "a0":
        return

    self.mark(f'A1_b{bi}')
    with contextlib.ExitStack() as ph:
        wp = self.sb("a1_wp", [128, NDB * D], BF16, ph)
        wd = self.sb("a1_wd", [128, NDB * D], BF16, ph)
        cosP = self.sb("a1_cos", [128, NDB * S], BF16, ph)
        sinP = self.sb("a1_sin", [128, NDB * S], BF16, ph)
        sc1 = self.sb("a1_sc1", [128, 3 * 512], F32, ph)
        sc2 = self.sb("a1_sc2", [128, 2 * 512], F32, ph)
        asqb = self.sb("a1_asq", [128, 2 * 512], BF16, ph)
        rln = self.sb("a1_rln", [128, 512], F32, ph)
        rinvf = self.sb("a1_rinv", [128, 2 * 512], F32, ph)
        ampv = self.sb("a1_ampv", [128, 512], F32, ph)
        dstg = self.sb("a1_dst", [128, 2 * S], BF16, ph)
        cstg = self.sb("a1_cst", [128, 2 * S], BF16, ph)
        sstg = self.sb("a1_sst", [128, 2 * S], BF16, ph)

        pe_last_p = [0]
        pe_last_a = [0]
        for side in ("q", "k"):
            w_p = self.Wqp if side == "q" else self.Wkp
            w_d = self.Wq if side == "q" else self.Wk
            d_s = self.qd_s if side == "q" else self.kd_s
            c_s = self.qc_s if side == "q" else self.kc_s
            s_s = self.qs_s if side == "q" else self.ks_s
            t_wp = self.dma_w3(wp[:, :], w_p, NDB, waits=[(PE, pe_last_p[0])])
            t_wd = self.dma_w3(wd[:, :], w_d, NDB, waits=[(PE, pe_last_a[0])])


            se_p = {}
            for it in range(int(self.cfg.get('a1n', 32))):
                ob, ch = divmod(it, 4)
                par = it % 2
                pss = self.ps_a if par == 0 else self.ps_b
                w = [t_wp] + ([(SE, se_p[it - 2])] if it - 2 in se_p else [])
                for db in range(NDB):
                    pem = self.mm(pss[:, :],
                                  wp[:, db * D + ob * 128: db * D + ob * 128 + 128],
                                  self.tTv(db, ch * 512, 512),
                                  start=(db == 0), stop=(db == NDB - 1),
                                  inc=(PE if db == NDB - 1 else None),
                                  waits=(w if db == 0 else ()))
                s1v = sc1[:, par * 512:(par + 1) * 512]
                s2v = sc2[:, par * 512:(par + 1) * 512]
                cpv = cosP[:, ob * S + ch * 512: ob * S + ch * 512 + 512]
                spv = sinP[:, ob * S + ch * 512: ob * S + ch * 512 + 512]
                self.sop(lambda e, o=s1v, i=pss: e.activation(o, i[:, :], AF.Tanh),
                         waits=[(PE, pem)])
                self.sop(lambda e, o=s2v, i=s1v: e.activation(o, i, AF.Abs))
                self.sop(lambda e, o=cpv, i=s2v: e.activation(o, i, AF.Sin,
                                                              scale=-PI, bias=self.cst[:, 2:3]))
                se_p[it] = self.sop(lambda e, o=spv, i=s1v: e.activation(o, i, AF.Sin,
                                                                         scale=PI))
            pe_last_p[0] = PE.v
            self.barrier(producers=("scalar",), consumers=("tensor",))
            if self.cfg.get("upto") == "a1p":
                self.barrier()
                return

            se_ex, se_sq, se_ri, ve_d, ve_f, rms_pe, st_s = {}, {}, {}, {}, {}, {}, {}

            def projA(it):
                ob, ch = divmod(it, 4)
                par = it % 2
                sbp = ob % 2
                pss = self.ps_a if par == 0 else self.ps_b
                w = [t_wd]
                if it - 2 in se_ex:
                    w += [(SE, se_ex[it - 2]), (VE, ve_d[it - 2])]
                pem = None
                for db in range(NDB):
                    pem = self.mm(pss[:, :],
                                  wd[:, db * D + ob * 128: db * D + ob * 128 + 128],
                                  self.tTv(db, ch * 512, 512),
                                  start=(db == 0), stop=(db == NDB - 1),
                                  inc=(PE if db == NDB - 1 else None),
                                  waits=(w if db == 0 else ()))
                s1v = sc1[:, (it % 3) * 512:(it % 3 + 1) * 512]
                s2v = sc2[:, par * 512:(par + 1) * 512]
                aqv = asqb[:, par * 512:(par + 1) * 512]
                ddst = dstg[:, sbp * S + ch * 512: sbp * S + ch * 512 + 512]
                w = [(PE, pem)]
                if ch == 0 and (ob - 2) in st_s:
                    w.append(st_s[ob - 2][0])
                ve_d[it] = self.vop(lambda e, o=ddst, i=pss: e.tensor_copy(o, i[:, :]), waits=w)
                se_ex[it] = self.sop(lambda e, o=s2v, i=pss: e.activation(o, i[:, :], AF.Exp),
                                     waits=[(PE, pem)])
                w = [(VE, ve_f[it - 3])] if it - 3 in ve_f else []
                self.sop(lambda e, o=s1v, i=s2v: e.activation(o, i, AF.Ln,
                                                              bias=self.cst[:, 1:2]), waits=w)
                w = [(PE, rms_pe[it - 2])] if it - 2 in rms_pe else []
                se_sq[it] = self.sop(lambda e, o=aqv, i=s1v: e.activation(o, i, AF.Square),
                                     waits=w)

            def postA(it):
                ob, ch = divmod(it, 4)
                par = it % 2
                sbp = ob % 2
                s1v = sc1[:, (it % 3) * 512:(it % 3 + 1) * 512]
                aqv = asqb[:, par * 512:(par + 1) * 512]
                riv = rinvf[:, par * 512:(par + 1) * 512]
                cdst = cstg[:, sbp * S + ch * 512: sbp * S + ch * 512 + 512]
                sdst = sstg[:, sbp * S + ch * 512: sbp * S + ch * 512 + 512]
                cpv = cosP[:, ob * S + ch * 512: ob * S + ch * 512 + 512]
                spv = sinP[:, ob * S + ch * 512: ob * S + ch * 512 + 512]
                prs = self.ps_big[:, (2 + par) * 512:(3 + par) * 512]
                w = [(SE, se_sq[it])]
                if it - 2 in se_ri:
                    w.append((SE, se_ri[it - 2]))
                rms_pe[it] = self.mm(prs, self.bd[:, :], aqv, start=True, stop=True,
                                     inc=PE, waits=w)
                self.sop(lambda e, i=prs: e.activation(rln[:, :], i[:, :] if False else i, AF.Ln,
                                                       scale=1.0 / 64, bias=self.cst[:, 3:4]),
                         waits=[(PE, rms_pe[it])])
                wri = [(VE, ve_f[it - 2])] if it - 2 in ve_f else []
                se_ri[it] = self.sop(lambda e, o=riv: e.activation(o, rln[:, :], AF.Exp,
                                                                   scale=-0.5), waits=wri)
                if side == "q":
                    self.vop(lambda e, o=ampv[:, :], a=s1v, r=riv, ob=ob: e.scalar_tensor_tensor(
                        o, a, self.lamcol[:, ob:ob + 1], r, ALU.mult, ALU.mult),
                        waits=[(SE, se_ri[it])])
                else:
                    self.vop(lambda e, o=ampv[:, :], a=s1v, r=riv: e.tensor_tensor(
                        o, a, r, ALU.mult), waits=[(SE, se_ri[it])])
                wc = [st_s[ob - 2][1]] if (ch == 0 and (ob - 2) in st_s) else []
                self.vop(lambda e, o=cdst, c=cpv, am=ampv[:, :]: e.tensor_tensor(o, am, c, ALU.mult),
                         waits=wc)
                ws = [st_s[ob - 2][2]] if (ch == 0 and (ob - 2) in st_s) else []
                ve_f[it] = self.vop(lambda e, o=sdst, s=spv, am=ampv[:, :]: e.tensor_tensor(
                    o, am, s, ALU.mult), waits=ws)
                if ch == 3:
                    w = [(VE, ve_f[it]), (VE, ve_d[it])]
                    t_d = self.dma_s(d_s[:, ob * S:(ob + 1) * S],
                                     dstg[:, sbp * S:(sbp + 1) * S], waits=w)
                    t_c = self.dma_s(c_s[:, ob * S:(ob + 1) * S], cstg[:, sbp * S:(sbp + 1) * S])
                    t_s2 = self.dma_s(s_s[:, ob * S:(ob + 1) * S], sstg[:, sbp * S:(sbp + 1) * S])
                    st_s[ob] = (t_d, t_c, t_s2)

            nA = int(self.cfg.get("a1an", 32))
            for it in range(nA + 1):
                if it < nA:
                    projA(it)
                if it >= 1:
                    postA(it - 1)
            pe_last_a[0] = PE.v
            self.barrier()
            if self.cfg.get("upto") == "a1q":
                return

        self.mark(f'A1V_b{bi}')
        t_wvc = self.dma_w3c(wp[:, :], self.Wv, NDB, D)
        vst = self.sb("a1_vst", [128, 2 * 1040], BF16, ph)
        mv = self.vop(lambda e: e.memset(vst[:, :], 1.0))
        ve_v, st_v = {}, {}
        for sb in range(NSB):
            par = sb % 2
            psy = self.ps_big[:, par * 1024:(par + 1) * 1024]
            w = [(VE, mv)] + ([(VE, ve_v[sb - 2])] if sb - 2 in ve_v else [])
            for db in range(NDB):
                for nn in range(2):
                    wdb = ([t_wvc[db]] if sb == 0 and nn == 0 else [])
                    pem = self.mm(psy[:, nn * 512:(nn + 1) * 512],
                                  self.tTv(db, sb * 128, 128),
                                  wp[:, db * D + nn * 512: db * D + (nn + 1) * 512],
                                  start=(db == 0), stop=(db == NDB - 1),
                                  inc=(PE if db == NDB - 1 and nn == 1 else None),
                                  waits=((w + wdb) if db == 0 and nn == 0 else wdb))
            dst = vst[:, par * 1040:(par + 1) * 1040].rearrange("p (h c) -> p h c", h=H)[:, :, 0:64]
            src = psy.rearrange("p (h c) -> p h c", h=H)
            w = [(PE, pem)] + ([st_v[sb - 2]] if sb - 2 in st_v else [])
            ve_v[sb] = self.vop(lambda e, d=dst, s=src: e.tensor_copy(d, s), waits=w)
            st_v[sb] = self.dma_s(self.vx_s[:, sb * 1040:(sb + 1) * 1040],
                                  vst[:, par * 1040:(par + 1) * 1040],
                                  waits=[(VE, ve_v[sb])])
        self.barrier()


def emit_cast_fw12(self, ph):
    wbufs = [self.sb("cwA", [128, 2730], BF16, ph),
             self.sb("cwB", [128, 2730], BF16, ph)]
    st = {"i": 0, "hist": {}}

    def emit_piece(src, dst, cols, db=None, np_=128):
        i = st["i"]
        st["i"] += 1
        wbuf = wbufs[i % 2]
        wt = [st["hist"][i - 2]] if i - 2 in st["hist"] else []
        wv = wbuf[0:np_, 0:cols]
        lv = wv.rearrange("p (db o) -> p db o", db=db) if db else wv
        t1 = self.dma_g(lv, src, waits=wt)
        st["hist"][i] = self.dma_s(dst, wv, waits=[t1])

    def pf(src, dst, cols, db=None, np_=128):
        return lambda: emit_piece(src, dst, cols, db, np_)

    fns = []
    for l in (0, 1):
        for j in range(NDB):
            for hh in (0, 1):
                fns.append(pf(
                    self.FW1[l][128 * j:128 * j + 128, hh * 2730:(hh + 1) * 2730],
                    self.fw1_c[l][:, j * HID2 + hh * 2730: j * HID2 + (hh + 1) * 2730],
                    2730))
        for g in range(10):
            fns.append(pf(
                self.FW2[l][256 * g:256 * g + 256].rearrange("(db p) o -> p db o", p=128),
                self.fw2_c[l][:, g * 2 * D:(g + 1) * 2 * D], 2 * D, db=2))
        fns.append(pf(self.FW2[l][2560:2688, :], self.fw2_c[l][:, 20 * D:21 * D], D))
        fns.append(pf(self.FW2[l][2688:2730, :], self.fw2_c[l][0:42, 21 * D:22 * D],
                      D, np_=42))
    return fns


def emit_attn2(self, bi):
    self.mark(f'A2_b{bi}')
    VE, SE, PE, LDG, STS, GE = self.VE, self.SE, self.PE, self.LDG, self.STS, self.GE
    with contextlib.ExitStack() as ph:
        cast_fns = emit_cast_fw12(self, ph) if bi == self.batches[0] else []
        ps_s = [self.ps_big[:, 0:1024], self.ps_big[:, 1024:2048]]
        ps_pv = self.ps("a2_pv", [128, 1024], F32, ph)
        ps_bc = self.ps("a2_bc", [128, 1024], F32, ph)
        kdcb = self.sb("a2_kdc", [128, 4 * S], BF16, ph)
        qdcb = self.sb("a2_qdc", [128, 4 * S], BF16, ph)
        ksb = self.sb("a2_ks", [128, 2 * S], BF16, ph)
        qszb = self.sb("a2_qsz", [128, 4 * S], BF16, ph)
        vxb = self.sb("a2_vx", [128, 2 * NSB * 196], BF16, ph)
        ptb = self.sb("a2_pt", [128, 2 * 1024], BF16, ph)
        otb = self.sb("a2_ot", [128, HP * S], BF16, ph)
        opv = self.sb("a2_op", [64, 2 * 512], BF16, ph)
        onrm = self.sb("a2_on", [64, 2 * 512], BF16, ph)
        rowA = self.sb("a2_rA", [1, 2 * 512], F32, ph)
        rowB = self.sb("a2_rB", [1, 2 * 512], BF16, ph)
        csu = self.sb("a2_csu", [128, 128], BF16, ph)
        cs_v = self.vop(lambda e: e.tensor_scalar(csu[:, :], self.m_lt[:, :],
                                                  30000.0, -30000.0, ALU.mult, ALU.add))
        mz0 = self.vop(lambda e: e.memset(ps_s[0][:, :], 0.0))
        mz1 = self.vop(lambda e: e.memset(ps_s[1][:, :], 0.0))
        mzq = self.vop(lambda e: e.memset(qszb[:, :], 0.0))
        mzv = self.vop(lambda e: e.memset(vxb[:, :], 1.0))
        pair_pe_done = {}
        pv_guard = {}
        bc_guard = {}
        row_guard = {}
        on_guard = {}
        ld_hp = {}

        def load_pair(hp):
            par = hp % 2
            w = [(PE, pair_pe_done[hp - 2])] if hp - 2 in pair_pe_done else []
            toks = []
            for hl in range(2):
                for half, st in ((0, self.kd_s), (1, self.kc_s)):
                    toks.append(self.dma_g(
                        kdcb[64 * half:64 * half + 64,
                             par * 2 * S + hl * S:par * 2 * S + (hl + 1) * S],
                        st[hl * 64:hl * 64 + 64, hp * S:(hp + 1) * S], waits=w))
                    w = []
                for half, st in ((0, self.qd_s), (1, self.qc_s)):
                    toks.append(self.dma_g(
                        qdcb[64 * half:64 * half + 64,
                             par * 2 * S + hl * S:par * 2 * S + (hl + 1) * S],
                        st[hl * 64:hl * 64 + 64, hp * S:(hp + 1) * S]))
            toks.append(self.dma_g(ksb[:, par * S:(par + 1) * S],
                                   self.ks_s[:, hp * S:(hp + 1) * S]))
            wq = [(VE, mzq)] if hp < 2 else []
            for hl in range(2):
                toks.append(self.dma_g(
                    qszb[hl * 64:hl * 64 + 64,
                         par * 2 * S + hl * S:par * 2 * S + hl * S + S],
                    self.qs_s[hl * 64:hl * 64 + 64, hp * S:(hp + 1) * S], waits=wq))
                wq = []
            vsrc = self.vx_s.rearrange("p (sb c) -> p sb c", sb=NSB)[:, :, 130 * hp:130 * (hp + 1)]
            vdst = vxb[:, par * NSB * 196:(par + 1) * NSB * 196] \
                .rearrange("p (sb c) -> p sb c", sb=NSB)[:, :, 0:130]
            toks.append(self.dma_g(vdst, vsrc, waits=([(VE, mzv)] if hp < 2 else [])))
            return toks

        ld_hp[0] = load_pair(0)
        for hp in range(HP):
            par = hp % 2
            if hp + 1 < HP:
                ld_hp[hp + 1] = load_pair(hp + 1)
            for _ in range(9):
                if cast_fns:
                    cast_fns.pop(0)()
            for parity in range(2):
                h = 2 * hp + parity
                base = 64 * parity
                tp = (base, 0)
                se_ex = {}
                first_w = list(ld_hp[hp]) + [(VE, cs_v), (VE, mz0), (VE, mz1)]
                tiles = []
                for c in range(4):
                    for kb in range(4 * c + 4):
                        tiles.append((kb, c))
                pairs = [tiles[i:i + 2] for i in range(0, len(tiles), 2)]

                def emit_S_pair(pi, pr):
                    sl = pi % 2
                    pss = ps_s[sl]
                    nonlocal first_w
                    w = list(first_w)
                    first_w = []
                    if pi - 2 in se_ex:
                        w.append((SE, se_ex[pi - 2]))
                    pem = None
                    for half, (kb, c) in enumerate(pr):
                        col0 = max(kb * 128, 512 * c)
                        n = 512 * (c + 1) - col0
                        off = half * 512
                        b2 = par * 2 * S + parity * S
                        self.mm(pss[:, off:off + n],
                                kdcb[:, b2 + kb * 128: b2 + kb * 128 + 128],
                                qdcb[:, b2 + col0: b2 + col0 + n],
                                start=True, stop=False, waits=w)
                        w = []
                        diag = (col0 == kb * 128)
                        qz = par * 2 * S + parity * S
                        pem = self.mm(pss[:, off:off + n],
                                      ksb[:, par * S + kb * 128: par * S + kb * 128 + 128],
                                      qszb[:, qz + col0: qz + col0 + n],
                                      start=False, stop=not diag,
                                      inc=(None if diag else PE))
                        if diag:
                            pem = self.mm(pss[:, off:off + 128], csu[:, :], self.idn[:, :],
                                          start=False, stop=True, inc=PE)
                    kb1, c1 = pr[-1]
                    ntot = 512 + 512 * (c1 + 1) - max(kb1 * 128, 512 * c1) if len(pr) == 2 \
                        else 512
                    ex = self.sop(lambda e, o=ptb[:, sl * 1024: sl * 1024 + ntot],
                                  i=pss, ntot=ntot:
                                  e.activation(o, i[:, 0:ntot], AF.Exp, scale=0.125),
                                  waits=[(PE, pem)])
                    se_ex[pi] = ex
                    return (sl, pr, ex)

                def emit_PV_pair(p):
                    sl, pr, ex = p
                    for half, (kb, c) in enumerate(pr):
                        col0 = max(kb * 128, 512 * c)
                        n = 512 * (c + 1) - col0
                        s = c % 2
                        lv = vxb[:, par * NSB * 196:(par + 1) * NSB * 196] \
                            .rearrange("p (sb c) -> p sb c", sb=NSB)[:, kb, parity * 65:parity * 65 + 128]
                        w = [(SE, ex)]
                        if kb == 0:
                            w += pv_guard.pop(s, [])
                        pv = self.mm(ps_pv[:, s * 512 + col0 - 512 * c:
                                           s * 512 + col0 - 512 * c + n], lv,
                                     ptb[:, sl * 1024 + half * 512: sl * 1024 + half * 512 + n],
                                     start=(kb == 0), stop=(kb == 4 * c + 3),
                                     inc=PE, waits=w)
                        if kb == 4 * c + 3:
                            emit_chunk_norm(c, s, pv)

                def emit_chunk_norm(c, s, pv):
                    ov = opv[:, s * 512:(s + 1) * 512]
                    cp = self.vop(lambda e, ov=ov, s=s: e.tensor_copy(
                        ov, ps_pv[0:64, s * 512:(s + 1) * 512]), waits=[(PE, pv)])
                    wl = [(PE, pv)] + ([row_guard[s]] if s in row_guard else [])
                    s1 = self.sop(lambda e, s=s: e.activation(
                        rowA[0:1, s * 512:(s + 1) * 512],
                        ps_pv[64:65, s * 512:(s + 1) * 512], AF.Ln), waits=wl)
                    s2 = self.sop(lambda e, s=s: e.activation(
                        rowB[0:1, s * 512:(s + 1) * 512],
                        rowA[0:1, s * 512:(s + 1) * 512], AF.Exp, scale=-1.0))
                    pv_guard[s] = [(VE, cp), (SE, s2)]
                    wb = [(SE, s2)] + ([bc_guard[s]] if s in bc_guard else [])
                    bc = self.mm(ps_bc[0:64, s * 512:(s + 1) * 512], self.ones1x64[:, :],
                                 rowB[0:1, s * 512:(s + 1) * 512], start=True, stop=True,
                                 inc=PE, waits=wb)
                    row_guard[s] = (PE, bc)
                    blk = otb[:, hp * S + c * 512: hp * S + (c + 1) * 512]
                    if parity == 0:
                        tt = self.vop(lambda e, blk=blk, ov=ov, s=s: e.tensor_tensor(
                            blk[0:64, :], ov, ps_bc[0:64, s * 512:(s + 1) * 512], ALU.mult),
                            waits=[(PE, bc)])
                        bc_guard[s] = (VE, tt)
                    else:
                        osl = c % 2
                        onv = onrm[:, osl * 512:(osl + 1) * 512]
                        w2 = [(PE, bc)] + ([on_guard[osl]] if osl in on_guard else [])
                        tt = self.vop(lambda e, onv=onv, ov=ov, s=s: e.tensor_tensor(
                            onv, ov, ps_bc[0:64, s * 512:(s + 1) * 512], ALU.mult), waits=w2)
                        bc_guard[s] = (VE, tt)
                        sv = self.sop(lambda e, blk=blk, onv=onv: e.activation(
                            blk[64:128, :], onv, AF.Copy), waits=[(VE, tt)])
                        on_guard[osl] = (SE, sv)

                pend = None
                for pi, pr in enumerate(pairs):
                    cur = emit_S_pair(pi, pr)
                    if pend is not None:
                        emit_PV_pair(pend)
                    pend = cur
                emit_PV_pair(pend)
                pair_pe_done[hp] = PE.v
        while cast_fns:
            cast_fns.pop(0)()
        self.barrier()
        self.tap("otb", otb[:, :])

        self.mark(f'A2Wo_b{bi}')
        wo = self.sb("a2_wo", [128, NDB * D], BF16, ph)
        self.lnjunk = self.sb("a2_junk", [128, 1024], F32, ph)
        x0t = self.sb("a2_x0", [128, 4 * D], F32, ph)
        z1t = self.sb("a2_z1", [128, 4 * D], F32, ph)
        zsc = self.sb("a2_zs", [128, 4 * D], F32, ph)
        t_woc = self.dma_w3c(wo[:, :], self.Wo, NDB, D)
        ve_ap, ve_z, st_z = {}, {}, {}
        for sb in range(NSB):
            par = sb % 2
            p4 = sb % 4
            w = [(VE, ve_z[sb - 4])] if sb - 4 in ve_z else []
            ldx = self.dma_g(x0t[:, p4 * D:(p4 + 1) * D], x_view(self, bi, sb), waits=w)
            psy = self.ps_big[:, par * 1024:(par + 1) * 1024]
            w = ([(VE, ve_ap[sb - 2])] if sb - 2 in ve_ap else [])
            for db in range(NDB):
                for nn in range(2):
                    wdb = ([t_woc[db]] if sb == 0 and nn == 0 else [])
                    pem = self.mm(psy[:, nn * 512:(nn + 1) * 512],
                                  otb[:, db * S + sb * 128: db * S + sb * 128 + 128],
                                  wo[:, db * D + nn * 512: db * D + (nn + 1) * 512],
                                  start=(db == 0), stop=(db == NDB - 1),
                                  inc=(PE if db == NDB - 1 and nn == 1 else None),
                                  waits=((w + wdb) if db == 0 and nn == 0 else wdb))
            c = p4 * 16
            emit_ln_stats(self, c, psy, pe_wait=(PE, pem))
            zv = z1t[:, p4 * D:(p4 + 1) * D]
            zs = zsc[:, p4 * D:(p4 + 1) * D]
            w = [(VE, self.VE.v), ldx] + ([st_z[sb - 4]] if sb - 4 in st_z else [])
            ve_ap[sb] = self.vop(lambda e, zs=zs, c=c, psy=psy: e.tensor_scalar(
                zs, psy, self.stat[:, c + 4:c + 5],
                self.stat[:, c + 5:c + 6], ALU.mult, ALU.add), waits=w)
            ve_z[sb] = self.vop(lambda e, zv=zv, zs=zs, x0=x0t[:, p4 * D:(p4 + 1) * D]:
                                e.tensor_tensor(zv, zs, x0, ALU.add))
            st_z[sb] = self.dma_s(self.z1_s[:, sb * D:(sb + 1) * D], zv,
                                  waits=[(VE, ve_z[sb])])
        self.barrier()


"""FFN (2-layer local-attention transformer) phases + final combine. Mixin for MK."""
import contextlib
import concourse.mybir as mybir

F32 = mybir.dt.float32
BF16 = mybir.dt.bfloat16
AF = mybir.ActivationFunctionType
ALU = mybir.AluOpType
AX = mybir.AxisListType

S, D = 2048, 1024
NSB, NDB = 16, 8
FH, FHD, FIN, WIN = 8, 64, 512, 512
HID, HID2, NJ = 2730, 5460, 22


def emit_ffn(self, bi, l):
    with contextlib.ExitStack() as tstack:
        self.tT = self.sb("tT_f", [128, 8 * 2049], BF16, tstack)
        self.ps_a = self.ps("pa", [128, 512], F32, tstack)
        self.ps_b = self.ps("pb", [128, 512], F32, tstack)
        tT_now = self.tT
        mz_t = self.vop(lambda e, t=tT_now: e.memset(t[:, :], 0.0))
        emit_f0(self, bi, l, mz_t)
        with contextlib.ExitStack() as s1:
            vxf = self.sb("f1_vx", [128, NSB * 584], BF16, s1)
            qrTz = self.sb("f2_qTz", [128, 8 * S], BF16, s1)
            krT = self.sb("f2_kT", [128, 4 * S], BF16, s1)
            with contextlib.ExitStack() as s2:
                qrot = self.sb("f1_qr", [128, NSB * 512], BF16, s2)
                krot = self.sb("f1_kr", [128, NSB * 512], BF16, s2)
                emit_f1(self, bi, l, qrot, krot, vxf)
                emit_f2(self, qrot, krot, qrTz, krT)
            with contextlib.ExitStack() as s3:
                oT = self.sb("f3_oT", [128, 4 * S], BF16, s3)
                emit_f3(self, bi, l, qrTz, krT, vxf, oT)
                emit_f4(self, bi, l, oT)
        emit_f56(self, bi, l)
    self.tT = None


def emit_f0(self, bi, l, mz_t):
    self.mark(f'F0_b{bi}l{l}')
    """x_f stream tile-wise: l==0: LN(z1) -> f0_s spill + bf16 -> transpose;
    l==1: f0_s -> bf16 -> transpose."""
    VE, SE, PE, LDG, STS = self.VE, self.SE, self.PE, self.LDG, self.STS
    with contextlib.ExitStack() as ph:
        self.ps_tr = self.ps("tr", [128, 1024], BF16, ph)
        self.ps_tr2 = self.ps("tr2", [128, 1024], BF16, ph)
        xnb = self.sb("f0_xn", [128, 4 * D], BF16, ph)
        self.lnjunk = self.sb("f0_junk", [128, 1024], F32, ph)
        z1t = self.sb("f0_z1", [128, 4 * D], F32, ph)
        xft = self.sb("f0_xf", [128, 4 * D], F32, ph)
        se_app, ve_ev, pe_tr, st_f = {}, {}, {}, {}
        for sb in range(NSB):
            par = sb % 2
            p4 = sb % 4
            xn = xnb[:, p4 * D:(p4 + 1) * D]
            if l == 0:
                z1 = z1t[:, p4 * D:(p4 + 1) * D]
                xfv = xft[:, p4 * D:(p4 + 1) * D]
                w = [(SE, se_app[sb - 4])] if sb - 4 in se_app else []
                tld = self.dma_g(z1, self.z1_s[:, sb * D:(sb + 1) * D], waits=w)
                c = p4 * 16
                emit_ln_stats(self, c, z1, pe_wait=tld)
                w = [(VE, self.VE.v)] + ([st_f[sb - 4]] if sb - 4 in st_f else [])
                sf = self.sop(lambda e, o=xfv, i=z1, c=c: e.activation(
                    o, i, AF.Identity, scale=self.stat[:, c + 4:c + 5],
                    bias=self.stat[:, c + 5:c + 6]), waits=w)
                se_app[sb] = sf
                st_f[sb] = self.dma_s(self.f0_s[:, sb * D:(sb + 1) * D], xfv,
                                      waits=[(SE, sf)])
                w = [(SE, sf)] + ([(PE, pe_tr[sb - 4])] if sb - 4 in pe_tr else [])
                ve_c = self.vop(lambda e, o=xn, i=xfv: e.tensor_copy(o, i), waits=w)
            else:
                xfv = xft[:, p4 * D:(p4 + 1) * D]
                w = [(SE, se_app[sb - 4])] if sb - 4 in se_app else []
                tld = self.dma_g(xfv, self.f0_s[:, sb * D:(sb + 1) * D], waits=w)
                w = [tld] + ([(PE, pe_tr[sb - 4])] if sb - 4 in pe_tr else [])
                se_app[sb] = self.sop(lambda e, o=xn, i=xfv: e.activation(o, i, AF.Copy),
                                      waits=w)
                ve_c = self.VE.v
            pst = self.ps_tr if par == 0 else self.ps_tr2
            if l == 0:
                w = [(VE, ve_c)] + ([(VE, ve_ev[sb - 2])] if sb - 2 in ve_ev else [])
            else:
                w = [(SE, se_app[sb])] + ([(VE, ve_ev[sb - 2])] if sb - 2 in ve_ev else [])
            for db in range(NDB):
                pe_tr[sb] = self.tr(pst[:, db * 128:(db + 1) * 128],
                                    xn[:, db * 128:(db + 1) * 128], inc=PE,
                                    waits=(w if db == 0 else ()))
            dst = self.tT[:, :].rearrange("p (db c) -> p db c", db=NDB)[:, :, 1 + sb * 128: 129 + sb * 128]
            src = pst[:, :].rearrange("p (k c) -> p k c", k=NDB)
            ve_ev[sb] = self.vop(lambda e, d=dst, s=src: e.tensor_copy(d, s),
                                 waits=[(PE, pe_tr[sb])] + ([(VE, mz_t)] if sb == 0 else []))
        self.barrier()


def emit_f1(self, bi, l, qrot, krot, vxf):
    self.mark(f'F1_b{bi}l{l}')
    VE, SE, PE, LDG, STS, GE = self.VE, self.SE, self.PE, self.LDG, self.STS, self.GE
    with contextlib.ExitStack() as ph:
        wqkv = self.sb("f1_w", [128, NDB * 1536], BF16, ph)
        qsb_t = self.sb("f1_qs", [128, 512], BF16, ph)
        ksb_t = self.sb("f1_ks", [128, 512], BF16, ph)
        rot_t = self.sb("f1_rot", [128, 2 * 2048], BF16, ph)
        sq_t = self.sb("f1_sqt", [128, 2 * 512], F32, ph)
        qn_t = self.sb("f1_qn", [128, 1024], F32, ph)
        qn2_t = self.sb("f1_qn2", [128, 1024], F32, ph)
        r1_t = self.sb("f1_r1", [128, 1024], F32, ph)
        r2_t = self.sb("f1_r2", [128, 1024], F32, ph)
        t_wc = self.dma_w3c(wqkv[:, :], self.FWQKV[l], NDB, 1536)
        t_qs = t_ks = None
        for hh in range(FH):
            t_qs = self.dma_g(qsb_t[:, hh * FHD:(hh + 1) * FHD],
                              self.FQS[l][None, :].partition_broadcast(128),
                              waits=([t_qs] if t_qs else []))
            t_ks = self.dma_g(ksb_t[:, hh * FHD:(hh + 1) * FHD],
                              self.FKS[l][None, :].partition_broadcast(128),
                              waits=([t_ks] if t_ks else []))
        mv = self.vop(lambda e: e.memset(vxf[:, :], 1.0))
        ve_done, ve_rot = {}, {}
        for sb in range(NSB):
            par = sb % 2
            t_rots = []
            for ti, tab in enumerate((self.cRQC, self.cRQS, self.cRKC, self.cRKS)):
                wt = [(VE, ve_rot[sb - 2])] if sb - 2 in ve_rot else []
                t_rots.append(self.dma_g(
                    rot_t[:, par * 2048 + ti * 512: par * 2048 + (ti + 1) * 512],
                    tab[sb * 128:(sb + 1) * 128, :], waits=wt))
            if par == 0:
                psums = [self.ps_big[:, 0:512], self.ps_big[:, 512:1024],
                         self.ps_big[:, 1024:1536]]
            else:
                psums = [self.ps_big[:, 1536:2048], self.ps_a[:, :], self.ps_b[:, :]]
            w = [t_qs, t_ks, (VE, mv)]
            if sb - 2 in ve_done:
                w.append((VE, ve_done[sb - 2]))
            pem = None
            for db in range(NDB):
                lhs = self.tTv(db, sb * 128, 128) if db < 4 else self.tTs(db, sb * 128, 128)
                for nn in range(3):
                    wdb = ([t_wc[db]] if sb == 0 and nn == 0 else [])
                    pem = self.mm(psums[nn], lhs,
                                  wqkv[:, db * 1536 + nn * 512: db * 1536 + (nn + 1) * 512],
                                  start=(db == 0), stop=(db == NDB - 1),
                                  inc=(PE if db == NDB - 1 and nn == 2 else None),
                                  waits=((w + wdb) if db == 0 and nn == 0 else wdb))
            last_rot = None
            for side in range(2):
                psv = psums[side]
                sqv = sq_t[:, par * 512:(par + 1) * 512]
                wsq = [(PE, pem)] + ([(VE, ve_done[sb - 2])] if sb - 2 in ve_done else [])
                sv = self.sop(lambda e, o=sqv, i=psv: e.activation(o, i, AF.Square),
                              waits=wsq)
                c = 32 + par * 32 + side * 16
                rv = self.vop(lambda e, c=c, sqv=sqv: e.tensor_reduce(
                    self.stat[:, c:c + 8], sqv.rearrange("p (h d) -> p h d", h=FH),
                    AX.X, ALU.add), waits=[(SE, sv)])
                rv = self.vop(lambda e, c=c: e.tensor_scalar(
                    self.stat[:, c + 8:c + 16], self.stat[:, c:c + 8], 1e-24, None, ALU.max),
                    waits=[(VE, rv)])
                s_l = self.sop(lambda e, c=c: e.activation(
                    self.stat[:, c:c + 8], self.stat[:, c + 8:c + 16], AF.Ln),
                    waits=[(VE, rv)])
                sv = self.sop(lambda e, c=c: e.activation(
                    self.stat[:, c + 8:c + 16], self.stat[:, c:c + 8], AF.Exp, scale=-0.5),
                    waits=[(SE, s_l)])
                qv = qn_t[:, side * 512:(side + 1) * 512]
                q2v = qn2_t[:, side * 512:(side + 1) * 512]
                sct = qsb_t if side == 0 else ksb_t
                self.vop(lambda e, qv=qv, i=psv, s=sct: e.tensor_tensor(qv, i, s[:, :], ALU.mult))
                for hh in range(FH):
                    self.vop(lambda e, hh=hh, c=c, qv=qv, q2v=q2v: e.tensor_scalar(
                        q2v[:, hh * 64:(hh + 1) * 64], qv[:, hh * 64:(hh + 1) * 64],
                        self.stat[:, c + 8 + hh:c + 9 + hh], None, ALU.mult),
                        waits=([(SE, sv)] if hh == 0 else ()))
                ct = rot_t[:, par * 2048 + (2 * side) * 512: par * 2048 + (2 * side) * 512 + 512]
                st_ = rot_t[:, par * 2048 + (2 * side + 1) * 512: par * 2048 + (2 * side + 1) * 512 + 512]
                t1 = r1_t[:, side * 512:(side + 1) * 512]
                t2 = r2_t[:, side * 512:(side + 1) * 512]
                self.vop(lambda e, t1=t1, q2v=q2v, ct=ct: e.tensor_tensor(t1, q2v, ct, ALU.mult),
                         waits=t_rots)
                q23 = q2v.rearrange("p (h two d) -> p h two d", h=FH, two=2)
                st3 = st_.rearrange("p (h two d) -> p h two d", h=FH, two=2)
                t23 = t2.rearrange("p (h two d) -> p h two d", h=FH, two=2)
                self.vop(lambda e, t23=t23, q23=q23, st3=st3: e.tensor_tensor(
                    t23[:, :, 0, :], q23[:, :, 1, :], st3[:, :, 0, :], ALU.mult))
                self.vop(lambda e, t23=t23, q23=q23, st3=st3: e.tensor_tensor(
                    t23[:, :, 1, :], q23[:, :, 0, :], st3[:, :, 1, :], ALU.mult))
                dstq = (qrot if side == 0 else krot)[:, sb * 512:(sb + 1) * 512]
                last_rot = self.vop(lambda e, d=dstq, t1=t1, t2=t2: e.tensor_tensor(
                    d, t1, t2, ALU.add))
            ve_rot[sb] = last_rot
            dst = vxf[:, sb * 584: sb * 584 + 520].rearrange("p (h c) -> p h c", h=FH)[:, :, 0:64]
            src = psums[2].rearrange("p (h c) -> p h c", h=FH)
            ve_done[sb] = self.vop(lambda e, d=dst, s=src: e.tensor_copy(d, s))
        self.barrier()


def emit_f2(self, qrot, krot, qrTz, krT):
    self.mark('F2')
    VE, SE, PE = self.VE, self.SE, self.PE
    with contextlib.ExitStack() as ph:
        self.ps_tr = self.ps("tr", [128, 1024], BF16, ph)
        self.ps_tr2 = self.ps("tr2", [128, 1024], BF16, ph)
        mzq = self.vop(lambda e: e.memset(qrTz[:, :], 0.0))
        ve_ev2, pe_tr2 = {}, {}
        qz = qrTz[:, :].rearrange("p (pb two c) -> p pb two c", pb=4, two=2)
        for sb in range(NSB):
            par = sb % 2
            pst = self.ps_tr if par == 0 else self.ps_tr2
            w = ([(VE, ve_ev2[sb - 2])] if sb - 2 in ve_ev2 else [])
            for pb in range(4):
                self.tr(pst[:, pb * 128:(pb + 1) * 128],
                        qrot[:, sb * 512 + pb * 128: sb * 512 + (pb + 1) * 128], inc=PE,
                        waits=(w if pb == 0 else ()))
            for pb in range(4):
                pe_tr2[sb] = self.tr(pst[:, 512 + pb * 128: 512 + (pb + 1) * 128],
                                     krot[:, sb * 512 + pb * 128: sb * 512 + (pb + 1) * 128],
                                     inc=PE)
            dqe = qz[0:64, :, 0, sb * 128:(sb + 1) * 128]
            dqo = qz[64:128, :, 1, sb * 128:(sb + 1) * 128]
            dk = krT[:, :].rearrange("p (pb c) -> p pb c", pb=4)[:, :, sb * 128:(sb + 1) * 128]
            sq_ = pst[:, 0:512].rearrange("p (k c) -> p k c", k=4)
            sk_ = pst[:, 512:1024].rearrange("p (k c) -> p k c", k=4)
            self.vop(lambda e, d=dqe, s=sq_: e.tensor_copy(d, s[0:64]),
                     waits=[(PE, pe_tr2[sb])] + ([(VE, mzq)] if sb == 0 else []))
            self.vop(lambda e, d=dqo, s=sq_: e.tensor_copy(d, s[64:128]))
            ve_ev2[sb] = self.vop(lambda e, d=dk, s=sk_: e.tensor_copy(d, s))
        self.barrier()


def emit_f3(self, bi, l, qrTz, krT, vxf, oT):
    self.mark(f'F3_b{bi}l{l}')
    VE, SE, PE, LDG, STS, GE = self.VE, self.SE, self.PE, self.LDG, self.STS, self.GE
    with contextlib.ExitStack() as ph:
        ps_bc = self.ps("f3_bc", [128, 1024], F32, ph)
        ptf = self.sb("f3_pt", [128, 8 * 512], BF16, ph)
        opv = self.sb("f3_op", [64, 2 * 512], BF16, ph)
        onrm = self.sb("f3_on", [64, 2 * 512], BF16, ph)
        csl = self.sb("f3_csl", [128, 128], BF16, ph)
        csu = self.sb("f3_csu", [128, 128], BF16, ph)
        rowA = self.sb("f3_rA", [1, 2 * 512], F32, ph)
        rowB = self.sb("f3_rB", [1, 2 * 512], BF16, ph)
        mz = self.vop(lambda e: e.memset(ptf[:, :], 0.0))
        self.vop(lambda e: e.tensor_scalar(csu[:, :], self.m_lt[:, :], 30000.0, -30000.0,
                                           ALU.mult, ALU.add))
        cs_v = self.vop(lambda e: e.tensor_scalar(csl[:, :], self.m_ut[:, :],
                                                  30000.0, -30000.0, ALU.mult, ALU.add))
        norm_fin, row_guard, bc_guard, on_guard, se_ex = {}, {}, {}, {}, {}
        sl_n = [0]
        units = []
        for w_ in range(4):
            for hp in range(4):
                for parity in range(2):
                    units.append((w_, hp, parity))

        def emit_S(w_, hp, parity, j):
            h = 2 * hp + parity
            a0 = max(0, j - 4)
            a1 = min(3, j)
            col0, n = a0 * 128, (a1 - a0 + 1) * 128
            kb_abs = 4 * (w_ - 1) + j
            sn = sl_n[0]
            sl_n[0] += 1
            sl = sn % 2
            pss = self.ps_a if sl == 0 else self.ps_b
            wt = [(VE, cs_v), (VE, mz)] if sn == 0 else []
            if sn - 2 in se_ex:
                wt.append((SE, se_ex[sn - 2]))
            self.mm(pss[:, 0:n],
                    krT[:, hp * S + kb_abs * 128: hp * S + kb_abs * 128 + 128],
                    qrTz[:, h * S + w_ * 512 + col0: h * S + w_ * 512 + col0 + n],
                    start=True, stop=False, waits=wt)
            if j >= 4:
                pem = self.mm(pss[:, 0:128], csu[:, :], self.idn[:, :],
                              start=False, stop=True, inc=PE)
            else:
                off = (a1 - a0) * 128
                pem = self.mm(pss[:, off:off + 128], csl[:, :], self.idn[:, :],
                              start=False, stop=True, inc=PE)
            ex = self.sop(lambda e, o=ptf[:, j * 512 + col0: j * 512 + col0 + n],
                          i=pss, n=n: e.activation(o, i[:, 0:n], AF.Exp, scale=8.0),
                          waits=[(PE, pem)])
            se_ex[sn] = ex
            return ex

        def emit_PV(it, w_, hp, parity, j, first, last, ex):
            h = 2 * hp + parity
            bank = it % 4
            kb_abs = 4 * (w_ - 1) + j
            wv = [(SE, ex)]
            if first and it - 4 in norm_fin:
                wv += norm_fin[it - 4]
            return self.mm(self.ps_big[:, bank * 512:(bank + 1) * 512],
                           vxf[:, kb_abs * 584 + h * 65: kb_abs * 584 + h * 65 + 128],
                           ptf[:, j * 512:(j + 1) * 512],
                           start=first, stop=last, inc=PE, waits=wv)

        def emit_norm(it, w_, hp, parity, pv):
            bank = it % 4
            rslot = it % 2
            ov = opv[:, rslot * 512:(rslot + 1) * 512]
            cp = self.vop(lambda e, ov=ov, bank=bank: e.tensor_copy(
                ov, self.ps_big[0:64, bank * 512:(bank + 1) * 512]), waits=[(PE, pv)])
            w = [(PE, pv)] + ([row_guard[rslot]] if rslot in row_guard else [])
            s1 = self.sop(lambda e, bank=bank, rslot=rslot: e.activation(
                rowA[0:1, rslot * 512:(rslot + 1) * 512],
                self.ps_big[64:65, bank * 512:(bank + 1) * 512], AF.Ln), waits=w)
            s2 = self.sop(lambda e, rslot=rslot: e.activation(
                rowB[0:1, rslot * 512:(rslot + 1) * 512],
                rowA[0:1, rslot * 512:(rslot + 1) * 512], AF.Exp, scale=-1.0))
            norm_fin[it] = [(VE, cp), (SE, s2)]
            wb = [(SE, s2)] + ([bc_guard[rslot]] if rslot in bc_guard else [])
            bc = self.mm(ps_bc[0:64, rslot * 512:(rslot + 1) * 512], self.ones1x64[:, :],
                         rowB[0:1, rslot * 512:(rslot + 1) * 512], start=True, stop=True,
                         inc=PE, waits=wb)
            row_guard[rslot] = (PE, bc)
            if parity == 0:
                tt = self.vop(lambda e, hp=hp, w_=w_, ov=ov, rslot=rslot: e.tensor_tensor(
                    oT[0:64, hp * S + w_ * 512: hp * S + (w_ + 1) * 512],
                    ov, ps_bc[0:64, rslot * 512:(rslot + 1) * 512], ALU.mult),
                    waits=[(PE, bc)])
                bc_guard[rslot] = (VE, tt)
            else:
                osl = (it // 2) % 2
                onv = onrm[:, osl * 512:(osl + 1) * 512]
                w2 = [(PE, bc)] + ([on_guard[osl]] if osl in on_guard else [])
                tt = self.vop(lambda e, onv=onv, ov=ov, rslot=rslot: e.tensor_tensor(
                    onv, ov, ps_bc[0:64, rslot * 512:(rslot + 1) * 512], ALU.mult), waits=w2)
                bc_guard[rslot] = (VE, tt)
                sv = self.sop(lambda e, hp=hp, w_=w_, onv=onv: e.activation(
                    oT[64:128, hp * S + w_ * 512: hp * S + (w_ + 1) * 512],
                    onv, AF.Copy), waits=[(VE, tt)])
                on_guard[osl] = (SE, sv)

        flat = []
        for ui, (w_, hp, parity) in enumerate(units):
            jlist = list(range(4, 8)) if w_ == 0 else list(range(8))
            for jn, j in enumerate(jlist):
                flat.append((ui, w_, hp, parity, j, jn == 0, jn == len(jlist) - 1))
        pend = None
        for rec in flat:
            ex = emit_S(rec[1], rec[2], rec[3], rec[4])
            if pend is not None:
                pv = emit_PV(pend[0], pend[1], pend[2], pend[3], pend[4], pend[5], pend[6],
                             pend[7])
                if pend[6]:
                    emit_norm(pend[0], pend[1], pend[2], pend[3], pv)
            pend = rec + (ex,)
        pv = emit_PV(pend[0], pend[1], pend[2], pend[3], pend[4], pend[5], pend[6], pend[7])
        emit_norm(pend[0], pend[1], pend[2], pend[3], pv)
        self.barrier()


def emit_f4(self, bi, l, oT):
    self.mark(f'F4_b{bi}l{l}')
    VE, SE, PE, LDG, STS, GE = self.VE, self.SE, self.PE, self.LDG, self.STS, self.GE
    with contextlib.ExitStack() as ph:
        self.ps_tr = self.ps("tr", [128, 1024], BF16, ph)
        self.ps_tr2 = self.ps("tr2", [128, 1024], BF16, ph)
        wo = self.sb("f4_wo", [128, 4 * D], BF16, ph)
        self.lnjunk = self.sb("f4_junk", [128, 1024], F32, ph)
        ut = self.sb("f4_u", [128, 4 * D], F32, ph)
        xft = self.sb("f4_xf", [128, 4 * D], F32, ph)
        shv4 = self.sb("f4_sh", [128, 4 * 512], F32, ph)
        h2nb = self.sb("f4_hn", [128, 2 * D], BF16, ph)
        lngb = self.sb("f4_lg", [128, D], BF16, ph)
        lnbb = self.sb("f4_lb", [128, D], BF16, ph)
        row2 = self.sb("f4_r2", [128, 4 * 1024], F32, ph)
        row3 = self.sb("f4_r3", [128, 2 * 1024], F32, ph)
        st4 = self.sb("f4_st", [128, 64], F32, ph)
        t_woc = self.dma_w3c(wo[:, :], self.FWO[l], 4, D)
        t_lg = self.dma_g(lngb[:, :], self.FLG[l][None, :].partition_broadcast(128))
        t_lb = self.dma_g(lnbb[:, :], self.FLB[l][None, :].partition_broadcast(128))
        ve_u, st_u, se_ap, ve_hn, pe_tr, ve_ev, sh_pem, cp_sh = {}, {}, {}, {}, {}, {}, {}, {}
        for sb in range(NSB):
            par = sb % 2
            p4 = sb % 4
            wx = [(VE, ve_u[sb - 4])] if sb - 4 in ve_u else []
            ldx = self.dma_g(xft[:, p4 * D:(p4 + 1) * D],
                             self.f0_s[:, sb * D:(sb + 1) * D], waits=wx)
            psy = self.ps_big[:, par * 1024:(par + 1) * 1024]
            w = ([(VE, ve_u[sb - 2])] if sb - 2 in ve_u else [])
            pem = None
            for pb in range(4):
                for nn in range(2):
                    wdb = ([t_woc[pb]] if sb == 0 and nn == 0 else [])
                    pem = self.mm(psy[:, nn * 512:(nn + 1) * 512],
                                  oT[:, pb * S + sb * 128: pb * S + sb * 128 + 128],
                                  wo[:, pb * D + nn * 512: pb * D + (nn + 1) * 512],
                                  start=(pb == 0), stop=(pb == 3),
                                  inc=(PE if pb == 3 and nn == 1 else None),
                                  waits=((w + wdb) if pb == 0 and nn == 0 else wdb))
            uv = ut[:, p4 * D:(p4 + 1) * D]
            w = [(PE, pem), ldx]
            if sb - 4 in st_u:
                w += [st_u[sb - 4], (SE, se_ap[sb - 4])]
            ve_u[sb] = self.vop(lambda e, uv=uv, xf=xft[:, p4 * D:(p4 + 1) * D], psy=psy:
                                e.tensor_tensor(uv, psy, xf, ALU.add), waits=w)
            st_u[sb] = self.dma_s(self.u_s[:, sb * D:(sb + 1) * D], uv, waits=[(VE, ve_u[sb])])
            psv = self.ps_a if par == 0 else self.ps_b
            w = [(VE, ve_u[sb])] + ([(VE, cp_sh[sb - 2])] if sb - 2 in cp_sh else [])
            m1 = self.mm(psv[:, :], self.shiftT[:, :], uv[:, 512:1024],
                         start=True, stop=(sb == 0), inc=(PE if sb == 0 else None),
                         waits=w)
            if sb > 0:
                pu = ut[:, ((sb - 1) % 4) * D:((sb - 1) % 4) * D + D]
                m1 = self.mm(psv[:, :], self.e127T[:, :], pu[:, 512:1024],
                             start=False, stop=True, inc=PE)
            sh_pem[sb] = m1
            shp = shv4[:, p4 * 512:(p4 + 1) * 512]
            cp_sh[sb] = self.vop(lambda e, shp=shp, psv=psv: e.tensor_copy(shp, psv[:, :]),
                                 waits=[(PE, m1)])
            c = p4 * 16
            st = st4
            junk = self.lnjunk
            w = [(VE, cp_sh[sb])]
            self.sop(lambda e, uv=uv, c=c, j=junk: e.activation(
                j[:, 0:512], uv[:, 0:512], AF.Identity,
                accum_out=st[:, c + 6:c + 7]), waits=w)
            self.sop(lambda e, shp=shp, c=c, j=junk: e.activation(
                j[:, 0:512], shp, AF.Identity, accum_out=st[:, c + 8:c + 9]))
            self.sop(lambda e, uv=uv, c=c, j=junk: e.activation(
                j[:, 0:512], uv[:, 0:512], AF.Square,
                accum_out=st[:, c + 7:c + 8]))
            s2 = self.sop(lambda e, shp=shp, c=c, j=junk: e.activation(
                j[:, 0:512], shp, AF.Square, accum_out=st[:, c + 9:c + 10]))
            v1 = self.vop(lambda e, c=c: e.tensor_tensor(
                st[:, c + 10:c + 11], st[:, c + 6:c + 7], st[:, c + 8:c + 9], ALU.add),
                waits=[(SE, s2)])
            v1 = self.vop(lambda e, c=c: e.tensor_tensor(
                st[:, c + 11:c + 12], st[:, c + 7:c + 8], st[:, c + 9:c + 10], ALU.add),
                waits=[(VE, v1)])
            v1 = self.vop(lambda e, c=c: e.tensor_scalar(st[:, c:c + 1], st[:, c + 10:c + 11],
                                                         1.0 / 1024, None, ALU.mult),
                          waits=[(VE, v1)])
            v1 = self.vop(lambda e, c=c: e.tensor_tensor(st[:, c + 2:c + 3], st[:, c:c + 1],
                                                         st[:, c:c + 1], ALU.mult),
                          waits=[(VE, v1)])
            v1 = self.vop(lambda e, c=c: e.scalar_tensor_tensor(
                st[:, c + 1:c + 2], st[:, c + 11:c + 12], 1.0 / 1024, st[:, c + 2:c + 3],
                ALU.mult, ALU.subtract), waits=[(VE, v1)])
            s3 = self.sop(lambda e, c=c: e.activation(st[:, c + 3:c + 4], st[:, c + 1:c + 2],
                                                      AF.Ln, bias=self.cst[:, 0:1]),
                          waits=[(VE, v1)])
            s2 = self.sop(lambda e, c=c: e.activation(st[:, c + 4:c + 5], st[:, c + 3:c + 4],
                                                      AF.Exp, scale=-0.5), waits=[(SE, s3)])
            tv2 = self.vop(lambda e, c=c: e.scalar_tensor_tensor(
                st[:, c + 5:c + 6], st[:, c:c + 1], -1.0, st[:, c + 4:c + 5],
                ALU.mult, ALU.mult), waits=[(SE, s2)])
            hv = h2nb[:, par * D:(par + 1) * D]
            r2v = row2[:, p4 * D:(p4 + 1) * D]
            r3v = row3[:, par * D:(par + 1) * D]
            w = [(VE, tv2), t_lg, t_lb]
            if sb - 4 in ve_hn:
                w.append((VE, ve_hn[sb - 4]))
            self.sop(lambda e, uv=uv, c=c, r2v=r2v: e.activation(
                r2v[:, 0:512], uv[:, 0:512], AF.Identity,
                scale=st[:, c + 4:c + 5], bias=st[:, c + 5:c + 6]), waits=w)
            se_ap[sb] = self.sop(lambda e, shp=shp, c=c, r2v=r2v: e.activation(
                r2v[:, 512:1024], shp, AF.Identity,
                scale=st[:, c + 4:c + 5], bias=st[:, c + 5:c + 6]))
            self.vop(lambda e, r2v=r2v, r3v=r3v: e.tensor_tensor(
                r3v, r2v, lngb[:, :], ALU.mult), waits=[(SE, se_ap[sb])])
            ve_hn[sb] = self.vop(lambda e, hv=hv, r3v=r3v: e.tensor_tensor(
                hv, r3v, lnbb[:, :], ALU.add))
            pst = self.ps_tr if par == 0 else self.ps_tr2
            w = [(VE, ve_hn[sb])] + ([(VE, ve_ev[sb - 2])] if sb - 2 in ve_ev else [])
            for db in range(NDB):
                pe_tr[sb] = self.tr(pst[:, db * 128:(db + 1) * 128],
                                    hv[:, db * 128:(db + 1) * 128], inc=PE,
                                    waits=(w if db == 0 else ()))
            dst = self.tT[:, :].rearrange("p (db c) -> p db c", db=NDB)[:, :, 1 + sb * 128: 129 + sb * 128]
            src = pst[:, :].rearrange("p (k c) -> p k c", k=NDB)
            ve_ev[sb] = self.vop(lambda e, d=dst, s=src: e.tensor_copy(d, s),
                                 waits=[(PE, pe_tr[sb])])
        self.barrier()


def emit_f56(self, bi, l):
    self.mark(f'F5_b{bi}l{l}')
    VE, SE, PE, LDG, STS, GE = self.VE, self.SE, self.PE, self.LDG, self.STS, self.GE
    with contextlib.ExitStack() as phA:
        hT = self.sb("f5_hT", [128, NJ * S], BF16, phA)
        w2b = self.sb("f5_w2", [128, NJ * D], BF16, phA)
        b2b = self.sb("f5_b2", [128, D], F32, phA)
        ph = contextlib.ExitStack()
        ph.__enter__()
        w1a = self.sb("f5_wa", [128, 2 * NDB * 128], BF16, ph)
        w1g = self.sb("f5_wg", [128, 2 * NDB * 128], BF16, ph)
        b1a = self.sb("f5_ba", [128, NJ], F32, ph)
        b1g = self.sb("f5_bg", [128, NJ], F32, ph)
        sga = self.sb("f5_sg", [128, 2 * 512], F32, ph)
        saa = self.sb("f5_sa", [128, 2 * 512], F32, ph)

        t_ba = self.dma_g(b1a[:, 0:21], self.FB1[l][0:2688].rearrange("(j p) -> p j", p=128),
                          slow=True)
        t_ba = self.dma_g(b1a[0:42, 21:22], self.FB1[l][2688:2730][:, None], slow=True,
                          waits=[t_ba])
        t_bg = self.dma_g(b1g[:, 0:21], self.FB1[l][2730:5418].rearrange("(j p) -> p j", p=128),
                          slow=True)
        t_bg = self.dma_g(b1g[0:42, 21:22], self.FB1[l][5418:5460][:, None], slow=True,
                          waits=[t_bg])
        t_b2 = self.dma_g(b2b[:, :], self.FB2[l][None, :].partition_broadcast(128))
        t_w2 = self.dma_g(w2b[:, 0:21 * D], self.fw2_c[l][:, 0:21 * D])
        t_w2 = self.dma_g(w2b[0:42, 21 * D:22 * D], self.fw2_c[l][0:42, 21 * D:22 * D],
                          waits=[t_w2])
        w1v = self.fw1_c[l].rearrange("p (db o) -> p db o", db=NDB)
        pe_j, ve_h, t_w1 = {}, {}, {}

        def load_w1(j):
            par = j % 2
            mj = 128 if j < 21 else 42
            wt = [(PE, pe_j[j - 2])] if j - 2 in pe_j else []
            dsta = w1a[:, par * 1024:par * 1024 + 1024].rearrange("p (db o) -> p db o", db=NDB)
            dstg = w1g[:, par * 1024:par * 1024 + 1024].rearrange("p (db o) -> p db o", db=NDB)
            ta = self.dma_g(dsta[:, :, 0:mj], w1v[:, :, 128 * j:128 * j + mj], waits=wt)
            return self.dma_g(dstg[:, :, 0:mj], w1v[:, :, 2730 + 128 * j:2730 + 128 * j + mj],
                              waits=[ta])

        t_w1[0] = load_w1(0)
        for j in range(NJ):
            par = j % 2
            mj = 128 if j < 21 else 42
            if j + 1 < NJ:
                t_w1[j + 1] = load_w1(j + 1)
            for ch in range(4):
                ba = ch % 2
                psa = self.ps_big[0:128, ba * 512:(ba + 1) * 512]
                psg = self.ps_big[0:128, (2 + ba) * 512:(3 + ba) * 512]
                w = [t_w1[j], t_ba, t_bg]
                key = j * 4 + ch - 2
                if key in ve_h:
                    w.append((VE, ve_h[key]))
                for db in range(NDB):
                    self.mm(psa[0:mj, :], w1a[:, par * 1024 + db * 128: par * 1024 + db * 128 + mj],
                            self.tTv(db, ch * 512, 512), start=(db == 0), stop=(db == NDB - 1),
                            waits=(w if db == 0 else ()))
                pem = None
                for db in range(NDB):
                    pem = self.mm(psg[0:mj, :],
                                  w1g[:, par * 1024 + db * 128: par * 1024 + db * 128 + mj],
                                  self.tTv(db, ch * 512, 512), start=(db == 0),
                                  stop=(db == NDB - 1), inc=(PE if db == NDB - 1 else None))
                pe_j[j] = pem
                sgv = sga[0:mj, ba * 512:ba * 512 + 512]
                sav = saa[0:mj, ba * 512:ba * 512 + 512]
                sg_ = self.sop(lambda e, o=sgv, i=psg[0:mj, :], j=j, mj=mj: e.activation(
                    o, i, AF.Gelu, bias=b1g[0:mj, j:j + 1]), waits=[(PE, pem)])
                self.vop(lambda e, o=sav, i=psa[0:mj, :], j=j, mj=mj: e.tensor_scalar(
                    o, i, b1a[0:mj, j:j + 1], None, ALU.add), waits=[(PE, pem)])
                ve_h[j * 4 + ch] = self.vop(
                    lambda e, o=hT[0:mj, j * S + ch * 512: j * S + ch * 512 + 512],
                    a=sav, g=sgv: e.tensor_tensor(o, a, g, ALU.mult), waits=[(SE, sg_)])
        self.barrier(producers=("vector",), consumers=("tensor",))
        self.barrier()
        ph.__exit__(None, None, None)

        self.mark(f'F6_b{bi}l{l}')
        ut2 = self.sb("f6_u", [128, 2 * D], F32, phA)
        xsc = self.sb("f6_x", [128, 2 * D], F32, phA)
        xout = self.sb("f6_o", [128, 2 * D], F32, phA)
        ve_x, st_x = {}, {}
        for sb in range(NSB):
            par = sb % 2
            w = [(VE, ve_x[sb - 2])] if sb - 2 in ve_x else []
            tld = self.dma_g(ut2[:, par * D:(par + 1) * D], self.u_s[:, sb * D:(sb + 1) * D],
                             waits=w)
            psy = self.ps_big[:, par * 1024:(par + 1) * 1024]
            w = [t_w2] + ([(VE, ve_x[sb - 2])] if sb - 2 in ve_x else [])
            pem = None
            for j in range(NJ):
                mj = 128 if j < 21 else 42
                for nn in range(2):
                    pem = self.mm(psy[:, nn * 512:(nn + 1) * 512],
                                  hT[0:mj, j * S + sb * 128: j * S + sb * 128 + 128],
                                  w2b[0:mj, j * D + nn * 512: j * D + (nn + 1) * 512],
                                  start=(j == 0), stop=(j == NJ - 1),
                                  inc=(PE if j == NJ - 1 and nn == 1 else None),
                                  waits=(w if j == 0 and nn == 0 else ()))
            w = [(PE, pem), t_b2, tld]
            if sb - 2 in st_x:
                w.append(st_x[sb - 2])
            self.vop(lambda e, par=par, psy=psy: e.tensor_tensor(
                xsc[:, par * D:(par + 1) * D], psy, b2b[:, :], ALU.add),
                waits=w)
            ve_x[sb] = self.vop(lambda e, par=par: e.tensor_tensor(
                xout[:, par * D:(par + 1) * D], xsc[:, par * D:(par + 1) * D],
                ut2[:, par * D:(par + 1) * D], ALU.add))
            st_x[sb] = self.dma_s(self.f0_s[:, sb * D:(sb + 1) * D],
                                  xout[:, par * D:(par + 1) * D], waits=[(VE, ve_x[sb])])
        self.barrier()


def emit_final(self, bi):
    self.mark(f'FIN_b{bi}')
    VE, SE, PE, LDG, STS = self.VE, self.SE, self.PE, self.LDG, self.STS
    with contextlib.ExitStack() as ph:
        z1t = self.sb("fin_z1", [128, 4 * D], F32, ph)
        self.lnjunk = self.sb("fin_junk", [128, 1024], F32, ph)
        xft = self.sb("fin_xf", [128, 4 * D], F32, ph)
        ot = self.sb("fin_o", [128, 4 * D], F32, ph)
        osc = self.sb("fin_s", [128, 4 * D], F32, ph)
        st_o, ve_o, se_a = {}, {}, {}
        for sb in range(NSB):
            p4 = sb % 4
            w = [(VE, ve_o[sb - 4])] if sb - 4 in ve_o else []
            tld = self.dma_g(z1t[:, p4 * D:(p4 + 1) * D],
                             self.z1_s[:, sb * D:(sb + 1) * D], waits=w)
            w = [(SE, se_a[sb - 4])] if sb - 4 in se_a else []
            tlx = self.dma_g(xft[:, p4 * D:(p4 + 1) * D],
                             self.f0_s[:, sb * D:(sb + 1) * D], waits=w)
            c = p4 * 16
            xfv = xft[:, p4 * D:(p4 + 1) * D]
            emit_ln_stats(self, c, xfv, pe_wait=tlx)
            w = [(VE, self.VE.v)] + ([st_o[sb - 4]] if sb - 4 in st_o else [])
            se_a[sb] = self.sop(lambda e, o=osc[:, p4 * D:(p4 + 1) * D], i=xfv, c=c:
                                e.activation(o, i, AF.Identity, scale=self.stat[:, c + 4:c + 5],
                                             bias=self.stat[:, c + 5:c + 6]), waits=w)
            ve_o[sb] = self.vop(lambda e, p4=p4: e.tensor_tensor(
                ot[:, p4 * D:(p4 + 1) * D], osc[:, p4 * D:(p4 + 1) * D],
                z1t[:, p4 * D:(p4 + 1) * D], ALU.add),
                waits=[(SE, se_a[sb]), tld])
            dst = self.Y[bi].rearrange("(sb p) d -> p sb d", p=128)[:, sb, :]
            st_o[sb] = self.dma_s(dst, ot[:, p4 * D:(p4 + 1) * D],
                                  waits=[(VE, ve_o[sb])])
        self.barrier()


"""Megakernel builder for mk_sandwich_d1 (16x2048x1024 sandwich block).

Layouts:
 - normal: [128, NSB*1024], col = sb*1024 + d (token = sb*128 + p)
 - ^T:     [128, NDB*2049], col = db*2049 + 1 + s (d = db*128 + p); col db*2049
           is a zero column so a col-offset of -1 implements token-shift.
Outer attention: S^T (k on partitions), no max subtraction, exp PSUM->bf16 P^T,
softmax denominators via ones column (idx 64) in V_ext, normalization after PV
via reciprocal + split-bf16 K=1 PE broadcast. All matmuls bf16 / fp32 PSUM.
"""
import contextlib
import numpy as np
import ml_dtypes
import concourse.bass as bass
import concourse.mybir as mybir
from concourse.bass_utils import run_bass_kernel_spmd

F32 = mybir.dt.float32
BF16 = mybir.dt.bfloat16
AF = mybir.ActivationFunctionType
ALU = mybir.AluOpType
AX = mybir.AxisListType

S, D, H, HD = 2048, 1024, 16, 64
NSB, NDB, HP = 16, 8, 8
TCOL = NDB * 2049
FH, FHD, FIN, WIN = 8, 64, 512, 512
HID, HID2 = 2730, 5460
NJ = 22
SCALE = 1.0 / 8.0
QK_SCALE = 8.0
ENGINES = ("tensor", "vector", "scalar", "gpsimd", "sync")


def host_constants():
    bf = ml_dtypes.bfloat16
    c = {}
    c["IDN"] = np.eye(128, dtype=np.float32).astype(bf)
    c["ONES64"] = np.ones((128, 1), np.float32).astype(bf)
    c["ONES1x64"] = np.ones((1, 64), np.float32).astype(bf)
    sel0 = np.zeros((1, 128), np.float32); sel0[0, :64] = 1.0
    sel1 = np.zeros((1, 128), np.float32); sel1[0, 64:] = 1.0
    c["SEL0"], c["SEL1"] = sel0.astype(bf), sel1.astype(bf)
    r = np.arange(128)
    c["M_UT"] = (r[:, None] <= r[None, :]).astype(np.float32).astype(bf)
    c["M_LT"] = (r[:, None] >= r[None, :]).astype(np.float32).astype(bf)
    c["BD"] = ((r[:, None] < 64) == (r[None, :] < 64)).astype(np.float32).astype(bf)
    c["SHIFTT"] = (r[None, :] == r[:, None] + 1).astype(np.float32)
    e127 = np.zeros((128, 128), np.float32); e127[127, 0] = 1.0
    c["E127T"] = e127
    n = 2048
    half = np.arange(0, FHD, 2, dtype=np.float64)
    inv_freq = 1.0 / 10000.0 ** (half / FHD)
    t = np.arange(n, dtype=np.float64)
    freqs = np.concatenate([t[:, None] * inv_freq[None, :]] * 2, axis=-1)
    base = (half + 0.4 * FHD) / (1.4 * FHD)
    power = (t - n // 2) / float(WIN // 2)
    scale = np.concatenate([base[None, :] ** power[:, None]] * 2, axis=-1)
    cosv, sinv = np.cos(freqs), np.sin(freqs)
    sin_signed = sinv.copy()
    sin_signed[:, :32] *= -1.0
    c["RQC"] = np.tile((cosv * scale).astype(np.float32), (1, FH)).astype(bf)
    c["RQS"] = np.tile((sin_signed * scale).astype(np.float32), (1, FH)).astype(bf)
    c["RKC"] = np.tile((cosv / scale).astype(np.float32), (1, FH)).astype(bf)
    c["RKS"] = np.tile((sin_signed / scale).astype(np.float32), (1, FH)).astype(bf)
    return c


class Sem:
    def __init__(self, h):
        self.h, self.v = h, 0

    def inc(self, n=1):
        self.v += n
        return self.v


class MK:
    def __init__(self, cfg=None):
        self.cfg = cfg or {}
        self.nc = bass.Bass("TRN2", target_bir_lowering=False)
        self.ops = {e: [] for e in ENGINES}
        self.tap_t = {}

    def emit(self, eng, fn, waits=(), inc=None, inc_n=1):
        tgt = inc.inc(inc_n) if inc is not None else None
        wl = [(s.h, v) for s, v in waits if v > 0]

        def run(e, fn=fn, wl=wl, inc=inc, inc_n=inc_n):
            for h, v in wl:
                e.wait_ge(h, v)
            inst = fn(e)
            if inc is not None:
                inst.then_inc(inc.h, inc_n)
        self.ops[eng].append(run)
        return tgt

    def wait(self, eng, *sems_vals):
        wl = [(s.h, v) for s, v in sems_vals if v > 0]
        if wl:
            self.ops[eng].append(lambda e, wl=wl: [e.wait_ge(h, v) for h, v in wl])

    def _dma(self, eng, ring, idx_attr, dst, src, waits, slow=False):
        i = getattr(self, idx_attr)
        setattr(self, idx_attr, i + 1)
        s = ring[i % len(ring)]
        pre = ([(s, s.v)] if s.v > 0 else []) + [t for t in waits if t[1] > 0]
        tgt = self.emit(eng, lambda e, d=dst, sr=src, sl=slow: e.dma_start(
            out=d, in_=sr, allow_slow_non_contiguous=sl), waits=pre, inc=s, inc_n=16)
        return (s, tgt)

    def dma_g(self, dst, src, waits=(), slow=False):
        return self._dma("gpsimd", self.LDGr, "_ldg_i", dst, src, waits, slow)

    def dma_s(self, dst, src, waits=()):
        return self._dma("sync", self.STSr, "_sts_i", dst, src, waits)

    def dma_w3(self, dst, w, nk, waits=()):
        return self.dma_g(dst.rearrange("p (db o) -> p db o", db=nk),
                          w.rearrange("(db p) o -> p db o", p=128), waits=waits)

    def dma_w3c(self, dst, w, nk, dout, waits=()):
        src = w.rearrange("(db p) o -> p db o", p=128)
        return [self.dma_g(dst[:, db * dout:(db + 1) * dout], src[:, db, :],
                           waits=(waits if db == 0 else ())) for db in range(nk)]

    def mark(self, name):
        if not hasattr(self, "_marks"):
            self._marks = []
            self._marks_e = []
        self._marks.append((name, getattr(self, "_mmn", 0)))
        self._marks_e.append((name, {e: len(self.ops[e]) for e in ENGINES}))

    def barrier(self, producers=ENGINES, consumers=ENGINES):
        tg = []
        for e in producers:
            if e == "sync":
                tg += [(s, s.v) for s in self.STSr]
            elif e == "gpsimd":
                tg.append((self.GE, self.GE.v))
                tg += [(s, s.v) for s in self.LDGr]
            else:
                tg.append((self.SEMS[e], self.SEMS[e].v))
        for e in consumers:
            self.wait(e, *tg)

    def _count_mm(self):
        self._mmn = getattr(self, "_mmn", 0) + 1

    def tr(self, out, in_, inc=None, waits=()):
        self._count_mm()
        return self.emit(
            "tensor",
            lambda e, o=out, i=in_: e.transpose(o, i, self.idn[:, :]),
            waits=waits, inc=inc)

    def mm(self, out, lhsT, rhs, start, stop, inc=None, tile_position=None, waits=()):
        self._count_mm()
        return self.emit(
            "tensor",
            lambda e, o=out, l=lhsT, r=rhs, st=start, sp=stop, tp=tile_position:
                e.matmul(o, l, r, start=st, stop=sp, tile_position=tp, skip_group_check=True),
            waits=waits, inc=inc)

    def vop(self, fn, waits=(), inc=None):
        sem = inc if inc is not None else self.VE
        w = ([(self.VE, self.VE.v)] if self.VE.v > 0 else []) + list(waits)
        return self.emit("vector", fn, waits=w, inc=sem)

    def sop(self, fn, waits=(), inc=None):
        sem = inc if inc is not None else self.SE
        w = ([(self.SE, self.SE.v)] if self.SE.v > 0 else []) + list(waits)
        return self.emit("scalar", fn, waits=w, inc=sem)

    def gop(self, fn, waits=(), inc=None):
        sem = inc if inc is not None else self.GE
        w = ([(self.GE, self.GE.v)] if self.GE.v > 0 else []) + list(waits)
        return self.emit("gpsimd", fn, waits=w, inc=sem)

    def tap(self, name, sbuf_ap, waits=()):
        if name in self.tap_t:
            self.dma_s(self.tap_t[name], sbuf_ap, waits=waits)

    def build(self):
        nc = self.nc
        cfg = self.cfg
        self.batches = cfg.get("batches", [0, 1])

        def din(name, shape, dt=F32):
            return nc.dram_tensor(name, list(shape), dt, kind="ExternalInput").ap()

        self.X = din("x", [2, S, D])
        self.Wq, self.Wk, self.Wv, self.Wo = (din("wq", [D, D]), din("wk", [D, D]),
                                              din("wv", [D, D]), din("wo", [D, D]))
        self.Wqp, self.Wkp = din("wqp", [D, D]), din("wkp", [D, D])
        self.LRAW = din("lraw", [1, H])
        self.FWQKV = din("fwqkv", [2, D, 3 * FIN])
        self.FQS, self.FKS = din("fqs", [2, FHD]), din("fks", [2, FHD])
        self.FWO = din("fwo", [2, FIN, D])
        self.FLG, self.FLB = din("flg", [2, D]), din("flb", [2, D])
        self.FW1, self.FB1 = din("fw1", [2, D, HID2]), din("fb1", [2, HID2])
        self.FW2, self.FB2 = din("fw2", [2, HID, D]), din("fb2", [2, D])
        for cname, cshape in (("IDN", [128, 128]), ("ONES64", [128, 1]), ("ONES1x64", [1, 64]),
                              ("SEL0", [1, 128]), ("SEL1", [1, 128]),
                              ("M_UT", [128, 128]), ("M_LT", [128, 128]),
                              ("BD", [128, 128]),
                              ("RQC", [S, 512]), ("RQS", [S, 512]),
                              ("RKC", [S, 512]), ("RKS", [S, 512])):
            setattr(self, "c" + cname, din(cname, cshape, BF16))
        self.cSHIFTT = din("SHIFTT", [128, 128], F32)
        self.cE127T = din("E127T", [128, 128], F32)
        self.Y = nc.dram_tensor("y", [2, S, D], F32, kind="ExternalOutput").ap()
        for tname, tshape, tdt in cfg.get("taps", []):
            self.tap_t[tname] = nc.dram_tensor("tap_" + tname, list(tshape), tdt,
                                               kind="ExternalOutput").ap()

        def scr(name, shape, dt):
            kind = "ExternalOutput" if name in cfg.get("dbg_out", ()) else "Internal"
            return nc.dram_tensor(name, list(shape), dt, kind=kind).ap()

        self.fw1_c = scr("fw1_c", [2, 128, NDB * HID2], BF16)
        self.fw2_c = scr("fw2_c", [2, 128, NJ * D], BF16)
        self.lam_d = scr("lam_d", [1, H], F32)
        self.qd_s = scr("qd_s", [128, HP * S], BF16)
        self.qc_s = scr("qc_s", [128, HP * S], BF16)
        self.qs_s = scr("qs_s", [128, HP * S], BF16)
        self.kd_s = scr("kd_s", [128, HP * S], BF16)
        self.kc_s = scr("kc_s", [128, HP * S], BF16)
        self.ks_s = scr("ks_s", [128, HP * S], BF16)
        self.vx_s = scr("vx_s", [128, NSB * H * 65], BF16)
        self.z1_s = scr("z1_s", [128, NSB * D], F32)
        self.f0_s = scr("f0_s", [128, NSB * D], F32)
        self.rbc_d = scr("rbc_d", [1, 4096], F32)
        self.u_s = scr("u_s", [128, NSB * D], F32)

        es = contextlib.ExitStack()
        self.es = es
        with es:
            self._nm = [0]
            def _sb(name, shape, dt=F32, stack=es):
                self._nm[0] += 1
                return stack.enter_context(
                    nc.sbuf_tensor(f"{name}_{self._nm[0]}", list(shape), dt))
            self.sb = _sb
            def _ps(name, shape, dt=F32, stack=es):
                self._nm[0] += 1
                return stack.enter_context(
                    nc.psum_tensor(f"{name}_{self._nm[0]}", list(shape), dt))
            self.ps = _ps
            self.ps_big = es.enter_context(nc.psum_tensor("ps_big", [128, 2048], F32))
            self.LDGr = [Sem(es.enter_context(nc.semaphore(f"ldg{i}"))) for i in range(8)]
            self.STSr = [Sem(es.enter_context(nc.semaphore(f"sts{i}"))) for i in range(8)]
            self._ldg_i = 0
            self._sts_i = 0
            self.LDG = self.LDGr[0]
            self.STS = self.STSr[0]
            self.PE = Sem(es.enter_context(nc.semaphore("pe")))
            self.VE = Sem(es.enter_context(nc.semaphore("ve")))
            self.SE = Sem(es.enter_context(nc.semaphore("se")))
            self.GE = Sem(es.enter_context(nc.semaphore("ge")))
            self.SEMS = dict(tensor=self.PE, vector=self.VE, scalar=self.SE,
                             gpsimd=self.GE)

            self.alloc_persistent()
            self.prologue()
            for bi in self.batches:
                self.emit_attn(bi)
                if not cfg.get("attn_only"):
                    for l in (0, 1):
                        self.emit_ffn(bi, l)
                    self.emit_final(bi)
            self.barrier(producers=ENGINES, consumers=("sync",))
            self.wait("sync", *[(s, s.v) for s in self.STSr])

            block = es.enter_context(nc.Block())
            for eng in ENGINES:
                def _replay(e, eng=eng):
                    ops = self.ops[eng]
                    bounds = [(nm, d[eng]) for nm, d in getattr(self, "_marks_e", [])]
                    idx = 0
                    for k, (nm, start) in enumerate(bounds):
                        while idx < start:
                            ops[idx](e)
                            idx += 1
                        end = bounds[k + 1][1] if k + 1 < len(bounds) else len(ops)
                        with nc.named_scope(nm):
                            while idx < end:
                                ops[idx](e)
                                idx += 1
                    while idx < len(ops):
                        ops[idx](e)
                        idx += 1
                getattr(block, eng)(_replay)
        return nc

    def alloc_persistent(self):
        sb = self.sb
        self.idn = sb("idn", [128, 128], BF16)
        self.ones64 = sb("ones64", [128, 1], BF16)
        self.ones1x64 = sb("ones1x64", [1, 64], BF16)
        self.sel0 = sb("sel0", [1, 128], BF16)
        self.sel1 = sb("sel1", [1, 128], BF16)
        self.m_ut = sb("m_ut", [128, 128], BF16)
        self.m_lt = sb("m_lt", [128, 128], BF16)
        self.bd = sb("bd", [128, 128], BF16)
        self.shiftT = sb("shiftT", [128, 128], F32)
        self.e127T = sb("e127T", [128, 128], F32)
        self.lamcol = sb("lamcol", [128, 8])
        self.lamq = sb("lamq", [1, H])
        self.cst = sb("cst", [128, 4])
        self.lnjunk = None
        self.stat = sb("stat", [128, 96])
        self.row = None
        self.rowb = None
        self.tT = None

    def tTv(self, db, s0, n):
        return self.tT[:, db * 2049 + 1 + s0: db * 2049 + 1 + s0 + n]

    def tTs(self, db, s0, n):
        return self.tT[:, db * 2049 + s0: db * 2049 + s0 + n]

    def prologue(self):
        sb, nc = self.sb, self.nc
        import math as _m
        self.vop(lambda e: e.memset(self.cst[:, 0:1], 1e-5))
        self.vop(lambda e: e.memset(self.cst[:, 1:2], 1.0))
        self.vop(lambda e: e.memset(self.cst[:, 2:3], _m.pi / 2))
        self.vop(lambda e: e.memset(self.cst[:, 3:4], 1e-6))
        for dst, src in ((self.idn, self.cIDN), (self.ones64, self.cONES64),
                         (self.ones1x64, self.cONES1x64), (self.sel0, self.cSEL0),
                         (self.sel1, self.cSEL1), (self.m_ut, self.cM_UT),
                         (self.m_lt, self.cM_LT), (self.bd, self.cBD),
                         (self.shiftT, self.cSHIFTT), (self.e127T, self.cE127T)):
            self.dma_g(dst[:, :], src)

        with contextlib.ExitStack() as ph:
            lr = self.sb("lr", [1, H], F32, ph)
            lr2 = self.sb("lr2", [1, H], F32, ph)
            t_ld = self.dma_g(lr[:, :], self.LRAW)
            t_tan = self.sop(lambda e: e.activation(lr2[:, :], lr[:, :], AF.Tanh, scale=1.0),
                             waits=[t_ld])
            t_lv = self.vop(lambda e: e.tensor_scalar(self.lamq[:, :], lr2[:, :], 0.5,
                                                      None, ALU.mult),
                            waits=[(self.SE, t_tan)])
            t_ls = self.dma_s(self.lam_d, self.lamq[:, :], waits=[(self.VE, t_lv)])
            lam2d = self.lam_d.rearrange("one (ob two) -> one ob two", two=2)
            t_l1 = self.dma_g(self.lamcol[0:64, :], lam2d[0:1, :, 0].partition_broadcast(64),
                              waits=[t_ls], slow=True)
            self.dma_g(self.lamcol[64:128, :], lam2d[0:1, :, 1].partition_broadcast(64),
                       waits=[t_l1], slow=True)

            self.barrier(producers=("sync", "gpsimd", "vector", "scalar"), consumers=ENGINES)



def shard_inputs(inputs):
    c = host_constants()
    xs = np.ascontiguousarray(inputs["L0_attn_x"])
    maps = []
    for core in range(8):
        m = dict(
            x=xs[2 * core:2 * core + 2],
            wq=inputs["L0_attn_Wq"], wk=inputs["L0_attn_Wk"], wv=inputs["L0_attn_Wv"],
            wo=inputs["L0_attn_Wo"], wqp=inputs["L0_attn_Wqp"], wkp=inputs["L0_attn_Wkp"],
            lraw=inputs["L0_attn_lambda_raw"].reshape(1, H),
            fwqkv=inputs["L0_ffn_wqkv"], fqs=inputs["L0_ffn_q_scale"],
            fks=inputs["L0_ffn_k_scale"], fwo=inputs["L0_ffn_wo"],
            flg=inputs["L0_ffn_ln_g"], flb=inputs["L0_ffn_ln_b"],
            fw1=inputs["L0_ffn_w1"], fb1=inputs["L0_ffn_b1"],
            fw2=inputs["L0_ffn_w2"], fb2=inputs["L0_ffn_b2"],
        )
        m.update(c)
        maps.append(m)
    return maps


def kernel(**inputs):
    mk = MK()
    nc = mk.build()
    maps = shard_inputs(inputs)
    res = run_bass_kernel_spmd(nc, maps, core_ids=list(range(8)))
    return np.concatenate([r["y"] for r in res.results], axis=0)


MK.emit_attn = emit_attn
MK.emit_ffn = emit_ffn
MK.emit_final = emit_final
